# version 10
# baseline (speedup 1.0000x reference)
import math
import numpy as np
import ml_dtypes
from contextlib import ExitStack

import concourse.bass as bass
import concourse.mybir as mybir
from concourse.bass_utils import run_bass_kernel_spmd

F32 = mybir.dt.float32
BF16 = mybir.dt.bfloat16
AF = mybir.ActivationFunctionType
ALU = mybir.AluOpType
AX = mybir.AxisListType

P = 128
S = 2048
DM = 2048
NH = 16
HD = 128
NE = 8
FF = 2048
NT = S // P
ND = DM // P
SOFT_SCALE = float(np.float32(1.0 / math.sqrt(HD)))
EPS = float(np.float32(1e-5))
GOLD_C = float(np.float32(1.0 / np.e))
GOLD_L = float(np.float32(0.5 - np.log(4.0 / 3.0)))
GOLD_U = float(np.float32(0.5))

ENGS = ("t", "v", "s", "p", "y")


class Res:
    __slots__ = ("w", "r")

    def __init__(self):
        self.w = None
        self.r = []


def seed(*olds):
    n = Res()
    toks = []
    for o in olds:
        toks.extend(o.r)
        if o.w is not None:
            toks.append(o.w)
    n.r = toks
    return n


class Em:
    def __init__(self, nc):
        self.nc = nc
        self.th = {e: [] for e in ENGS}
        self.sem = {}
        self.cnt = {}
        self.waited = {e: {} for e in ENGS}

    def mksem(self, ctx, names):
        for n in names:
            self.sem[n] = ctx.enter_context(self.nc.semaphore(n))
            self.cnt[n] = 0

    def wait(self, eng, sname, val):
        if val is None or val <= 0:
            return
        if self.waited[eng].get(sname, 0) >= val:
            return
        self.waited[eng][sname] = val
        h = self.sem[sname]
        self.th[eng].append(lambda E, h=h, v=val: E.wait_ge(h, v))

    def _wait_tok(self, eng, tok):
        self.wait(eng, tok[0], tok[1])

    def barrier_read(self, eng, reads):
        for r in reads:
            if r.w is not None:
                self._wait_tok(eng, r.w)

    def warw(self, eng, res):
        for rr in res.r:
            self._wait_tok(eng, rr)
        if res.w is not None:
            self._wait_tok(eng, res.w)

    def raw(self, eng, fn):
        self.th[eng].append(fn)

    def inc(self, eng, fn, dma=False):
        if dma:
            sname = {"y": "dy", "p": "dp", "s": "ds", "v": "dv"}[eng]
            step = 16
        else:
            sname = eng
            step = 1
        h = self.sem[sname]
        self.cnt[sname] += step
        self.th[eng].append(lambda E, fn=fn, h=h, s=step: fn(E).then_inc(h, s))
        return (sname, self.cnt[sname])

    def emit(self, eng, fn, reads=(), writes=(), dma=False):
        for r in reads:
            if r.w is not None:
                self._wait_tok(eng, r.w)
        for w in writes:
            self.warw(eng, w)
        tok = self.inc(eng, fn, dma=dma)
        for r in reads:
            r.r.append(tok)
        for w in writes:
            w.w = tok
            w.r = []
        return tok


def _np_rope_tables():
    inv_freq = 1.0 / 10000.0 ** (np.arange(0, HD, 2, dtype=np.float32) / HD)
    t = np.arange(S, dtype=np.float32)
    freqs = np.einsum("i,j->ij", t, inv_freq)
    emb = np.concatenate((freqs, freqs), axis=-1)
    cosT = np.ascontiguousarray(np.cos(emb).astype(np.float32).T)
    sinT = np.ascontiguousarray(np.sin(emb).astype(np.float32).T)
    sin_rope = sinT.copy()
    sin_rope[:64, :] *= -1.0
    return cosT, sin_rope


def host_consts():
    ident = np.eye(P, dtype=np.float32).astype(ml_dtypes.bfloat16)
    cosT, sinT = _np_rope_tables()
    mask = np.where(np.arange(P)[:, None] <= np.arange(P)[None, :], 1.0, 0.0)
    maskc = mask.astype(np.float32).astype(ml_dtypes.bfloat16)
    rev = np.broadcast_to(np.arange(NE, 0, -1, dtype=np.float32)[None, :], (P, NE)).copy()
    ones1 = np.ones((1, P), dtype=np.float32)
    ones1b = np.ones((1, P), dtype=np.float32).astype(ml_dtypes.bfloat16)
    return {"c_ident": ident, "c_cos": cosT, "c_sin": sinT, "c_mask": maskc,
            "c_rev": rev, "c_ones1": ones1, "c_ones1b": ones1b}


def build(debug=False):
    nc = bass.Bass("TRN2", target_bir_lowering=False, dynamic_dma_scratch_size=16384)
    dt = nc.dram_tensor
    kind_sp = "ExternalOutput" if debug else "Internal"

    x0_h = dt("x0", [S, DM], F32, kind="ExternalInput").ap()
    wq_h = dt("wq", [DM, DM], F32, kind="ExternalInput").ap()
    wk_h = dt("wk", [DM, DM], F32, kind="ExternalInput").ap()
    wv_h = dt("wv", [DM, DM], F32, kind="ExternalInput").ap()
    wo_h = dt("wo", [DM, DM], F32, kind="ExternalInput").ap()
    wg_h = dt("wg", [NE, DM, FF], F32, kind="ExternalInput").ap()
    wu_h = dt("wu", [NE, DM, FF], F32, kind="ExternalInput").ap()
    wd_h = dt("wd", [NE, FF, DM], F32, kind="ExternalInput").ap()
    wr_h = dt("wr", [DM, NE], F32, kind="ExternalInput").ap()
    temp_h = dt("temp", [1, 1], F32, kind="ExternalInput").ap()
    c_ident = dt("c_ident", [P, P], BF16, kind="ExternalInput").ap()
    c_cos = dt("c_cos", [P, S], F32, kind="ExternalInput").ap()
    c_sin = dt("c_sin", [P, S], F32, kind="ExternalInput").ap()
    c_mask = dt("c_mask", [P, P], BF16, kind="ExternalInput").ap()
    c_rev = dt("c_rev", [P, NE], F32, kind="ExternalInput").ap()
    c_ones1 = dt("c_ones1", [1, P], F32, kind="ExternalInput").ap()
    c_ones1b = dt("c_ones1b", [1, P], BF16, kind="ExternalInput").ap()

    out_h = dt("out", [S, DM], F32, kind="ExternalOutput").ap()

    dbg1 = dt("dbg_ln1bf", [P, 2048], BF16, kind=kind_sp).ap() if debug else None
    dbg2 = dt("dbg_x0f", [P, 2048], F32, kind=kind_sp).ap() if debug else None
    ln1t_sp = dt("ln1t_sp", [NT, P, ND, P], BF16, kind=kind_sp).ap()
    q_sp = dt("q_sp", [NH, P, S], BF16, kind=kind_sp).ap()
    ao_sp = dt("ao_sp", [NT, P, DM], BF16, kind=kind_sp).ap()
    x1_sp = dt("x1_sp", [S, DM], F32, kind=kind_sp).ap()
    ln2t_sp = dt("ln2t_sp", [2, P, ND, 1024], BF16, kind=kind_sp).ap()
    wg_c = dt("wg_c", [NE, DM, FF], BF16, kind="Internal").ap()
    wu_c = dt("wu_c", [NE, DM, FF], BF16, kind="Internal").ap()
    wd_c = dt("wd_c", [NE, FF, DM], BF16, kind="Internal").ap()

    with ExitStack() as ctx:
        en = ctx.enter_context
        A1 = en(nc.sbuf_tensor("A1", [P, 32768], BF16))
        A2 = en(nc.sbuf_tensor("A2", [P, 32768], BF16))
        A3 = en(nc.sbuf_tensor("A3", [P, 32768], BF16))
        wstage = en(nc.sbuf_tensor("wstage", [P, 2048], F32))
        cast_buf = en(nc.sbuf_tensor("cast_buf", [P, 2048], BF16))
        stt = en(nc.sbuf_tensor("stt", [P, 32], F32))
        sttC = en(nc.sbuf_tensor("sttC", [P, 32], F32))
        rt = en(nc.sbuf_tensor("rt", [P, 176], F32))
        w_all = en(nc.sbuf_tensor("w_all", [P, 128], F32))
        w_bfe = en(nc.sbuf_tensor("w_bfe", [P, 128], BF16))
        ident = en(nc.sbuf_tensor("ident", [P, P], BF16))
        maskc = en(nc.sbuf_tensor("maskc", [P, P], BF16))
        rev = en(nc.sbuf_tensor("rev", [P, NE], F32))
        ones1 = en(nc.sbuf_tensor("ones1", [1, P], F32))
        ones1b = en(nc.sbuf_tensor("ones1b", [1, P], BF16))
        wr_hi = en(nc.sbuf_tensor("wr_hi", [P, ND * NE], BF16))
        wr_lo = en(nc.sbuf_tensor("wr_lo", [P, ND * NE], BF16))
        wr_f = en(nc.sbuf_tensor("wr_f", [P, NE], F32))
        temp_s = en(nc.sbuf_tensor("temp_s", [1, 1], F32))
        pb0 = en(nc.psum_tensor("pb0", [P, 2048], F32))
        pb4 = en(nc.psum_tensor("pb4", [P, 1024], BF16))
        pb5 = en(nc.psum_tensor("pb5", [P, 1024], BF16))
        pb6 = en(nc.psum_tensor("pb6", [P, 512], F32))
        pb7 = en(nc.psum_tensor("pb7", [P, 512], F32))
        wT = A3[0:8, 28672:30720]
        wrow = A3[0:1, 30720:31744]

        em = Em(nc)
        em.mksem(ctx, ["t", "v", "s", "p", "dy", "dp", "ds", "dv"])

        b = [Res() for _ in range(8)]
        r_consts = Res()
        r_tabs = Res()
        r_wr = Res()
        r_temp = Res()
        r_stt = Res()
        r_rt = Res()
        r_xfa = Res()
        r_wstage = [Res(), Res()]
        r_castbuf = [Res(), Res()]
        wslot_ctr = [0]
        r_spill = {k: Res() for k in
                   ("ln1t", "q", "ao", "x1", "ln2t", "out")}
        for nm in ("wg", "wu", "wd"):
            for ex_ in range(NE):
                r_spill[nm + str(ex_)] = Res()

        def mm_open(bank):
            em.warw("t", bank)

        def mm_raw(dst, lhs, rhs, start, stop):
            em.raw("t", lambda E, d=dst, l=lhs, r=rhs, st=start, sp=stop:
                   E.matmul(d, l, r, start=st, stop=sp))

        def mm_close(dst, lhs, rhs, start, bank, readers):
            tok = em.inc("t", lambda E, d=dst, l=lhs, r=rhs, st=start:
                         E.matmul(d, l, r, start=st, stop=True))
            bank.w = tok
            bank.r = []
            for r in readers:
                r.r.append(tok)
            return tok

        def tr_raw(dst, src):
            em.raw("t", lambda E, d=dst, s=src: E.transpose(d, s, ident[:]))

        def tr_close(dst, src, bank, readers):
            tok = em.inc("t", lambda E, d=dst, s=src: E.transpose(d, s, ident[:]))
            bank.w = tok
            bank.r = []
            for r in readers:
                r.r.append(tok)
            return tok

        def prep_chunk(w_rows_ap, dst_fn, dst_res):
            for hf in range(2):
                cb, cres = stage_load_cast(w_rows_ap[:, hf * 1024:(hf + 1) * 1024])
                em.barrier_read("t", [cres, r_consts])
                pb = pb4 if hf == 0 else pb5
                bank = b[4 + hf]
                mm_open(bank)
                for j in range(8):
                    s = cb[:, j * P:(j + 1) * P]
                    d = pb[:, j * P:(j + 1) * P]
                    if j < 7:
                        tr_raw(d, s)
                    else:
                        tr_close(d, s, bank, [cres])
                sp = pb[:, :].rearrange("p (c n) -> p c n", n=P)
                em.emit("v", lambda E, s=sp, d=dst_fn(hf): E.tensor_copy(d, s),
                        reads=[bank], writes=[dst_res])

        def transpose16(src_bf, src_res, dst_fn, dst_res, nhalf=2):
            em.barrier_read("t", [src_res, r_consts])
            for half in range(nhalf):
                pb = pb4 if half % 2 == 0 else pb5
                bank = b[4 + (half % 2)]
                mm_open(bank)
                for j in range(8):
                    dc = half * 8 + j
                    s = src_bf[:, dc * P:(dc + 1) * P]
                    d = pb[:, j * P:(j + 1) * P]
                    if j < 7:
                        tr_raw(d, s)
                    else:
                        tr_close(d, s, bank, [src_res])
                sp = pb[:, :].rearrange("p (c n) -> p c n", n=P)
                em.emit("v", lambda E, s=sp, d=dst_fn(half): E.tensor_copy(d, s),
                        reads=[bank], writes=[dst_res])

        for dst_ap, src_ap, res in ((ident[:], c_ident, r_consts), (maskc[:], c_mask, r_consts),
                                    (rev[:], c_rev, r_consts), (ones1[:], c_ones1, r_consts),
                                    (ones1b[:], c_ones1b, r_consts), (temp_s[:], temp_h, r_temp)):
            em.emit("y", lambda E, d=dst_ap, s=src_ap: E.dma_start(out=d, in_=s),
                    writes=[res], dma=True)
        cosT = A3[:, 0:4096].bitcast(F32)
        sinT = A3[:, 4096:8192].bitcast(F32)
        em.emit("y", lambda E: E.dma_start(out=cosT, in_=c_cos), writes=[r_tabs], dma=True)
        em.emit("y", lambda E: E.dma_start(out=sinT, in_=c_sin), writes=[r_tabs], dma=True)
        for dc_ in range(ND):
            em.emit("y", lambda E, s=wr_h[dc_ * P:(dc_ + 1) * P, :]: E.dma_start(out=wr_f[:], in_=s),
                    writes=[r_wr], dma=True)
            em.emit("v", lambda E, d=wr_hi[:, dc_ * NE:(dc_ + 1) * NE]: E.tensor_copy(d, wr_f[:]),
                    reads=[r_wr], writes=[r_wr])
            em.emit("v", lambda E, d=wr_lo[:, dc_ * NE:(dc_ + 1) * NE],
                    h=wr_hi[:, dc_ * NE:(dc_ + 1) * NE]:
                    E.tensor_tensor(d, wr_f[:], h, ALU.subtract),
                    reads=[r_wr], writes=[r_wr])
        em.emit("t", lambda E: E.matmul(pb6[:, 0:1], ones1[:], temp_s[:], start=True, stop=True),
                reads=[r_consts, r_temp], writes=[b[6]])
        em.emit("v", lambda E: E.tensor_copy(stt[:, 15:16], pb6[:, 0:1]), reads=[b[6]], writes=[r_stt])
        em.emit("v", lambda E: E.memset(stt[:, 23:24], EPS), writes=[r_stt])
        em.emit("v", lambda E: E.memset(stt[:, 30:31], -20.0), writes=[r_stt])
        em.emit("v", lambda E: E.reciprocal(stt[:, 9:10], stt[:, 15:16]), reads=[r_stt], writes=[r_stt])

        cache_jobs = []
        for ex in range(NE):
            for mat_h, mat_c, name in ((wg_h, wg_c, "wg"), (wu_h, wu_c, "wu"), (wd_h, wd_c, "wd")):
                for rc in range(ND):
                    for hf in range(2):
                        cache_jobs.append((mat_h[ex, rc * P:(rc + 1) * P, hf * 1024:(hf + 1) * 1024],
                                           mat_c[ex, rc * P:(rc + 1) * P, hf * 1024:(hf + 1) * 1024],
                                           name + str(ex)))
        cache_pos = [0]

        def stage_load_cast(src_ap):
            wv = wstage[:, 0:1024]
            cb = cast_buf[:, 0:1024]
            em.emit("y", lambda E, d=wv, s=src_ap: E.dma_start(out=d, in_=s),
                    writes=[r_wstage[0]], dma=True)
            em.emit("v", lambda E, d=cb, s=wv: E.tensor_copy(d, s),
                    reads=[r_wstage[0]], writes=[r_castbuf[0]])
            return cb, r_castbuf[0]

        r_cstage = [Res(), Res()]
        r_ccast = [Res(), Res()]
        cache_sub = [0]

        def emit_cache_chunks(k):
            for _ in range(k):
                if cache_pos[0] >= len(cache_jobs):
                    return
                src, dst, name = cache_jobs[cache_pos[0]]
                cache_pos[0] += 1
                for sub in range(2):
                    sl = cache_sub[0] % 2
                    cache_sub[0] += 1
                    wv = wstage[:, 1024 + sl * 512: 1024 + (sl + 1) * 512]
                    cb = cast_buf[:, 1024 + sl * 512: 1024 + (sl + 1) * 512]
                    em.emit("p", lambda E, d=wv, s=src[:, sub * 512:(sub + 1) * 512]:
                            E.dma_start(out=d, in_=s),
                            writes=[r_cstage[sl]], dma=True)
                    em.emit("p", lambda E, d=cb, s=wv: E.tensor_copy(d, s),
                            reads=[r_cstage[sl]], writes=[r_ccast[sl]])
                    em.emit("s", lambda E, d=dst[:, sub * 512:(sub + 1) * 512], s=cb:
                            E.dma_start(out=d, in_=s),
                            reads=[r_ccast[sl]], writes=[r_spill[name]], dma=True)

        def emit_ln(src_ap, src_res, dst_ap, dst_res, dump_ap, dump_writes, lo=None):
            em.emit("v", lambda E: E.reduce_sum(stt[:, 0:1], src_ap, axis=AX.X),
                    reads=[src_res], writes=[r_stt])
            em.emit("s", lambda E: E.mul(stt[:, 1:2], stt[:, 0:1], -1.0 / DM),
                    reads=[r_stt], writes=[r_stt])
            em.emit("s", lambda E: E.activation(out=dump_ap, in_=src_ap, func=AF.Square,
                                                bias=stt[:, 1:2], scale=1.0,
                                                accum_out=stt[:, 2:3]),
                    reads=[src_res, r_stt], writes=[r_stt] + list(dump_writes))
            em.emit("s", lambda E: E.activation(out=stt[:, 3:4], in_=stt[:, 2:3], func=AF.Ln,
                                                bias=stt[:, 23:24], scale=float(np.float32(1.0 / DM))),
                    reads=[r_stt], writes=[r_stt])
            em.emit("s", lambda E: E.activation(out=stt[:, 4:5], in_=stt[:, 3:4], func=AF.Exp,
                                                bias=0.0, scale=-0.5),
                    reads=[r_stt], writes=[r_stt])
            if lo is None:
                em.emit("v", lambda E: E.tensor_scalar(out=dst_ap, in0=src_ap,
                                                       scalar1=stt[:, 1:2], scalar2=stt[:, 4:5],
                                                       op0=ALU.add, op1=ALU.mult),
                        reads=[src_res, r_stt], writes=[dst_res])
            else:
                lnf_ap, lnf_res, lo_ap, lo_res = lo
                em.emit("v", lambda E: E.tensor_scalar(out=lnf_ap, in0=src_ap,
                                                       scalar1=stt[:, 1:2], scalar2=stt[:, 4:5],
                                                       op0=ALU.add, op1=ALU.mult),
                        reads=[src_res, r_stt], writes=[lnf_res])
                em.emit("s", lambda E: E.activation(out=dst_ap, in_=lnf_ap, func=AF.Copy,
                                                    bias=0.0, scale=1.0),
                        reads=[lnf_res], writes=[dst_res])
                em.emit("v", lambda E: E.tensor_tensor(lo_ap, lnf_ap, dst_ap, ALU.subtract),
                        reads=[lnf_res, dst_res], writes=[lo_res])

        r_ln1T = Res()
        r_A2s = [Res(), Res()]
        ln1bf2 = [A2[:, 0:2048], A2[:, 2048:4096]]
        xf_p1 = [A2[:, 4096:8192].bitcast(F32), A2[:, 8192:12288].bitcast(F32)]
        r_xf_p1 = [Res(), Res()]
        A1v = A1[:, :].rearrange("p (dc n) -> p dc n", n=2048)
        for tt in range(NT):
            xsl = tt % 2
            em.emit("y", lambda E, d=xf_p1[xsl][:], s=x0_h[tt * P:(tt + 1) * P, :]: E.dma_start(out=d, in_=s),
                    writes=[r_xf_p1[xsl]], dma=True)
            emit_ln(xf_p1[xsl][:], r_xf_p1[xsl], ln1bf2[xsl], r_A2s[xsl], pb0[:, 0:2048],
                    [b[0], b[1], b[2], b[3]])
            transpose16(ln1bf2[xsl], r_A2s[xsl],
                        lambda half, tt=tt: A1v[:, half * 8:half * 8 + 8, tt * P:(tt + 1) * P],
                        r_ln1T)
            em.emit("p", lambda E, s=A1v[:, :, tt * P:(tt + 1) * P], d=ln1t_sp[tt]:
                    E.dma_start(out=d, in_=s),
                    reads=[r_ln1T], writes=[r_spill["ln1t"]], dma=True)
            emit_cache_chunks(12)

        r_t12 = Res()
        r_wqt = seed(r_A2s[0], r_A2s[1])
        r_wkt = Res()
        r_K = seed(r_wqt, r_xf_p1[0], r_xf_p1[1])
        t12f = A3[:, 8192:12288].bitcast(F32)
        t1 = t12f[:, 0:512]
        t2 = t12f[:, 512:1024]
        qbf2 = [A3[:, 12288:12800], A3[:, 12800:13312]]
        r_qbf = [Res(), Res()]
        wkt_st = A3[:, 13312:15360]

        def emit_proj(w_src_h, mode):
            for oc in range(NH):
                if mode == "q":
                    wt_res, wt_buf = r_wqt, A2[:, 0:2048]
                else:
                    wt_res, wt_buf = r_wkt, wkt_st
                wtv = wt_buf.rearrange("p (c n) -> p c n", n=P)
                prep_chunk(w_src_h[oc * P:(oc + 1) * P, :],
                           lambda half, wtv=wtv: wtv[:, half * 8:half * 8 + 8, :], wt_res)
                em.barrier_read("t", [wt_res, r_ln1T])
                for tb in range(4):
                    bank = b[tb]
                    for dc in range(ND):
                        lhs = wt_buf[:, dc * P:(dc + 1) * P]
                        rhs = A1[:, dc * 2048 + tb * 512: dc * 2048 + (tb + 1) * 512]
                        dst = pb0[:, tb * 512:(tb + 1) * 512]
                        if dc == 0:
                            mm_open(bank)
                            mm_raw(dst, lhs, rhs, True, False)
                        elif dc == ND - 1:
                            mm_close(dst, lhs, rhs, False, bank, [wt_res, r_ln1T])
                        else:
                            mm_raw(dst, lhs, rhs, False, False)
                    ps = pb0[:, tb * 512:(tb + 1) * 512]
                    ps_lo = pb0[64:128, tb * 512:(tb + 1) * 512]
                    ps_hi = pb0[0:64, tb * 512:(tb + 1) * 512]
                    cs = cosT[:, tb * 512:(tb + 1) * 512]
                    sn = sinT[:, tb * 512:(tb + 1) * 512]
                    em.emit("v", lambda E, p_=ps, c_=cs: E.tensor_tensor(t1[:, :], p_, c_, ALU.mult),
                            reads=[bank, r_tabs], writes=[r_t12])
                    em.emit("v", lambda E, p_=ps_lo, s_=sn[0:64, :]:
                            E.tensor_tensor(t2[0:64, :], p_, s_, ALU.mult),
                            reads=[bank, r_tabs], writes=[r_t12])
                    em.emit("v", lambda E, p_=ps_hi, s_=sn[64:128, :]:
                            E.tensor_tensor(t2[64:128, :], p_, s_, ALU.mult),
                            reads=[bank, r_tabs], writes=[r_t12])
                    if mode == "q":
                        qsl = (oc * 4 + tb) % 2
                        em.emit("v", lambda E, d=qbf2[qsl]: E.tensor_tensor(d, t1[:, :], t2[:, :], ALU.add),
                                reads=[r_t12], writes=[r_qbf[qsl]])
                        em.emit("p", lambda E, d=q_sp[oc, :, tb * 512:(tb + 1) * 512], s=qbf2[qsl]:
                                E.dma_start(out=d, in_=s),
                                reads=[r_qbf[qsl]], writes=[r_spill["q"]], dma=True)
                    else:
                        dstk = A2[:, oc * 2048 + tb * 512: oc * 2048 + (tb + 1) * 512]
                        em.emit("v", lambda E, d=dstk: E.tensor_tensor(d, t1[:, :], t2[:, :], ALU.add),
                                reads=[r_t12], writes=[r_K])
                emit_cache_chunks(8)

        emit_proj(wq_h, "q")
        emit_proj(wk_h, "k")

        VW = NH * 129
        r_wvt = [seed(r_ln1T), seed(r_ln1T)]
        r_slice = [seed(r_ln1T), seed(r_ln1T)]
        r_V = seed(r_tabs, r_t12, r_wkt, r_qbf[0], r_qbf[1], r_ln1T)

        def vtile(tt):
            if tt < 15:
                return A3[:, tt * VW:(tt + 1) * VW]
            return A1[:, 20480:20480 + VW]
        for ob in range(4):
            slot = ob % 2
            wv_slot = A1[:, slot * 8192:(slot + 1) * 8192]
            wvv = wv_slot.rearrange("p (dc n) -> p dc n", n=512)
            for j in range(4):
                oc = ob * 4 + j
                prep_chunk(wv_h[oc * P:(oc + 1) * P, :],
                           lambda half, wvv=wvv, j=j: wvv[:, half * 8:half * 8 + 8, j * P:(j + 1) * P],
                           r_wvt[slot])
            for tt in range(NT):
                sl = tt % 2
                sbuf = A1[:, 16384 + sl * 2048: 16384 + (sl + 1) * 2048]
                em.emit("y", lambda E, s=ln1t_sp[tt], d=sbuf.rearrange("p (c n) -> p c n", n=P):
                        E.dma_start(out=d, in_=s),
                        reads=[r_spill["ln1t"]], writes=[r_slice[sl]], dma=True)
                bank = b[tt % 4]
                em.barrier_read("t", [r_slice[sl], r_wvt[slot]])
                dst = pb0[:, (tt % 4) * 512:((tt % 4) + 1) * 512]
                for dc in range(ND):
                    lhs = sbuf[:, dc * P:(dc + 1) * P]
                    rhs = wv_slot[:, dc * 512:(dc + 1) * 512]
                    if dc == 0:
                        mm_open(bank)
                        mm_raw(dst, lhs, rhs, True, False)
                    elif dc == ND - 1:
                        mm_close(dst, lhs, rhs, False, bank, [r_slice[sl], r_wvt[slot]])
                    else:
                        mm_raw(dst, lhs, rhs, False, False)
                vdst = vtile(tt).rearrange("p (h n) -> p h n", n=129)[:, ob * 4:(ob + 1) * 4, 0:128]
                em.emit("s", lambda E, d=vdst, s=dst.rearrange("p (h n) -> p h n", n=P):
                        E.activation(out=d, in_=s, func=AF.Copy, bias=0.0, scale=1.0),
                        reads=[bank], writes=[r_V])
                if tt % 4 == 3:
                    emit_cache_chunks(2)
        for tt in range(NT):
            em.emit("v", lambda E, d=vtile(tt).rearrange("p (h n) -> p h n", n=129)[:, :, 128:129]:
                    E.memset(d, 1.0), writes=[r_V])

        r_qh = [seed(r_wvt[0], r_wvt[1]), seed(r_wvt[0], r_wvt[1])]
        r_expc = [seed(r_wvt[0], r_wvt[1]) for _ in range(3)]
        r_stage = [seed(r_wvt[0], r_wvt[1]), seed(r_wvt[0], r_wvt[1])]
        r_sC = Res()
        qh_buf = [A1[:, 0:2048], A1[:, 2048:4096]]
        expc = [A1[:, 4096:4608], A1[:, 4608:5120], A1[:, 5120:5632]]
        stg = [A1[:, 5632:6144], A1[:, 6144:6656]]
        ec_ctr = [0]

        def qh_load(hh):
            sl = hh % 2
            em.emit("y", lambda E, d=qh_buf[sl][:], s=q_sp[hh]: E.dma_start(out=d, in_=s),
                    reads=[r_spill["q"]], writes=[r_qh[sl]], dma=True)

        def score_exp(hh, Q, kk):
            sl = hh % 2
            ep = ec_ctr[0] % 3
            ep2 = ec_ctr[0] % 2
            ec_ctr[0] += 1
            sb = b[6 + ep2]
            spp = pb6 if ep2 == 0 else pb7
            em.barrier_read("t", [r_K, r_qh[sl]])
            em.warw("t", sb)
            tok = em.inc("t", lambda E, d=spp[:, 0:512],
                         l=A2[:, hh * 2048 + kk * P: hh * 2048 + (kk + 1) * P],
                         r=qh_buf[sl][:, Q * 512:(Q + 1) * 512]:
                         E.matmul(d, l, r, start=True, stop=True))
            sb.w = tok
            sb.r = []
            r_K.r.append(tok)
            r_qh[sl].r.append(tok)
            em.emit("s", lambda E, s=spp[:, 0:512], d=expc[ep][:]:
                    E.activation(out=d, in_=s, func=AF.Exp, bias=stt[:, 30:31],
                                 scale=SOFT_SCALE),
                    reads=[sb, r_stt], writes=[r_expc[ep]])
            if kk >= 4 * Q:
                jd = kk - 4 * Q
                em.emit("v", lambda E, d=expc[ep][:, jd * P:(jd + 1) * P]:
                        E.tensor_tensor(d, d, maskc[:], ALU.mult),
                        reads=[r_consts], writes=[r_expc[ep]])
            return ep

        def avs(hh, Q, kk, ep):
            sg = Q % 2
            jls = list(range(max(kk - 4 * Q, 0), 4))
            em.barrier_read("t", [r_expc[ep], r_V])
            for idx, jl in enumerate(jls):
                qt = 4 * Q + jl
                bank = b[jl]
                dst = pb0[:, jl * 512: jl * 512 + 129]
                lhs = expc[ep][:, jl * P:(jl + 1) * P]
                rhs = vtile(kk)[:, hh * 129:(hh + 1) * 129]
                if kk == 0:
                    em.warw("t", bank)
                if kk == qt or idx == len(jls) - 1:
                    tok = em.inc("t", lambda E, d=dst, l=lhs, r=rhs,
                                 s1=(kk == 0), s2=(kk == qt):
                                 E.matmul(d, l, r, start=s1, stop=s2))
                    r_expc[ep].r.append(tok)
                    r_V.r.append(tok)
                    if kk == qt:
                        bank.w = tok
                        bank.r = []
                        sc = (Q % 2) * 4 + jl
                        em.emit("v", lambda E, d=sttC[:, sc:sc + 1],
                                s=pb0[:, jl * 512 + 128: jl * 512 + 129]:
                                E.reciprocal(d, s),
                                reads=[bank], writes=[r_sC])
                        em.emit("v", lambda E, d=stg[sg][:, jl * P:(jl + 1) * P],
                                s=pb0[:, jl * 512: jl * 512 + 128],
                                r_=sttC[:, sc:sc + 1]:
                                E.tensor_scalar(out=d, in0=s, scalar1=r_,
                                                scalar2=None, op0=ALU.mult),
                                reads=[bank, r_sC], writes=[r_stage[sg]])
                else:
                    mm_raw(dst, lhs, rhs, (kk == 0), False)

        def stage_dma(hh, Q):
            sg = Q % 2
            em.emit("p", lambda E, s=stg[sg].rearrange("p (t n) -> p t n", n=P),
                    d=ao_sp[4 * Q:4 * Q + 4, :, hh * P:(hh + 1) * P].rearrange("t p n -> p t n"):
                    E.dma_start(out=d, in_=s),
                    reads=[r_stage[sg]], writes=[r_spill["ao"]], dma=True)

        qh_load(0)
        qh_load(1)
        for hh in range(NH):
            chunks = [(Q, kk) for Q in range(4) for kk in range(4 * Q + 4)]
            eps = {0: score_exp(hh, *chunks[0]), 1: score_exp(hh, *chunks[1])}
            for i, (Q, kk) in enumerate(chunks):
                if i + 2 < len(chunks):
                    eps[i + 2] = score_exp(hh, *chunks[i + 2])
                avs(hh, Q, kk, eps[i])
                if kk == 4 * Q + 3:
                    stage_dma(hh, Q)
            if hh + 2 < NH:
                qh_load(hh + 2)
            emit_cache_chunks(12)

        r_WoT = seed(r_qh[0], r_qh[1], r_expc[0], r_expc[1], r_stage[0], r_stage[1],
                     r_slice[0], r_slice[1], r_V)
        r_ao = [seed(r_K), seed(r_K)]
        r_aot = seed(r_K)
        r_ln2hi = seed(r_K)
        r_ln2lo = seed(r_K)
        r_hiT = seed(r_K)
        r_loT = seed(r_K)
        r_lnf = seed(r_V)
        r_sqd = seed(r_V)
        r_wall = Res()
        r_wT = seed(r_V)

        A1w = A1[:, :].rearrange("p (hc n) -> p hc n", n=2048)
        for oc in range(NH):
            prep_chunk(wo_h[oc * P:(oc + 1) * P, :],
                       lambda half, oc=oc: A1w[:, half * 8:half * 8 + 8, oc * P:(oc + 1) * P],
                       r_WoT)

        aosl = [A2[:, 0:2048], A2[:, 2048:4096]]
        aotb = A2[:, 4096:6144]
        ln2hi = A2[:, 6144:8192]
        ln2lo = A2[:, 8192:10240]
        hiT = A2[:, 10240:12288]
        loT = A2[:, 12288:14336]
        lnf = A3[:, 0:4096].bitcast(F32)
        sqd = A3[:, 8192:12288].bitcast(F32)[:, 0:2048]
        xfa2 = [A2[:, 14336:18432].bitcast(F32), A2[:, 18432:22528].bitcast(F32)]
        r_xfa2 = [seed(r_K), seed(r_K)]
        aotv = aotb.rearrange("p (c n) -> p c n", n=P)
        hiTv = hiT.rearrange("p (c n) -> p c n", n=P)
        loTv = loT.rearrange("p (c n) -> p c n", n=P)

        def c8(i):
            return rt[:, i * NE:(i + 1) * NE]

        (RZ, REX, RINH, RD0, RDIST, REX1, RFB, RZONE, RWT, RSELR, RFB2,
         RFBM, RFBW, RG1, RG2, RW0, RDIFF, RDSC, RINV, RSEL2R, RT1, RT2) = range(22)

        def c2_stageA(tt):
            sl = tt % 2
            xf = xfa2[sl]
            rxf = r_xfa2[sl]
            em.emit("y", lambda E, s=ao_sp[tt], d=aosl[sl]: E.dma_start(out=d, in_=s),
                    reads=[r_spill["ao"]], writes=[r_ao[sl]], dma=True)
            em.emit("y", lambda E, d=xf[:], s=x0_h[tt * P:(tt + 1) * P, :]: E.dma_start(out=d, in_=s),
                    writes=[rxf], dma=True)
            transpose16(aosl[sl], r_ao[sl],
                        lambda half: aotv[:, half * 8:half * 8 + 8, :], r_aot)
            em.barrier_read("t", [r_aot, r_WoT])
            for hc in range(NH):
                for ob in range(4):
                    bank = b[ob]
                    lhs = aotb[:, hc * P:(hc + 1) * P]
                    rhs = A1[:, hc * 2048 + ob * 512: hc * 2048 + (ob + 1) * 512]
                    dst = pb0[:, ob * 512:(ob + 1) * 512]
                    if hc == 0:
                        mm_open(bank)
                        mm_raw(dst, lhs, rhs, True, False)
                    elif hc == NH - 1:
                        mm_close(dst, lhs, rhs, False, bank, [r_aot, r_WoT])
                    else:
                        mm_raw(dst, lhs, rhs, False, False)
            for ob in range(4):
                em.emit("v", lambda E, s=pb0[:, ob * 512:(ob + 1) * 512],
                        x=xf[:, ob * 512:(ob + 1) * 512]:
                        E.scalar_tensor_tensor(out=x, in0=s, scalar=1.0, in1=x,
                                               op0=ALU.mult, op1=ALU.add),
                        reads=[b[ob]], writes=[rxf])
            em.emit("p", lambda E, d=x1_sp[tt * P:(tt + 1) * P, :], s=xf[:]: E.dma_start(out=d, in_=s),
                    reads=[rxf], writes=[r_spill["x1"]], dma=True)

        def c2_stageB(tt):
            sl = tt % 2
            xf = xfa2[sl]
            rxf = r_xfa2[sl]
            emit_ln(xf[:], rxf, ln2hi, r_ln2hi, sqd, [r_sqd],
                    lo=(lnf[:, 0:2048], r_lnf, ln2lo, r_ln2lo))
            transpose16(ln2hi, r_ln2hi, lambda half: hiTv[:, half * 8:half * 8 + 8, :], r_hiT)
            transpose16(ln2lo, r_ln2lo, lambda half: loTv[:, half * 8:half * 8 + 8, :], r_loT)
            tb2, off = tt // 8, (tt % 8) * P
            em.emit("p", lambda E, d=ln2t_sp[tb2, :, :, off:off + P], s=hiTv:
                    E.dma_start(out=d, in_=s),
                    reads=[r_hiT], writes=[r_spill["ln2t"]], dma=True)
            em.barrier_read("t", [r_hiT, r_loT, r_wr])
            mm_open(b[6])
            k = 0
            nmm = ND * 3
            for dc in range(ND):
                for (lb, rb) in ((hiT, wr_hi), (loT, wr_hi), (hiT, wr_lo)):
                    lhs = lb[:, dc * P:(dc + 1) * P]
                    rhs = rb[:, dc * NE:(dc + 1) * NE]
                    if k == nmm - 1:
                        mm_close(pb6[:, 0:NE], lhs, rhs, False, b[6], [r_hiT, r_loT, r_wr])
                    else:
                        mm_raw(pb6[:, 0:NE], lhs, rhs, (k == 0), False)
                    k += 1
            em.emit("v", lambda E: E.tensor_scalar(out=c8(RZ), in0=pb6[:, 0:NE],
                                                   scalar1=stt[:, 9:10], scalar2=None, op0=ALU.mult),
                    reads=[b[6], r_stt], writes=[r_rt])
            em.emit("s", lambda E: E.activation(out=c8(REX), in_=c8(RZ), func=AF.Exp,
                                                bias=0.0, scale=-1.0),
                    reads=[r_rt], writes=[r_rt])
            em.emit("v", lambda E: E.tensor_scalar(out=c8(RT1), in0=c8(REX), scalar1=1.0,
                                                   scalar2=None, op0=ALU.add),
                    reads=[r_rt], writes=[r_rt])
            em.emit("v", lambda E: E.reciprocal(c8(RINH), c8(RT1)), reads=[r_rt], writes=[r_rt])
            em.emit("v", lambda E: E.tensor_scalar(out=c8(RD0), in0=c8(RINH), scalar1=GOLD_C,
                                                   scalar2=None, op0=ALU.subtract),
                    reads=[r_rt], writes=[r_rt])
            em.emit("s", lambda E: E.activation(out=c8(RDIST), in_=c8(RD0), func=AF.Abs,
                                                bias=0.0, scale=1.0),
                    reads=[r_rt], writes=[r_rt])
            em.emit("s", lambda E: E.activation(out=c8(REX1), in_=c8(RDIST), func=AF.Exp,
                                                bias=0.0, scale=-10.0),
                    reads=[r_rt], writes=[r_rt])
            em.emit("s", lambda E: E.activation(out=c8(RFB), in_=c8(RDIST), func=AF.Exp,
                                                bias=0.0, scale=float(np.float32(-1.0 / 0.3))),
                    reads=[r_rt], writes=[r_rt])
            em.emit("v", lambda E: E.tensor_single_scalar(c8(RG1), c8(RINH), GOLD_L, ALU.is_ge),
                    reads=[r_rt], writes=[r_rt])
            em.emit("v", lambda E: E.tensor_single_scalar(c8(RG2), c8(RINH), GOLD_U, ALU.is_le),
                    reads=[r_rt], writes=[r_rt])
            em.emit("v", lambda E: E.tensor_tensor(c8(RZONE), c8(RG1), c8(RG2), ALU.mult),
                    reads=[r_rt], writes=[r_rt])
            em.emit("v", lambda E: E.tensor_tensor(c8(RWT), c8(REX1), c8(RZONE), ALU.mult),
                    reads=[r_rt], writes=[r_rt])
            em.emit("v", lambda E: E.reduce_sum(stt[:, 10:11], c8(RWT), axis=AX.X),
                    reads=[r_rt], writes=[r_stt])
            em.emit("v", lambda E: E.tensor_single_scalar(stt[:, 11:12], stt[:, 10:11],
                                                          1e-8, ALU.is_lt),
                    reads=[r_stt], writes=[r_stt])
            em.emit("v", lambda E: E.reduce_max(stt[:, 12:13], c8(RFB), axis=AX.X),
                    reads=[r_rt], writes=[r_stt])
            em.emit("v", lambda E: E.tensor_scalar(out=c8(RT1), in0=c8(RFB), scalar1=stt[:, 12:13],
                                                   scalar2=None, op0=ALU.is_ge),
                    reads=[r_rt, r_stt], writes=[r_rt])
            em.emit("v", lambda E: E.tensor_tensor(c8(RSELR), c8(RT1), rev[:], ALU.mult),
                    reads=[r_rt, r_consts], writes=[r_rt])
            em.emit("v", lambda E: E.reduce_max(stt[:, 13:14], c8(RSELR), axis=AX.X),
                    reads=[r_rt], writes=[r_stt])
            em.emit("v", lambda E: E.tensor_scalar(out=c8(RT1), in0=c8(RSELR), scalar1=stt[:, 13:14],
                                                   scalar2=None, op0=ALU.is_ge),
                    reads=[r_rt, r_stt], writes=[r_rt])
            em.emit("v", lambda E: E.tensor_scalar(out=c8(RINV), in0=c8(RT1), scalar1=-1.0,
                                                   scalar2=1.0, op0=ALU.mult, op1=ALU.add),
                    reads=[r_rt], writes=[r_rt])
            em.emit("v", lambda E: E.tensor_tensor(c8(RFB2), c8(RFB), c8(RINV), ALU.mult),
                    reads=[r_rt], writes=[r_rt])
            em.emit("v", lambda E: E.reduce_max(stt[:, 14:15], c8(RFB2), axis=AX.X),
                    reads=[r_rt], writes=[r_stt])
            em.emit("v", lambda E: E.tensor_scalar(out=c8(RT2), in0=c8(RFB2), scalar1=stt[:, 14:15],
                                                   scalar2=None, op0=ALU.is_ge),
                    reads=[r_rt, r_stt], writes=[r_rt])
            em.emit("v", lambda E: E.tensor_tensor(c8(RSEL2R), c8(RT2), rev[:], ALU.mult),
                    reads=[r_rt, r_consts], writes=[r_rt])
            em.emit("v", lambda E: E.reduce_max(stt[:, 16:17], c8(RSEL2R), axis=AX.X),
                    reads=[r_rt], writes=[r_stt])
            em.emit("v", lambda E: E.tensor_scalar(out=c8(RT2), in0=c8(RSEL2R), scalar1=stt[:, 16:17],
                                                   scalar2=None, op0=ALU.is_ge),
                    reads=[r_rt, r_stt], writes=[r_rt])
            em.emit("v", lambda E: E.tensor_tensor(c8(RFBM), c8(RT1), c8(RT2), ALU.add),
                    reads=[r_rt], writes=[r_rt])
            em.emit("v", lambda E: E.tensor_tensor(c8(RFBW), c8(RFB), c8(RFBM), ALU.mult),
                    reads=[r_rt], writes=[r_rt])
            em.emit("v", lambda E: E.reduce_sum(stt[:, 17:18], c8(RFBW), axis=AX.X),
                    reads=[r_rt], writes=[r_stt])
            em.emit("v", lambda E: E.tensor_single_scalar(stt[:, 18:19], stt[:, 17:18],
                                                          1e-8, ALU.max),
                    reads=[r_stt], writes=[r_stt])
            em.emit("v", lambda E: E.reciprocal(stt[:, 19:20], stt[:, 18:19]),
                    reads=[r_stt], writes=[r_stt])
            em.emit("v", lambda E: E.tensor_scalar(out=c8(RFBW), in0=c8(RFBW), scalar1=stt[:, 19:20],
                                                   scalar2=None, op0=ALU.mult),
                    reads=[r_stt], writes=[r_rt])
            em.emit("v", lambda E: E.tensor_tensor(c8(RDIFF), c8(RFBW), c8(RWT), ALU.subtract),
                    reads=[r_rt], writes=[r_rt])
            em.emit("v", lambda E: E.tensor_scalar(out=c8(RDSC), in0=c8(RDIFF), scalar1=stt[:, 11:12],
                                                   scalar2=None, op0=ALU.mult),
                    reads=[r_rt, r_stt], writes=[r_rt])
            em.emit("v", lambda E: E.tensor_tensor(c8(RW0), c8(RWT), c8(RDSC), ALU.add),
                    reads=[r_rt], writes=[r_rt])
            em.emit("v", lambda E: E.reduce_sum(stt[:, 20:21], c8(RW0), axis=AX.X),
                    reads=[r_rt], writes=[r_stt])
            em.emit("v", lambda E: E.tensor_single_scalar(stt[:, 21:22], stt[:, 20:21],
                                                          1e-8, ALU.max),
                    reads=[r_stt], writes=[r_stt])
            em.emit("v", lambda E: E.reciprocal(stt[:, 22:23], stt[:, 21:22]),
                    reads=[r_stt], writes=[r_stt])
            em.emit("v", lambda E, d=w_all[:, tt * NE:(tt + 1) * NE]:
                    E.tensor_scalar(out=d, in0=c8(RW0), scalar1=stt[:, 22:23],
                                    scalar2=None, op0=ALU.mult),
                    reads=[r_rt, r_stt], writes=[r_wall])
            emit_cache_chunks(4)

        c2_stageA(0)
        for tt in range(NT):
            if tt + 1 < NT:
                c2_stageA(tt + 1)
            c2_stageB(tt)

        em.emit("v", lambda E: E.tensor_copy(w_bfe[:], w_all[:]), reads=[r_wall], writes=[r_wall])
        for tt in range(NT):
            em.barrier_read("t", [r_wall, r_consts])
            em.warw("t", b[4])
            tok = em.inc("t", lambda E, s=w_bfe[:, tt * NE:(tt + 1) * NE]:
                         E.transpose(pb4[0:NE, 0:P], s, ident[:]))
            b[4].w = tok
            b[4].r = []
            r_wall.r.append(tok)
            em.emit("v", lambda E, d=wT[:, tt * P:(tt + 1) * P]: E.tensor_copy(d, pb4[0:NE, 0:P]),
                    reads=[b[4]], writes=[r_wT])

        r_wgwu = [seed(r_WoT), seed(r_WoT)]
        r_wd = [seed(r_WoT), seed(r_WoT)]
        r_ln2blk = seed(r_ao[0], r_ao[1], r_aot, r_ln2hi, r_ln2lo, r_hiT, r_loT, r_xfa2[0], r_xfa2[1])
        r_h = seed(r_ln2blk)
        r_part = seed(r_lnf, r_sqd, r_V)
        r_wbc = seed(r_lnf, r_sqd, r_V)
        r_x1b = [seed(r_lnf, r_sqd, r_V), seed(r_lnf, r_sqd, r_V)]
        r_outst = [seed(r_lnf, r_sqd, r_V), seed(r_lnf, r_sqd, r_V)]
        r_s12 = seed(r_lnf, r_sqd, r_V)
        r_wrow = seed(r_V)

        ln2blk = A2[:, 0:16384]
        hbuf = A2[:, 16384:32768]
        gsl_b = A1[:, 0:8192]
        usl_b = A1[:, 8192:16384]
        wdsl = [A1[:, 16384:24576], A1[:, 24576:32768]]
        partial = A3[:, 0:16384]
        wbc = A3[:, 16384:20480].bitcast(F32)[:, 0:1024]
        x1b = A3[:, 20480:22528].bitcast(F32)
        outst = A3[:, 22528:24576].bitcast(F32)
        s1b = A3[:, 24576:26624]
        s2b = A3[:, 26624:28672]

        emit_cache_chunks(len(cache_jobs))

        for tb in range(2):
            em.emit("y", lambda E, s=ln2t_sp[tb], d=ln2blk.rearrange("p (c n) -> p c n", n=1024):
                    E.dma_start(out=d, in_=s),
                    reads=[r_spill["ln2t"]], writes=[r_ln2blk], dma=True)
            for ex in range(NE):
                em.emit("y", lambda E, s=wT[ex:ex + 1, tb * 1024:(tb + 1) * 1024]:
                        E.dma_start(out=wrow, in_=s),
                        reads=[r_wT], writes=[r_wrow], dma=True)
                for half in range(2):
                    pbx = pb6 if half == 0 else pb7
                    bankx = b[6 + half]
                    em.emit("t", lambda E, d=pbx[:, 0:512], r=wrow[:, half * 512:(half + 1) * 512]:
                            E.matmul(d, ones1b[:], r, start=True, stop=True),
                            reads=[r_wrow, r_consts], writes=[bankx])
                    em.emit("v", lambda E, d=wbc[:, half * 512:(half + 1) * 512], s=pbx[:, 0:512]:
                            E.tensor_copy(d, s),
                            reads=[bankx], writes=[r_wbc])
                for fb in range(4):
                    em.emit("y", lambda E, s=wg_c[ex, :, fb * 512:(fb + 1) * 512]
                            .rearrange("(dc p) n -> p dc n", p=P),
                            d=gsl_b.rearrange("p (dc n) -> p dc n", n=512):
                            E.dma_start(out=d, in_=s),
                            reads=[r_spill["wg" + str(ex)]], writes=[r_wgwu[0]], dma=True)
                    em.emit("p", lambda E, s=wu_c[ex, :, fb * 512:(fb + 1) * 512]
                            .rearrange("(dc p) n -> p dc n", p=P),
                            d=usl_b.rearrange("p (dc n) -> p dc n", n=512):
                            E.dma_start(out=d, in_=s),
                            reads=[r_spill["wu" + str(ex)]], writes=[r_wgwu[1]], dma=True)
                    for fc4 in range(4):
                        fc = fb * 4 + fc4
                        em.barrier_read("t", [r_wgwu[0], r_wgwu[1], r_ln2blk, r_wbc])
                        for which in range(2):
                            wslot = gsl_b if which == 0 else usl_b
                            bps = 0 if which == 0 else 2
                            for dc in range(ND):
                                lhs = wslot[:, dc * 512 + fc4 * P: dc * 512 + (fc4 + 1) * P]
                                for th in range(2):
                                    bank = b[bps + th]
                                    rhs = ln2blk[:, dc * 1024 + th * 512: dc * 1024 + (th + 1) * 512]
                                    dst = pb0[:, (bps + th) * 512:(bps + th + 1) * 512]
                                    if dc == 0:
                                        mm_open(bank)
                                        mm_raw(dst, lhs, rhs, True, False)
                                    elif dc == ND - 1:
                                        mm_close(dst, lhs, rhs, False, bank,
                                                 [r_wgwu[which], r_ln2blk])
                                    else:
                                        mm_raw(dst, lhs, rhs, False, False)
                        em.emit("s", lambda E: E.activation(out=s1b[:, 0:1024], in_=pb0[:, 0:1024],
                                                            func=AF.Silu, bias=0.0, scale=1.0),
                                reads=[b[0], b[1]], writes=[r_s12])
                        em.emit("v", lambda E: E.tensor_tensor(s2b[:, 0:1024], pb0[:, 1024:2048],
                                                               wbc[:, 0:1024], ALU.mult),
                                reads=[b[2], b[3], r_wbc], writes=[r_s12])
                        em.emit("p", lambda E, d=hbuf[:, fc * 1024:(fc + 1) * 1024]:
                                E.tensor_tensor(d, s1b[:, 0:1024], s2b[:, 0:1024], ALU.mult),
                                reads=[r_s12], writes=[r_h])
                for dblk in range(4):
                    dsl = wdsl[dblk % 2]
                    em.emit("s", lambda E, s=wd_c[ex, :, dblk * 512:(dblk + 1) * 512]
                            .rearrange("(fc p) n -> p fc n", p=P),
                            d=dsl.rearrange("p (fc n) -> p fc n", n=512):
                            E.dma_start(out=d, in_=s),
                            reads=[r_spill["wd" + str(ex)]], writes=[r_wd[dblk % 2]], dma=True)
                    em.barrier_read("t", [r_wd[dblk % 2], r_h])
                    for t128 in range(8):
                        avp = pb6 if t128 % 2 == 0 else pb7
                        bank = b[6 + (t128 % 2)]
                        mm_open(bank)
                        for fc in range(16):
                            lhs = hbuf[:, fc * 1024 + t128 * P: fc * 1024 + (t128 + 1) * P]
                            rhs = dsl[:, fc * 512:(fc + 1) * 512]
                            if fc == 15:
                                mm_close(avp[:, 0:512], lhs, rhs, False, bank,
                                         [r_wd[dblk % 2], r_h])
                            else:
                                mm_raw(avp[:, 0:512], lhs, rhs, (fc == 0), False)
                        pslice = partial[:, t128 * 2048 + dblk * 512: t128 * 2048 + (dblk + 1) * 512]
                        if ex == 0:
                            em.emit("v", lambda E, d=pslice, s=avp[:, 0:512]: E.tensor_copy(d, s),
                                    reads=[bank], writes=[r_part])
                        elif ex < NE - 1:
                            em.emit("v", lambda E, d=pslice, s=avp[:, 0:512]:
                                    E.tensor_tensor(d, s, d, ALU.add),
                                    reads=[bank, r_part], writes=[r_part])
                        else:
                            rows = tb * 1024 + t128 * P
                            xsl = x1b[:, (t128 % 2) * 512:((t128 % 2) + 1) * 512]
                            osl = outst[:, (t128 % 2) * 512:((t128 % 2) + 1) * 512]
                            em.emit("y", lambda E, d=xsl,
                                    s=x1_sp[rows:rows + P, dblk * 512:(dblk + 1) * 512]:
                                    E.dma_start(out=d, in_=s),
                                    reads=[r_spill["x1"]], writes=[r_x1b[t128 % 2]], dma=True)
                            em.emit("v", lambda E, d=osl, s=avp[:, 0:512], x=xsl:
                                    E.tensor_tensor(d, s, x, ALU.add),
                                    reads=[bank, r_x1b[t128 % 2]], writes=[r_outst[t128 % 2]])
                            em.emit("v", lambda E, d=osl, p_=pslice:
                                    E.tensor_tensor(d, d, p_, ALU.add),
                                    reads=[r_part], writes=[r_outst[t128 % 2]])
                            em.emit("p", lambda E, s=osl,
                                    d=out_h[rows:rows + P, dblk * 512:(dblk + 1) * 512]:
                                    E.dma_start(out=d, in_=s),
                                    reads=[r_outst[t128 % 2]], writes=[r_spill["out"]], dma=True)

        em.wait("p", "dp", em.cnt["dp"])

        with nc.Block() as block:
            block.sync(lambda E: [f(E) for f in em.th["y"]])
            block.tensor(lambda E: [f(E) for f in em.th["t"]])
            block.vector(lambda E: [f(E) for f in em.th["v"]])
            block.scalar(lambda E: [f(E) for f in em.th["s"]])
            block.gpsimd(lambda E: [f(E) for f in em.th["p"]])

    return nc


_NC_CACHE = {}


def kernel(L0_attn_hidden_states, L0_attn_Wq, L0_attn_Wk, L0_attn_Wv, L0_attn_Wo,
           L0_ffn_Wg, L0_ffn_Wu, L0_ffn_Wd, L0_ffn_Wr, L0_ffn_temperature):
    x = np.ascontiguousarray(np.asarray(L0_attn_hidden_states, dtype=np.float32))
    base = {
        "wq": np.ascontiguousarray(np.asarray(L0_attn_Wq, dtype=np.float32)),
        "wk": np.ascontiguousarray(np.asarray(L0_attn_Wk, dtype=np.float32)),
        "wv": np.ascontiguousarray(np.asarray(L0_attn_Wv, dtype=np.float32)),
        "wo": np.ascontiguousarray(np.asarray(L0_attn_Wo, dtype=np.float32)),
        "wg": np.ascontiguousarray(np.asarray(L0_ffn_Wg, dtype=np.float32)),
        "wu": np.ascontiguousarray(np.asarray(L0_ffn_Wu, dtype=np.float32)),
        "wd": np.ascontiguousarray(np.asarray(L0_ffn_Wd, dtype=np.float32)),
        "wr": np.ascontiguousarray(np.asarray(L0_ffn_Wr, dtype=np.float32)),
        "temp": np.asarray(L0_ffn_temperature, dtype=np.float32).reshape(1, 1),
    }
    base.update(host_consts())
    if "nc" not in _NC_CACHE:
        _NC_CACHE["nc"] = build(debug=False)
    nc = _NC_CACHE["nc"]
    in_maps = [dict(base, x0=x[c]) for c in range(8)]
    res = run_bass_kernel_spmd(nc, in_maps, core_ids=list(range(8)))
    return np.stack([r["out"] for r in res.results], axis=0).astype(np.float32)


# version 11
# speedup vs baseline: 1.0327x; 1.0327x over previous
import math
import numpy as np
import ml_dtypes
from contextlib import ExitStack

import concourse.bass as bass
import concourse.mybir as mybir
from concourse.bass_utils import run_bass_kernel_spmd

F32 = mybir.dt.float32
BF16 = mybir.dt.bfloat16
AF = mybir.ActivationFunctionType
ALU = mybir.AluOpType
AX = mybir.AxisListType

P = 128
S = 2048
DM = 2048
NH = 16
HD = 128
NE = 8
FF = 2048
NT = S // P
ND = DM // P
SOFT_SCALE = float(np.float32(1.0 / math.sqrt(HD)))
EPS = float(np.float32(1e-5))
GOLD_C = float(np.float32(1.0 / np.e))
GOLD_L = float(np.float32(0.5 - np.log(4.0 / 3.0)))
GOLD_U = float(np.float32(0.5))

ENGS = ("t", "v", "s", "p", "y")


class Res:
    __slots__ = ("w", "r")

    def __init__(self):
        self.w = None
        self.r = []


def seed(*olds):
    n = Res()
    toks = []
    for o in olds:
        toks.extend(o.r)
        if o.w is not None:
            toks.append(o.w)
    n.r = toks
    return n


class Em:
    def __init__(self, nc):
        self.nc = nc
        self.th = {e: [] for e in ENGS}
        self.sem = {}
        self.cnt = {}
        self.waited = {e: {} for e in ENGS}

    def mksem(self, ctx, names):
        for n in names:
            self.sem[n] = ctx.enter_context(self.nc.semaphore(n))
            self.cnt[n] = 0

    def wait(self, eng, sname, val):
        if val is None or val <= 0:
            return
        if self.waited[eng].get(sname, 0) >= val:
            return
        self.waited[eng][sname] = val
        h = self.sem[sname]
        self.th[eng].append(lambda E, h=h, v=val: E.wait_ge(h, v))

    def _wait_tok(self, eng, tok):
        self.wait(eng, tok[0], tok[1])

    def barrier_read(self, eng, reads):
        for r in reads:
            if r.w is not None:
                self._wait_tok(eng, r.w)

    def warw(self, eng, res):
        for rr in res.r:
            self._wait_tok(eng, rr)
        if res.w is not None:
            self._wait_tok(eng, res.w)

    def raw(self, eng, fn):
        self.th[eng].append(fn)

    def inc(self, eng, fn, dma=False):
        if dma:
            sname = {"y": "dy", "p": "dp", "s": "ds", "v": "dv"}[eng]
            step = 16
        else:
            sname = eng
            step = 1
        h = self.sem[sname]
        self.cnt[sname] += step
        self.th[eng].append(lambda E, fn=fn, h=h, s=step: fn(E).then_inc(h, s))
        return (sname, self.cnt[sname])

    def emit(self, eng, fn, reads=(), writes=(), dma=False):
        for r in reads:
            if r.w is not None:
                self._wait_tok(eng, r.w)
        for w in writes:
            self.warw(eng, w)
        tok = self.inc(eng, fn, dma=dma)
        for r in reads:
            r.r.append(tok)
        for w in writes:
            w.w = tok
            w.r = []
        return tok


def _np_rope_tables():
    inv_freq = 1.0 / 10000.0 ** (np.arange(0, HD, 2, dtype=np.float32) / HD)
    t = np.arange(S, dtype=np.float32)
    freqs = np.einsum("i,j->ij", t, inv_freq)
    emb = np.concatenate((freqs, freqs), axis=-1)
    cosT = np.ascontiguousarray(np.cos(emb).astype(np.float32).T)
    sinT = np.ascontiguousarray(np.sin(emb).astype(np.float32).T)
    sin_rope = sinT.copy()
    sin_rope[:64, :] *= -1.0
    return cosT, sin_rope


def host_consts():
    ident = np.eye(P, dtype=np.float32).astype(ml_dtypes.bfloat16)
    cosT, sinT = _np_rope_tables()
    mask = np.where(np.arange(P)[:, None] <= np.arange(P)[None, :], 1.0, 0.0)
    maskc = mask.astype(np.float32).astype(ml_dtypes.bfloat16)
    rev = np.broadcast_to(np.arange(NE, 0, -1, dtype=np.float32)[None, :], (P, NE)).copy()
    ones1 = np.ones((1, P), dtype=np.float32)
    ones1b = np.ones((1, P), dtype=np.float32).astype(ml_dtypes.bfloat16)
    return {"c_ident": ident, "c_cos": cosT, "c_sin": sinT, "c_mask": maskc,
            "c_rev": rev, "c_ones1": ones1, "c_ones1b": ones1b}


def build(debug=False):
    nc = bass.Bass("TRN2", target_bir_lowering=False, dynamic_dma_scratch_size=16384)
    dt = nc.dram_tensor
    kind_sp = "ExternalOutput" if debug else "Internal"

    x0_h = dt("x0", [S, DM], F32, kind="ExternalInput").ap()
    wq_h = dt("wq", [DM, DM], F32, kind="ExternalInput").ap()
    wk_h = dt("wk", [DM, DM], F32, kind="ExternalInput").ap()
    wv_h = dt("wv", [DM, DM], F32, kind="ExternalInput").ap()
    wo_h = dt("wo", [DM, DM], F32, kind="ExternalInput").ap()
    wg_h = dt("wg", [NE, DM, FF], F32, kind="ExternalInput").ap()
    wu_h = dt("wu", [NE, DM, FF], F32, kind="ExternalInput").ap()
    wd_h = dt("wd", [NE, FF, DM], F32, kind="ExternalInput").ap()
    wr_h = dt("wr", [DM, NE], F32, kind="ExternalInput").ap()
    temp_h = dt("temp", [1, 1], F32, kind="ExternalInput").ap()
    c_ident = dt("c_ident", [P, P], BF16, kind="ExternalInput").ap()
    c_cos = dt("c_cos", [P, S], F32, kind="ExternalInput").ap()
    c_sin = dt("c_sin", [P, S], F32, kind="ExternalInput").ap()
    c_mask = dt("c_mask", [P, P], BF16, kind="ExternalInput").ap()
    c_rev = dt("c_rev", [P, NE], F32, kind="ExternalInput").ap()
    c_ones1 = dt("c_ones1", [1, P], F32, kind="ExternalInput").ap()
    c_ones1b = dt("c_ones1b", [1, P], BF16, kind="ExternalInput").ap()

    out_h = dt("out", [S, DM], F32, kind="ExternalOutput").ap()

    dbg1 = dt("dbg_ln1bf", [P, 2048], BF16, kind=kind_sp).ap() if debug else None
    dbg2 = dt("dbg_x0f", [P, 2048], F32, kind=kind_sp).ap() if debug else None
    ln1t_sp = dt("ln1t_sp", [NT, P, ND, P], BF16, kind=kind_sp).ap()
    q_sp = dt("q_sp", [NH, P, S], BF16, kind=kind_sp).ap()
    ao_sp = dt("ao_sp", [NT, P, DM], BF16, kind=kind_sp).ap()
    x1_sp = dt("x1_sp", [S, DM], F32, kind=kind_sp).ap()
    ln2t_sp = dt("ln2t_sp", [2, P, ND, 1024], BF16, kind=kind_sp).ap()
    wg_c = dt("wg_c", [NE, DM, FF], BF16, kind="Internal").ap()
    wu_c = dt("wu_c", [NE, DM, FF], BF16, kind="Internal").ap()
    wd_c = dt("wd_c", [NE, FF, DM], BF16, kind="Internal").ap()

    with ExitStack() as ctx:
        en = ctx.enter_context
        A1 = en(nc.sbuf_tensor("A1", [P, 32768], BF16))
        A2 = en(nc.sbuf_tensor("A2", [P, 32768], BF16))
        A3 = en(nc.sbuf_tensor("A3", [P, 32768], BF16))
        wstage = en(nc.sbuf_tensor("wstage", [P, 2048], F32))
        cast_buf = en(nc.sbuf_tensor("cast_buf", [P, 2048], BF16))
        stt = en(nc.sbuf_tensor("stt", [P, 32], F32))
        sttC = en(nc.sbuf_tensor("sttC", [P, 32], F32))
        rt = en(nc.sbuf_tensor("rt", [P, 176], F32))
        w_all = en(nc.sbuf_tensor("w_all", [P, 128], F32))
        w_bfe = en(nc.sbuf_tensor("w_bfe", [P, 128], BF16))
        ident = en(nc.sbuf_tensor("ident", [P, P], BF16))
        maskc = en(nc.sbuf_tensor("maskc", [P, P], BF16))
        rev = en(nc.sbuf_tensor("rev", [P, NE], F32))
        ones1 = en(nc.sbuf_tensor("ones1", [1, P], F32))
        ones1b = en(nc.sbuf_tensor("ones1b", [1, P], BF16))
        wr_hi = en(nc.sbuf_tensor("wr_hi", [P, ND * NE], BF16))
        wr_lo = en(nc.sbuf_tensor("wr_lo", [P, ND * NE], BF16))
        wr_f = en(nc.sbuf_tensor("wr_f", [P, NE], F32))
        temp_s = en(nc.sbuf_tensor("temp_s", [1, 1], F32))
        pb0 = en(nc.psum_tensor("pb0", [P, 2048], F32))
        pb4 = en(nc.psum_tensor("pb4", [P, 1024], BF16))
        pb5 = en(nc.psum_tensor("pb5", [P, 1024], BF16))
        pb6 = en(nc.psum_tensor("pb6", [P, 512], F32))
        pb7 = en(nc.psum_tensor("pb7", [P, 512], F32))
        wT = A3[0:8, 28672:30720]
        wrow = A3[0:1, 30720:31744]

        em = Em(nc)
        em.mksem(ctx, ["t", "v", "s", "p", "dy", "dp", "ds", "dv"])

        b = [Res() for _ in range(8)]
        r_consts = Res()
        r_tabs = Res()
        r_wr = Res()
        r_temp = Res()
        r_stt = Res()
        r_rt = Res()
        r_xfa = Res()
        r_wstage = [Res(), Res()]
        r_castbuf = [Res(), Res()]
        wslot_ctr = [0]
        r_spill = {k: Res() for k in
                   ("ln1t", "q", "ao", "x1", "ln2t", "out")}
        for nm in ("wg", "wu", "wd"):
            for ex_ in range(NE):
                r_spill[nm + str(ex_)] = Res()

        def mm_open(bank):
            em.warw("t", bank)

        def mm_raw(dst, lhs, rhs, start, stop):
            em.raw("t", lambda E, d=dst, l=lhs, r=rhs, st=start, sp=stop:
                   E.matmul(d, l, r, start=st, stop=sp))

        def mm_close(dst, lhs, rhs, start, bank, readers):
            tok = em.inc("t", lambda E, d=dst, l=lhs, r=rhs, st=start:
                         E.matmul(d, l, r, start=st, stop=True))
            bank.w = tok
            bank.r = []
            for r in readers:
                r.r.append(tok)
            return tok

        def tr_raw(dst, src):
            em.raw("t", lambda E, d=dst, s=src: E.transpose(d, s, ident[:]))

        def tr_close(dst, src, bank, readers):
            tok = em.inc("t", lambda E, d=dst, s=src: E.transpose(d, s, ident[:]))
            bank.w = tok
            bank.r = []
            for r in readers:
                r.r.append(tok)
            return tok

        def prep_chunk(w_rows_ap, dst_fn, dst_res):
            for hf in range(2):
                cb, cres = stage_load_cast(w_rows_ap[:, hf * 1024:(hf + 1) * 1024])
                em.barrier_read("t", [cres, r_consts])
                pb = pb4 if hf == 0 else pb5
                bank = b[4 + hf]
                mm_open(bank)
                for j in range(8):
                    s = cb[:, j * P:(j + 1) * P]
                    d = pb[:, j * P:(j + 1) * P]
                    if j < 7:
                        tr_raw(d, s)
                    else:
                        tr_close(d, s, bank, [cres])
                sp = pb[:, :].rearrange("p (c n) -> p c n", n=P)
                em.emit("v", lambda E, s=sp, d=dst_fn(hf): E.tensor_copy(d, s),
                        reads=[bank], writes=[dst_res])

        def transpose16(src_bf, src_res, dst_fn, dst_res, nhalf=2):
            em.barrier_read("t", [src_res, r_consts])
            for half in range(nhalf):
                pb = pb4 if half % 2 == 0 else pb5
                bank = b[4 + (half % 2)]
                mm_open(bank)
                for j in range(8):
                    dc = half * 8 + j
                    s = src_bf[:, dc * P:(dc + 1) * P]
                    d = pb[:, j * P:(j + 1) * P]
                    if j < 7:
                        tr_raw(d, s)
                    else:
                        tr_close(d, s, bank, [src_res])
                sp = pb[:, :].rearrange("p (c n) -> p c n", n=P)
                em.emit("v", lambda E, s=sp, d=dst_fn(half): E.tensor_copy(d, s),
                        reads=[bank], writes=[dst_res])

        for dst_ap, src_ap, res in ((ident[:], c_ident, r_consts), (maskc[:], c_mask, r_consts),
                                    (rev[:], c_rev, r_consts), (ones1[:], c_ones1, r_consts),
                                    (ones1b[:], c_ones1b, r_consts), (temp_s[:], temp_h, r_temp)):
            em.emit("y", lambda E, d=dst_ap, s=src_ap: E.dma_start(out=d, in_=s),
                    writes=[res], dma=True)
        cosT = A3[:, 0:4096].bitcast(F32)
        sinT = A3[:, 4096:8192].bitcast(F32)
        em.emit("y", lambda E: E.dma_start(out=cosT, in_=c_cos), writes=[r_tabs], dma=True)
        em.emit("y", lambda E: E.dma_start(out=sinT, in_=c_sin), writes=[r_tabs], dma=True)
        for dc_ in range(ND):
            em.emit("y", lambda E, s=wr_h[dc_ * P:(dc_ + 1) * P, :]: E.dma_start(out=wr_f[:], in_=s),
                    writes=[r_wr], dma=True)
            em.emit("v", lambda E, d=wr_hi[:, dc_ * NE:(dc_ + 1) * NE]: E.tensor_copy(d, wr_f[:]),
                    reads=[r_wr], writes=[r_wr])
            em.emit("v", lambda E, d=wr_lo[:, dc_ * NE:(dc_ + 1) * NE],
                    h=wr_hi[:, dc_ * NE:(dc_ + 1) * NE]:
                    E.tensor_tensor(d, wr_f[:], h, ALU.subtract),
                    reads=[r_wr], writes=[r_wr])
        em.emit("t", lambda E: E.matmul(pb6[:, 0:1], ones1[:], temp_s[:], start=True, stop=True),
                reads=[r_consts, r_temp], writes=[b[6]])
        em.emit("v", lambda E: E.tensor_copy(stt[:, 15:16], pb6[:, 0:1]), reads=[b[6]], writes=[r_stt])
        em.emit("v", lambda E: E.memset(stt[:, 23:24], EPS), writes=[r_stt])
        em.emit("v", lambda E: E.memset(stt[:, 30:31], -20.0), writes=[r_stt])
        em.emit("v", lambda E: E.reciprocal(stt[:, 9:10], stt[:, 15:16]), reads=[r_stt], writes=[r_stt])

        cache_jobs = []
        for ex in range(NE):
            for mat_h, mat_c, name in ((wg_h, wg_c, "wg"), (wu_h, wu_c, "wu"), (wd_h, wd_c, "wd")):
                for rc in range(ND):
                    for hf in range(2):
                        cache_jobs.append((mat_h[ex, rc * P:(rc + 1) * P, hf * 1024:(hf + 1) * 1024],
                                           mat_c[ex, rc * P:(rc + 1) * P, hf * 1024:(hf + 1) * 1024],
                                           name + str(ex)))
        cache_pos = [0]

        def stage_load_cast(src_ap):
            wv = wstage[:, 0:1024]
            cb = cast_buf[:, 0:1024]
            em.emit("y", lambda E, d=wv, s=src_ap: E.dma_start(out=d, in_=s),
                    writes=[r_wstage[0]], dma=True)
            em.emit("v", lambda E, d=cb, s=wv: E.tensor_copy(d, s),
                    reads=[r_wstage[0]], writes=[r_castbuf[0]])
            return cb, r_castbuf[0]

        r_cstage = [Res() for _ in range(4)]
        r_ccast = [Res() for _ in range(4)]
        cache_sub = [0]

        def emit_cache_chunks(k):
            for _ in range(k):
                if cache_pos[0] >= len(cache_jobs):
                    return
                src, dst, name = cache_jobs[cache_pos[0]]
                cache_pos[0] += 1
                for sub in range(4):
                    sl = cache_sub[0] % 4
                    cache_sub[0] += 1
                    qe = "y" if sl % 2 == 0 else "p"
                    wv = wstage[:, 1024 + sl * 256: 1024 + (sl + 1) * 256]
                    cb = cast_buf[:, 1024 + sl * 256: 1024 + (sl + 1) * 256]
                    em.emit(qe, lambda E, d=wv, s=src[:, sub * 256:(sub + 1) * 256]:
                            E.dma_start(out=d, in_=s),
                            writes=[r_cstage[sl]], dma=True)
                    em.emit("v", lambda E, d=cb, s=wv: E.tensor_copy(d, s),
                            reads=[r_cstage[sl]], writes=[r_ccast[sl]])
                    em.emit("s", lambda E, d=dst[:, sub * 256:(sub + 1) * 256], s=cb:
                            E.dma_start(out=d, in_=s),
                            reads=[r_ccast[sl]], writes=[r_spill[name]], dma=True)

        def emit_ln(src_ap, src_res, dst_ap, dst_res, dump_ap, dump_writes, lo=None):
            em.emit("v", lambda E: E.reduce_sum(stt[:, 0:1], src_ap, axis=AX.X),
                    reads=[src_res], writes=[r_stt])
            em.emit("s", lambda E: E.mul(stt[:, 1:2], stt[:, 0:1], -1.0 / DM),
                    reads=[r_stt], writes=[r_stt])
            em.emit("s", lambda E: E.activation(out=dump_ap, in_=src_ap, func=AF.Square,
                                                bias=stt[:, 1:2], scale=1.0,
                                                accum_out=stt[:, 2:3]),
                    reads=[src_res, r_stt], writes=[r_stt] + list(dump_writes))
            em.emit("s", lambda E: E.activation(out=stt[:, 3:4], in_=stt[:, 2:3], func=AF.Ln,
                                                bias=stt[:, 23:24], scale=float(np.float32(1.0 / DM))),
                    reads=[r_stt], writes=[r_stt])
            em.emit("s", lambda E: E.activation(out=stt[:, 4:5], in_=stt[:, 3:4], func=AF.Exp,
                                                bias=0.0, scale=-0.5),
                    reads=[r_stt], writes=[r_stt])
            if lo is None:
                em.emit("v", lambda E: E.tensor_scalar(out=dst_ap, in0=src_ap,
                                                       scalar1=stt[:, 1:2], scalar2=stt[:, 4:5],
                                                       op0=ALU.add, op1=ALU.mult),
                        reads=[src_res, r_stt], writes=[dst_res])
            else:
                lnf_ap, lnf_res, lo_ap, lo_res = lo
                em.emit("v", lambda E: E.tensor_scalar(out=lnf_ap, in0=src_ap,
                                                       scalar1=stt[:, 1:2], scalar2=stt[:, 4:5],
                                                       op0=ALU.add, op1=ALU.mult),
                        reads=[src_res, r_stt], writes=[lnf_res])
                em.emit("s", lambda E: E.activation(out=dst_ap, in_=lnf_ap, func=AF.Copy,
                                                    bias=0.0, scale=1.0),
                        reads=[lnf_res], writes=[dst_res])
                em.emit("v", lambda E: E.tensor_tensor(lo_ap, lnf_ap, dst_ap, ALU.subtract),
                        reads=[lnf_res, dst_res], writes=[lo_res])

        r_ln1T = Res()
        r_A2s = [Res(), Res()]
        ln1bf2 = [A2[:, 0:2048], A2[:, 2048:4096]]
        xf_p1 = [A2[:, 4096:8192].bitcast(F32), A2[:, 8192:12288].bitcast(F32)]
        r_xf_p1 = [Res(), Res()]
        A1v = A1[:, :].rearrange("p (dc n) -> p dc n", n=2048)
        for tt in range(NT):
            xsl = tt % 2
            em.emit("y", lambda E, d=xf_p1[xsl][:], s=x0_h[tt * P:(tt + 1) * P, :]: E.dma_start(out=d, in_=s),
                    writes=[r_xf_p1[xsl]], dma=True)
            emit_ln(xf_p1[xsl][:], r_xf_p1[xsl], ln1bf2[xsl], r_A2s[xsl], pb0[:, 0:2048],
                    [b[0], b[1], b[2], b[3]])
            transpose16(ln1bf2[xsl], r_A2s[xsl],
                        lambda half, tt=tt: A1v[:, half * 8:half * 8 + 8, tt * P:(tt + 1) * P],
                        r_ln1T)
            em.emit("p", lambda E, s=A1v[:, :, tt * P:(tt + 1) * P], d=ln1t_sp[tt]:
                    E.dma_start(out=d, in_=s),
                    reads=[r_ln1T], writes=[r_spill["ln1t"]], dma=True)
            emit_cache_chunks(12)

        r_t12 = Res()
        r_wqt = seed(r_A2s[0], r_A2s[1])
        r_wkt = Res()
        r_K = seed(r_wqt, r_xf_p1[0], r_xf_p1[1])
        t12f = A3[:, 8192:12288].bitcast(F32)
        t1 = t12f[:, 0:512]
        t2 = t12f[:, 512:1024]
        qbf2 = [A3[:, 12288:12800], A3[:, 12800:13312]]
        r_qbf = [Res(), Res()]
        wkt_st = A3[:, 13312:15360]

        def emit_proj(w_src_h, mode):
            for oc in range(NH):
                if mode == "q":
                    wt_res, wt_buf = r_wqt, A2[:, 0:2048]
                else:
                    wt_res, wt_buf = r_wkt, wkt_st
                wtv = wt_buf.rearrange("p (c n) -> p c n", n=P)
                prep_chunk(w_src_h[oc * P:(oc + 1) * P, :],
                           lambda half, wtv=wtv: wtv[:, half * 8:half * 8 + 8, :], wt_res)
                em.barrier_read("t", [wt_res, r_ln1T])
                for tb in range(4):
                    bank = b[tb]
                    for dc in range(ND):
                        lhs = wt_buf[:, dc * P:(dc + 1) * P]
                        rhs = A1[:, dc * 2048 + tb * 512: dc * 2048 + (tb + 1) * 512]
                        dst = pb0[:, tb * 512:(tb + 1) * 512]
                        if dc == 0:
                            mm_open(bank)
                            mm_raw(dst, lhs, rhs, True, False)
                        elif dc == ND - 1:
                            mm_close(dst, lhs, rhs, False, bank, [wt_res, r_ln1T])
                        else:
                            mm_raw(dst, lhs, rhs, False, False)
                    ps = pb0[:, tb * 512:(tb + 1) * 512]
                    ps_lo = pb0[64:128, tb * 512:(tb + 1) * 512]
                    ps_hi = pb0[0:64, tb * 512:(tb + 1) * 512]
                    cs = cosT[:, tb * 512:(tb + 1) * 512]
                    sn = sinT[:, tb * 512:(tb + 1) * 512]
                    em.emit("v", lambda E, p_=ps, c_=cs: E.tensor_tensor(t1[:, :], p_, c_, ALU.mult),
                            reads=[bank, r_tabs], writes=[r_t12])
                    em.emit("v", lambda E, p_=ps_lo, s_=sn[0:64, :]:
                            E.tensor_tensor(t2[0:64, :], p_, s_, ALU.mult),
                            reads=[bank, r_tabs], writes=[r_t12])
                    em.emit("v", lambda E, p_=ps_hi, s_=sn[64:128, :]:
                            E.tensor_tensor(t2[64:128, :], p_, s_, ALU.mult),
                            reads=[bank, r_tabs], writes=[r_t12])
                    if mode == "q":
                        qsl = (oc * 4 + tb) % 2
                        em.emit("v", lambda E, d=qbf2[qsl]: E.tensor_tensor(d, t1[:, :], t2[:, :], ALU.add),
                                reads=[r_t12], writes=[r_qbf[qsl]])
                        em.emit("p", lambda E, d=q_sp[oc, :, tb * 512:(tb + 1) * 512], s=qbf2[qsl]:
                                E.dma_start(out=d, in_=s),
                                reads=[r_qbf[qsl]], writes=[r_spill["q"]], dma=True)
                    else:
                        dstk = A2[:, oc * 2048 + tb * 512: oc * 2048 + (tb + 1) * 512]
                        em.emit("v", lambda E, d=dstk: E.tensor_tensor(d, t1[:, :], t2[:, :], ALU.add),
                                reads=[r_t12], writes=[r_K])
                emit_cache_chunks(8)

        emit_proj(wq_h, "q")
        emit_proj(wk_h, "k")

        VW = NH * 129
        r_wvt = [seed(r_ln1T), seed(r_ln1T)]
        r_slice = [seed(r_ln1T), seed(r_ln1T)]
        r_V = seed(r_tabs, r_t12, r_wkt, r_qbf[0], r_qbf[1], r_ln1T)

        def vtile(tt):
            if tt < 15:
                return A3[:, tt * VW:(tt + 1) * VW]
            return A1[:, 20480:20480 + VW]
        for ob in range(4):
            slot = ob % 2
            wv_slot = A1[:, slot * 8192:(slot + 1) * 8192]
            wvv = wv_slot.rearrange("p (dc n) -> p dc n", n=512)
            for j in range(4):
                oc = ob * 4 + j
                prep_chunk(wv_h[oc * P:(oc + 1) * P, :],
                           lambda half, wvv=wvv, j=j: wvv[:, half * 8:half * 8 + 8, j * P:(j + 1) * P],
                           r_wvt[slot])
            for tt in range(NT):
                sl = tt % 2
                sbuf = A1[:, 16384 + sl * 2048: 16384 + (sl + 1) * 2048]
                em.emit("y", lambda E, s=ln1t_sp[tt], d=sbuf.rearrange("p (c n) -> p c n", n=P):
                        E.dma_start(out=d, in_=s),
                        reads=[r_spill["ln1t"]], writes=[r_slice[sl]], dma=True)
                bank = b[tt % 4]
                em.barrier_read("t", [r_slice[sl], r_wvt[slot]])
                dst = pb0[:, (tt % 4) * 512:((tt % 4) + 1) * 512]
                for dc in range(ND):
                    lhs = sbuf[:, dc * P:(dc + 1) * P]
                    rhs = wv_slot[:, dc * 512:(dc + 1) * 512]
                    if dc == 0:
                        mm_open(bank)
                        mm_raw(dst, lhs, rhs, True, False)
                    elif dc == ND - 1:
                        mm_close(dst, lhs, rhs, False, bank, [r_slice[sl], r_wvt[slot]])
                    else:
                        mm_raw(dst, lhs, rhs, False, False)
                vdst = vtile(tt).rearrange("p (h n) -> p h n", n=129)[:, ob * 4:(ob + 1) * 4, 0:128]
                em.emit("s", lambda E, d=vdst, s=dst.rearrange("p (h n) -> p h n", n=P):
                        E.activation(out=d, in_=s, func=AF.Copy, bias=0.0, scale=1.0),
                        reads=[bank], writes=[r_V])
                if tt % 4 == 3:
                    emit_cache_chunks(2)
        for tt in range(NT):
            em.emit("v", lambda E, d=vtile(tt).rearrange("p (h n) -> p h n", n=129)[:, :, 128:129]:
                    E.memset(d, 1.0), writes=[r_V])

        r_qh = [seed(r_wvt[0], r_wvt[1]), seed(r_wvt[0], r_wvt[1])]
        r_expc = [seed(r_wvt[0], r_wvt[1]) for _ in range(3)]
        r_stage = [seed(r_wvt[0], r_wvt[1]), seed(r_wvt[0], r_wvt[1])]
        r_sC = Res()
        qh_buf = [A1[:, 0:2048], A1[:, 2048:4096]]
        expc = [A1[:, 4096:4608], A1[:, 4608:5120], A1[:, 5120:5632]]
        stg = [A1[:, 5632:6144], A1[:, 6144:6656]]
        ec_ctr = [0]

        def qh_load(hh):
            sl = hh % 2
            em.emit("y", lambda E, d=qh_buf[sl][:], s=q_sp[hh]: E.dma_start(out=d, in_=s),
                    reads=[r_spill["q"]], writes=[r_qh[sl]], dma=True)

        def score_exp(hh, Q, kk):
            sl = hh % 2
            ep = ec_ctr[0] % 3
            ep2 = ec_ctr[0] % 2
            ec_ctr[0] += 1
            sb = b[6 + ep2]
            spp = pb6 if ep2 == 0 else pb7
            em.barrier_read("t", [r_K, r_qh[sl]])
            em.warw("t", sb)
            tok = em.inc("t", lambda E, d=spp[:, 0:512],
                         l=A2[:, hh * 2048 + kk * P: hh * 2048 + (kk + 1) * P],
                         r=qh_buf[sl][:, Q * 512:(Q + 1) * 512]:
                         E.matmul(d, l, r, start=True, stop=True))
            sb.w = tok
            sb.r = []
            r_K.r.append(tok)
            r_qh[sl].r.append(tok)
            em.emit("s", lambda E, s=spp[:, 0:512], d=expc[ep][:]:
                    E.activation(out=d, in_=s, func=AF.Exp, bias=stt[:, 30:31],
                                 scale=SOFT_SCALE),
                    reads=[sb, r_stt], writes=[r_expc[ep]])
            if kk >= 4 * Q:
                jd = kk - 4 * Q
                em.emit("v", lambda E, d=expc[ep][:, jd * P:(jd + 1) * P]:
                        E.tensor_tensor(d, d, maskc[:], ALU.mult),
                        reads=[r_consts], writes=[r_expc[ep]])
            return ep

        def avs(hh, Q, kk, ep):
            sg = Q % 2
            jls = list(range(max(kk - 4 * Q, 0), 4))
            em.barrier_read("t", [r_expc[ep], r_V])
            for idx, jl in enumerate(jls):
                qt = 4 * Q + jl
                bank = b[jl]
                dst = pb0[:, jl * 512: jl * 512 + 129]
                lhs = expc[ep][:, jl * P:(jl + 1) * P]
                rhs = vtile(kk)[:, hh * 129:(hh + 1) * 129]
                if kk == 0:
                    em.warw("t", bank)
                if kk == qt or idx == len(jls) - 1:
                    tok = em.inc("t", lambda E, d=dst, l=lhs, r=rhs,
                                 s1=(kk == 0), s2=(kk == qt):
                                 E.matmul(d, l, r, start=s1, stop=s2))
                    r_expc[ep].r.append(tok)
                    r_V.r.append(tok)
                    if kk == qt:
                        bank.w = tok
                        bank.r = []
                        sc = (Q % 2) * 4 + jl
                        em.emit("v", lambda E, d=sttC[:, sc:sc + 1],
                                s=pb0[:, jl * 512 + 128: jl * 512 + 129]:
                                E.reciprocal(d, s),
                                reads=[bank], writes=[r_sC])
                        em.emit("v", lambda E, d=stg[sg][:, jl * P:(jl + 1) * P],
                                s=pb0[:, jl * 512: jl * 512 + 128],
                                r_=sttC[:, sc:sc + 1]:
                                E.tensor_scalar(out=d, in0=s, scalar1=r_,
                                                scalar2=None, op0=ALU.mult),
                                reads=[bank, r_sC], writes=[r_stage[sg]])
                else:
                    mm_raw(dst, lhs, rhs, (kk == 0), False)

        def stage_dma(hh, Q):
            sg = Q % 2
            em.emit("p", lambda E, s=stg[sg].rearrange("p (t n) -> p t n", n=P),
                    d=ao_sp[4 * Q:4 * Q + 4, :, hh * P:(hh + 1) * P].rearrange("t p n -> p t n"):
                    E.dma_start(out=d, in_=s),
                    reads=[r_stage[sg]], writes=[r_spill["ao"]], dma=True)

        qh_load(0)
        qh_load(1)
        for hh in range(NH):
            chunks = [(Q, kk) for Q in range(4) for kk in range(4 * Q + 4)]
            eps = {0: score_exp(hh, *chunks[0]), 1: score_exp(hh, *chunks[1])}
            for i, (Q, kk) in enumerate(chunks):
                if i + 2 < len(chunks):
                    eps[i + 2] = score_exp(hh, *chunks[i + 2])
                avs(hh, Q, kk, eps[i])
                if kk == 4 * Q + 3:
                    stage_dma(hh, Q)
            if hh + 2 < NH:
                qh_load(hh + 2)
            emit_cache_chunks(12)

        r_WoT = seed(r_qh[0], r_qh[1], r_expc[0], r_expc[1], r_stage[0], r_stage[1],
                     r_slice[0], r_slice[1], r_V)
        r_ao = [seed(r_K), seed(r_K)]
        r_aot = seed(r_K)
        r_ln2hi = seed(r_K)
        r_ln2lo = seed(r_K)
        r_hiT = seed(r_K)
        r_loT = seed(r_K)
        r_lnf = seed(r_V)
        r_sqd = seed(r_V)
        r_wall = Res()
        r_wT = seed(r_V)

        A1w = A1[:, :].rearrange("p (hc n) -> p hc n", n=2048)
        for oc in range(NH):
            prep_chunk(wo_h[oc * P:(oc + 1) * P, :],
                       lambda half, oc=oc: A1w[:, half * 8:half * 8 + 8, oc * P:(oc + 1) * P],
                       r_WoT)

        aosl = [A2[:, 0:2048], A2[:, 2048:4096]]
        aotb = A2[:, 4096:6144]
        ln2hi = A2[:, 6144:8192]
        ln2lo = A2[:, 8192:10240]
        hiT = A2[:, 10240:12288]
        loT = A2[:, 12288:14336]
        lnf = A3[:, 0:4096].bitcast(F32)
        sqd = A3[:, 8192:12288].bitcast(F32)[:, 0:2048]
        xfa2 = [A2[:, 14336:18432].bitcast(F32), A2[:, 18432:22528].bitcast(F32)]
        r_xfa2 = [seed(r_K), seed(r_K)]
        aotv = aotb.rearrange("p (c n) -> p c n", n=P)
        hiTv = hiT.rearrange("p (c n) -> p c n", n=P)
        loTv = loT.rearrange("p (c n) -> p c n", n=P)

        def c8(i):
            return rt[:, i * NE:(i + 1) * NE]

        (RZ, REX, RINH, RD0, RDIST, REX1, RFB, RZONE, RWT, RSELR, RFB2,
         RFBM, RFBW, RG1, RG2, RW0, RDIFF, RDSC, RINV, RSEL2R, RT1, RT2) = range(22)

        def c2_stageA(tt):
            sl = tt % 2
            xf = xfa2[sl]
            rxf = r_xfa2[sl]
            em.emit("y", lambda E, s=ao_sp[tt], d=aosl[sl]: E.dma_start(out=d, in_=s),
                    reads=[r_spill["ao"]], writes=[r_ao[sl]], dma=True)
            em.emit("y", lambda E, d=xf[:], s=x0_h[tt * P:(tt + 1) * P, :]: E.dma_start(out=d, in_=s),
                    writes=[rxf], dma=True)
            transpose16(aosl[sl], r_ao[sl],
                        lambda half: aotv[:, half * 8:half * 8 + 8, :], r_aot)
            em.barrier_read("t", [r_aot, r_WoT])
            for hc in range(NH):
                for ob in range(4):
                    bank = b[ob]
                    lhs = aotb[:, hc * P:(hc + 1) * P]
                    rhs = A1[:, hc * 2048 + ob * 512: hc * 2048 + (ob + 1) * 512]
                    dst = pb0[:, ob * 512:(ob + 1) * 512]
                    if hc == 0:
                        mm_open(bank)
                        mm_raw(dst, lhs, rhs, True, False)
                    elif hc == NH - 1:
                        mm_close(dst, lhs, rhs, False, bank, [r_aot, r_WoT])
                    else:
                        mm_raw(dst, lhs, rhs, False, False)
            for ob in range(4):
                em.emit("v", lambda E, s=pb0[:, ob * 512:(ob + 1) * 512],
                        x=xf[:, ob * 512:(ob + 1) * 512]:
                        E.scalar_tensor_tensor(out=x, in0=s, scalar=1.0, in1=x,
                                               op0=ALU.mult, op1=ALU.add),
                        reads=[b[ob]], writes=[rxf])
            em.emit("p", lambda E, d=x1_sp[tt * P:(tt + 1) * P, :], s=xf[:]: E.dma_start(out=d, in_=s),
                    reads=[rxf], writes=[r_spill["x1"]], dma=True)

        def c2_stageB(tt):
            sl = tt % 2
            xf = xfa2[sl]
            rxf = r_xfa2[sl]
            emit_ln(xf[:], rxf, ln2hi, r_ln2hi, sqd, [r_sqd],
                    lo=(lnf[:, 0:2048], r_lnf, ln2lo, r_ln2lo))
            transpose16(ln2hi, r_ln2hi, lambda half: hiTv[:, half * 8:half * 8 + 8, :], r_hiT)
            transpose16(ln2lo, r_ln2lo, lambda half: loTv[:, half * 8:half * 8 + 8, :], r_loT)
            tb2, off = tt // 8, (tt % 8) * P
            em.emit("p", lambda E, d=ln2t_sp[tb2, :, :, off:off + P], s=hiTv:
                    E.dma_start(out=d, in_=s),
                    reads=[r_hiT], writes=[r_spill["ln2t"]], dma=True)
            em.barrier_read("t", [r_hiT, r_loT, r_wr])
            mm_open(b[6])
            k = 0
            nmm = ND * 3
            for dc in range(ND):
                for (lb, rb) in ((hiT, wr_hi), (loT, wr_hi), (hiT, wr_lo)):
                    lhs = lb[:, dc * P:(dc + 1) * P]
                    rhs = rb[:, dc * NE:(dc + 1) * NE]
                    if k == nmm - 1:
                        mm_close(pb6[:, 0:NE], lhs, rhs, False, b[6], [r_hiT, r_loT, r_wr])
                    else:
                        mm_raw(pb6[:, 0:NE], lhs, rhs, (k == 0), False)
                    k += 1
            em.emit("v", lambda E: E.tensor_scalar(out=c8(RZ), in0=pb6[:, 0:NE],
                                                   scalar1=stt[:, 9:10], scalar2=None, op0=ALU.mult),
                    reads=[b[6], r_stt], writes=[r_rt])
            em.emit("s", lambda E: E.activation(out=c8(REX), in_=c8(RZ), func=AF.Exp,
                                                bias=0.0, scale=-1.0),
                    reads=[r_rt], writes=[r_rt])
            em.emit("v", lambda E: E.tensor_scalar(out=c8(RT1), in0=c8(REX), scalar1=1.0,
                                                   scalar2=None, op0=ALU.add),
                    reads=[r_rt], writes=[r_rt])
            em.emit("v", lambda E: E.reciprocal(c8(RINH), c8(RT1)), reads=[r_rt], writes=[r_rt])
            em.emit("v", lambda E: E.tensor_scalar(out=c8(RD0), in0=c8(RINH), scalar1=GOLD_C,
                                                   scalar2=None, op0=ALU.subtract),
                    reads=[r_rt], writes=[r_rt])
            em.emit("s", lambda E: E.activation(out=c8(RDIST), in_=c8(RD0), func=AF.Abs,
                                                bias=0.0, scale=1.0),
                    reads=[r_rt], writes=[r_rt])
            em.emit("s", lambda E: E.activation(out=c8(REX1), in_=c8(RDIST), func=AF.Exp,
                                                bias=0.0, scale=-10.0),
                    reads=[r_rt], writes=[r_rt])
            em.emit("s", lambda E: E.activation(out=c8(RFB), in_=c8(RDIST), func=AF.Exp,
                                                bias=0.0, scale=float(np.float32(-1.0 / 0.3))),
                    reads=[r_rt], writes=[r_rt])
            em.emit("v", lambda E: E.tensor_single_scalar(c8(RG1), c8(RINH), GOLD_L, ALU.is_ge),
                    reads=[r_rt], writes=[r_rt])
            em.emit("v", lambda E: E.tensor_single_scalar(c8(RG2), c8(RINH), GOLD_U, ALU.is_le),
                    reads=[r_rt], writes=[r_rt])
            em.emit("v", lambda E: E.tensor_tensor(c8(RZONE), c8(RG1), c8(RG2), ALU.mult),
                    reads=[r_rt], writes=[r_rt])
            em.emit("v", lambda E: E.tensor_tensor(c8(RWT), c8(REX1), c8(RZONE), ALU.mult),
                    reads=[r_rt], writes=[r_rt])
            em.emit("v", lambda E: E.reduce_sum(stt[:, 10:11], c8(RWT), axis=AX.X),
                    reads=[r_rt], writes=[r_stt])
            em.emit("v", lambda E: E.tensor_single_scalar(stt[:, 11:12], stt[:, 10:11],
                                                          1e-8, ALU.is_lt),
                    reads=[r_stt], writes=[r_stt])
            em.emit("v", lambda E: E.reduce_max(stt[:, 12:13], c8(RFB), axis=AX.X),
                    reads=[r_rt], writes=[r_stt])
            em.emit("v", lambda E: E.tensor_scalar(out=c8(RT1), in0=c8(RFB), scalar1=stt[:, 12:13],
                                                   scalar2=None, op0=ALU.is_ge),
                    reads=[r_rt, r_stt], writes=[r_rt])
            em.emit("v", lambda E: E.tensor_tensor(c8(RSELR), c8(RT1), rev[:], ALU.mult),
                    reads=[r_rt, r_consts], writes=[r_rt])
            em.emit("v", lambda E: E.reduce_max(stt[:, 13:14], c8(RSELR), axis=AX.X),
                    reads=[r_rt], writes=[r_stt])
            em.emit("v", lambda E: E.tensor_scalar(out=c8(RT1), in0=c8(RSELR), scalar1=stt[:, 13:14],
                                                   scalar2=None, op0=ALU.is_ge),
                    reads=[r_rt, r_stt], writes=[r_rt])
            em.emit("v", lambda E: E.tensor_scalar(out=c8(RINV), in0=c8(RT1), scalar1=-1.0,
                                                   scalar2=1.0, op0=ALU.mult, op1=ALU.add),
                    reads=[r_rt], writes=[r_rt])
            em.emit("v", lambda E: E.tensor_tensor(c8(RFB2), c8(RFB), c8(RINV), ALU.mult),
                    reads=[r_rt], writes=[r_rt])
            em.emit("v", lambda E: E.reduce_max(stt[:, 14:15], c8(RFB2), axis=AX.X),
                    reads=[r_rt], writes=[r_stt])
            em.emit("v", lambda E: E.tensor_scalar(out=c8(RT2), in0=c8(RFB2), scalar1=stt[:, 14:15],
                                                   scalar2=None, op0=ALU.is_ge),
                    reads=[r_rt, r_stt], writes=[r_rt])
            em.emit("v", lambda E: E.tensor_tensor(c8(RSEL2R), c8(RT2), rev[:], ALU.mult),
                    reads=[r_rt, r_consts], writes=[r_rt])
            em.emit("v", lambda E: E.reduce_max(stt[:, 16:17], c8(RSEL2R), axis=AX.X),
                    reads=[r_rt], writes=[r_stt])
            em.emit("v", lambda E: E.tensor_scalar(out=c8(RT2), in0=c8(RSEL2R), scalar1=stt[:, 16:17],
                                                   scalar2=None, op0=ALU.is_ge),
                    reads=[r_rt, r_stt], writes=[r_rt])
            em.emit("v", lambda E: E.tensor_tensor(c8(RFBM), c8(RT1), c8(RT2), ALU.add),
                    reads=[r_rt], writes=[r_rt])
            em.emit("v", lambda E: E.tensor_tensor(c8(RFBW), c8(RFB), c8(RFBM), ALU.mult),
                    reads=[r_rt], writes=[r_rt])
            em.emit("v", lambda E: E.reduce_sum(stt[:, 17:18], c8(RFBW), axis=AX.X),
                    reads=[r_rt], writes=[r_stt])
            em.emit("v", lambda E: E.tensor_single_scalar(stt[:, 18:19], stt[:, 17:18],
                                                          1e-8, ALU.max),
                    reads=[r_stt], writes=[r_stt])
            em.emit("v", lambda E: E.reciprocal(stt[:, 19:20], stt[:, 18:19]),
                    reads=[r_stt], writes=[r_stt])
            em.emit("v", lambda E: E.tensor_scalar(out=c8(RFBW), in0=c8(RFBW), scalar1=stt[:, 19:20],
                                                   scalar2=None, op0=ALU.mult),
                    reads=[r_stt], writes=[r_rt])
            em.emit("v", lambda E: E.tensor_tensor(c8(RDIFF), c8(RFBW), c8(RWT), ALU.subtract),
                    reads=[r_rt], writes=[r_rt])
            em.emit("v", lambda E: E.tensor_scalar(out=c8(RDSC), in0=c8(RDIFF), scalar1=stt[:, 11:12],
                                                   scalar2=None, op0=ALU.mult),
                    reads=[r_rt, r_stt], writes=[r_rt])
            em.emit("v", lambda E: E.tensor_tensor(c8(RW0), c8(RWT), c8(RDSC), ALU.add),
                    reads=[r_rt], writes=[r_rt])
            em.emit("v", lambda E: E.reduce_sum(stt[:, 20:21], c8(RW0), axis=AX.X),
                    reads=[r_rt], writes=[r_stt])
            em.emit("v", lambda E: E.tensor_single_scalar(stt[:, 21:22], stt[:, 20:21],
                                                          1e-8, ALU.max),
                    reads=[r_stt], writes=[r_stt])
            em.emit("v", lambda E: E.reciprocal(stt[:, 22:23], stt[:, 21:22]),
                    reads=[r_stt], writes=[r_stt])
            em.emit("v", lambda E, d=w_all[:, tt * NE:(tt + 1) * NE]:
                    E.tensor_scalar(out=d, in0=c8(RW0), scalar1=stt[:, 22:23],
                                    scalar2=None, op0=ALU.mult),
                    reads=[r_rt, r_stt], writes=[r_wall])
            emit_cache_chunks(4)

        c2_stageA(0)
        for tt in range(NT):
            if tt + 1 < NT:
                c2_stageA(tt + 1)
            c2_stageB(tt)

        em.emit("v", lambda E: E.tensor_copy(w_bfe[:], w_all[:]), reads=[r_wall], writes=[r_wall])
        for tt in range(NT):
            em.barrier_read("t", [r_wall, r_consts])
            em.warw("t", b[4])
            tok = em.inc("t", lambda E, s=w_bfe[:, tt * NE:(tt + 1) * NE]:
                         E.transpose(pb4[0:NE, 0:P], s, ident[:]))
            b[4].w = tok
            b[4].r = []
            r_wall.r.append(tok)
            em.emit("v", lambda E, d=wT[:, tt * P:(tt + 1) * P]: E.tensor_copy(d, pb4[0:NE, 0:P]),
                    reads=[b[4]], writes=[r_wT])

        r_wgwu = [seed(r_WoT), seed(r_WoT)]
        r_wd = [seed(r_WoT), seed(r_WoT)]
        r_ln2blk = seed(r_ao[0], r_ao[1], r_aot, r_ln2hi, r_ln2lo, r_hiT, r_loT, r_xfa2[0], r_xfa2[1])
        r_h = seed(r_ln2blk)
        r_part = seed(r_lnf, r_sqd, r_V)
        r_wbc = seed(r_lnf, r_sqd, r_V)
        r_x1b = [seed(r_lnf, r_sqd, r_V), seed(r_lnf, r_sqd, r_V)]
        r_outst = [seed(r_lnf, r_sqd, r_V), seed(r_lnf, r_sqd, r_V)]
        r_s12 = seed(r_lnf, r_sqd, r_V)
        r_wrow = seed(r_V)

        ln2blk = A2[:, 0:16384]
        hbuf = A2[:, 16384:32768]
        gsl_b = A1[:, 0:8192]
        usl_b = A1[:, 8192:16384]
        wdsl = [A1[:, 16384:24576], A1[:, 24576:32768]]
        partial = A3[:, 0:16384]
        wbc = A3[:, 16384:20480].bitcast(F32)[:, 0:1024]
        x1b = A3[:, 20480:22528].bitcast(F32)
        outst = A3[:, 22528:24576].bitcast(F32)
        s1b = A3[:, 24576:26624]
        s2b = A3[:, 26624:28672]

        emit_cache_chunks(len(cache_jobs))

        for tb in range(2):
            em.emit("y", lambda E, s=ln2t_sp[tb], d=ln2blk.rearrange("p (c n) -> p c n", n=1024):
                    E.dma_start(out=d, in_=s),
                    reads=[r_spill["ln2t"]], writes=[r_ln2blk], dma=True)
            for ex in range(NE):
                em.emit("y", lambda E, s=wT[ex:ex + 1, tb * 1024:(tb + 1) * 1024]:
                        E.dma_start(out=wrow, in_=s),
                        reads=[r_wT], writes=[r_wrow], dma=True)
                for half in range(2):
                    pbx = pb6 if half == 0 else pb7
                    bankx = b[6 + half]
                    em.emit("t", lambda E, d=pbx[:, 0:512], r=wrow[:, half * 512:(half + 1) * 512]:
                            E.matmul(d, ones1b[:], r, start=True, stop=True),
                            reads=[r_wrow, r_consts], writes=[bankx])
                    em.emit("v", lambda E, d=wbc[:, half * 512:(half + 1) * 512], s=pbx[:, 0:512]:
                            E.tensor_copy(d, s),
                            reads=[bankx], writes=[r_wbc])
                for fb in range(4):
                    em.emit("y", lambda E, s=wg_c[ex, :, fb * 512:(fb + 1) * 512]
                            .rearrange("(dc p) n -> p dc n", p=P),
                            d=gsl_b.rearrange("p (dc n) -> p dc n", n=512):
                            E.dma_start(out=d, in_=s),
                            reads=[r_spill["wg" + str(ex)]], writes=[r_wgwu[0]], dma=True)
                    em.emit("p", lambda E, s=wu_c[ex, :, fb * 512:(fb + 1) * 512]
                            .rearrange("(dc p) n -> p dc n", p=P),
                            d=usl_b.rearrange("p (dc n) -> p dc n", n=512):
                            E.dma_start(out=d, in_=s),
                            reads=[r_spill["wu" + str(ex)]], writes=[r_wgwu[1]], dma=True)
                    for fc4 in range(4):
                        fc = fb * 4 + fc4
                        em.barrier_read("t", [r_wgwu[0], r_wgwu[1], r_ln2blk, r_wbc])
                        for which in range(2):
                            wslot = gsl_b if which == 0 else usl_b
                            bps = 0 if which == 0 else 2
                            for dc in range(ND):
                                lhs = wslot[:, dc * 512 + fc4 * P: dc * 512 + (fc4 + 1) * P]
                                for th in range(2):
                                    bank = b[bps + th]
                                    rhs = ln2blk[:, dc * 1024 + th * 512: dc * 1024 + (th + 1) * 512]
                                    dst = pb0[:, (bps + th) * 512:(bps + th + 1) * 512]
                                    if dc == 0:
                                        mm_open(bank)
                                        mm_raw(dst, lhs, rhs, True, False)
                                    elif dc == ND - 1:
                                        mm_close(dst, lhs, rhs, False, bank,
                                                 [r_wgwu[which], r_ln2blk])
                                    else:
                                        mm_raw(dst, lhs, rhs, False, False)
                        em.emit("s", lambda E: E.activation(out=s1b[:, 0:1024], in_=pb0[:, 0:1024],
                                                            func=AF.Silu, bias=0.0, scale=1.0),
                                reads=[b[0], b[1]], writes=[r_s12])
                        em.emit("v", lambda E: E.tensor_tensor(s2b[:, 0:1024], pb0[:, 1024:2048],
                                                               wbc[:, 0:1024], ALU.mult),
                                reads=[b[2], b[3], r_wbc], writes=[r_s12])
                        em.emit("p", lambda E, d=hbuf[:, fc * 1024:(fc + 1) * 1024]:
                                E.tensor_tensor(d, s1b[:, 0:1024], s2b[:, 0:1024], ALU.mult),
                                reads=[r_s12], writes=[r_h])
                for dblk in range(4):
                    dsl = wdsl[dblk % 2]
                    em.emit("s", lambda E, s=wd_c[ex, :, dblk * 512:(dblk + 1) * 512]
                            .rearrange("(fc p) n -> p fc n", p=P),
                            d=dsl.rearrange("p (fc n) -> p fc n", n=512):
                            E.dma_start(out=d, in_=s),
                            reads=[r_spill["wd" + str(ex)]], writes=[r_wd[dblk % 2]], dma=True)
                    em.barrier_read("t", [r_wd[dblk % 2], r_h])
                    for t128 in range(8):
                        avp = pb6 if t128 % 2 == 0 else pb7
                        bank = b[6 + (t128 % 2)]
                        mm_open(bank)
                        for fc in range(16):
                            lhs = hbuf[:, fc * 1024 + t128 * P: fc * 1024 + (t128 + 1) * P]
                            rhs = dsl[:, fc * 512:(fc + 1) * 512]
                            if fc == 15:
                                mm_close(avp[:, 0:512], lhs, rhs, False, bank,
                                         [r_wd[dblk % 2], r_h])
                            else:
                                mm_raw(avp[:, 0:512], lhs, rhs, (fc == 0), False)
                        pslice = partial[:, t128 * 2048 + dblk * 512: t128 * 2048 + (dblk + 1) * 512]
                        if ex == 0:
                            em.emit("v", lambda E, d=pslice, s=avp[:, 0:512]: E.tensor_copy(d, s),
                                    reads=[bank], writes=[r_part])
                        elif ex < NE - 1:
                            em.emit("v", lambda E, d=pslice, s=avp[:, 0:512]:
                                    E.tensor_tensor(d, s, d, ALU.add),
                                    reads=[bank, r_part], writes=[r_part])
                        else:
                            rows = tb * 1024 + t128 * P
                            xsl = x1b[:, (t128 % 2) * 512:((t128 % 2) + 1) * 512]
                            osl = outst[:, (t128 % 2) * 512:((t128 % 2) + 1) * 512]
                            em.emit("y", lambda E, d=xsl,
                                    s=x1_sp[rows:rows + P, dblk * 512:(dblk + 1) * 512]:
                                    E.dma_start(out=d, in_=s),
                                    reads=[r_spill["x1"]], writes=[r_x1b[t128 % 2]], dma=True)
                            em.emit("v", lambda E, d=osl, s=avp[:, 0:512], x=xsl:
                                    E.tensor_tensor(d, s, x, ALU.add),
                                    reads=[bank, r_x1b[t128 % 2]], writes=[r_outst[t128 % 2]])
                            em.emit("v", lambda E, d=osl, p_=pslice:
                                    E.tensor_tensor(d, d, p_, ALU.add),
                                    reads=[r_part], writes=[r_outst[t128 % 2]])
                            em.emit("p", lambda E, s=osl,
                                    d=out_h[rows:rows + P, dblk * 512:(dblk + 1) * 512]:
                                    E.dma_start(out=d, in_=s),
                                    reads=[r_outst[t128 % 2]], writes=[r_spill["out"]], dma=True)

        em.wait("p", "dp", em.cnt["dp"])

        with nc.Block() as block:
            block.sync(lambda E: [f(E) for f in em.th["y"]])
            block.tensor(lambda E: [f(E) for f in em.th["t"]])
            block.vector(lambda E: [f(E) for f in em.th["v"]])
            block.scalar(lambda E: [f(E) for f in em.th["s"]])
            block.gpsimd(lambda E: [f(E) for f in em.th["p"]])

    return nc


_NC_CACHE = {}


def kernel(L0_attn_hidden_states, L0_attn_Wq, L0_attn_Wk, L0_attn_Wv, L0_attn_Wo,
           L0_ffn_Wg, L0_ffn_Wu, L0_ffn_Wd, L0_ffn_Wr, L0_ffn_temperature):
    x = np.ascontiguousarray(np.asarray(L0_attn_hidden_states, dtype=np.float32))
    base = {
        "wq": np.ascontiguousarray(np.asarray(L0_attn_Wq, dtype=np.float32)),
        "wk": np.ascontiguousarray(np.asarray(L0_attn_Wk, dtype=np.float32)),
        "wv": np.ascontiguousarray(np.asarray(L0_attn_Wv, dtype=np.float32)),
        "wo": np.ascontiguousarray(np.asarray(L0_attn_Wo, dtype=np.float32)),
        "wg": np.ascontiguousarray(np.asarray(L0_ffn_Wg, dtype=np.float32)),
        "wu": np.ascontiguousarray(np.asarray(L0_ffn_Wu, dtype=np.float32)),
        "wd": np.ascontiguousarray(np.asarray(L0_ffn_Wd, dtype=np.float32)),
        "wr": np.ascontiguousarray(np.asarray(L0_ffn_Wr, dtype=np.float32)),
        "temp": np.asarray(L0_ffn_temperature, dtype=np.float32).reshape(1, 1),
    }
    base.update(host_consts())
    if "nc" not in _NC_CACHE:
        _NC_CACHE["nc"] = build(debug=False)
    nc = _NC_CACHE["nc"]
    in_maps = [dict(base, x0=x[c]) for c in range(8)]
    res = run_bass_kernel_spmd(nc, in_maps, core_ids=list(range(8)))
    return np.stack([r["out"] for r in res.results], axis=0).astype(np.float32)


# version 13
# speedup vs baseline: 1.3173x; 1.2756x over previous
import math
import numpy as np
import ml_dtypes
from contextlib import ExitStack

import concourse.bass as bass
import concourse.mybir as mybir
from concourse.bass_utils import run_bass_kernel_spmd

F32 = mybir.dt.float32
BF16 = mybir.dt.bfloat16
AF = mybir.ActivationFunctionType
ALU = mybir.AluOpType
AX = mybir.AxisListType

P = 128
S = 2048
DM = 2048
NH = 16
HD = 128
NE = 8
FF = 2048
NT = S // P
ND = DM // P
SOFT_SCALE = float(np.float32(1.0 / math.sqrt(HD)))
EPS = float(np.float32(1e-5))
GOLD_C = float(np.float32(1.0 / np.e))
GOLD_L = float(np.float32(0.5 - np.log(4.0 / 3.0)))
GOLD_U = float(np.float32(0.5))

ENGS = ("t", "v", "s", "p", "y")


class Res:
    __slots__ = ("w", "r")

    def __init__(self):
        self.w = None
        self.r = []


def seed(*olds):
    n = Res()
    toks = []
    for o in olds:
        toks.extend(o.r)
        if o.w is not None:
            toks.append(o.w)
    n.r = toks
    return n


class Em:
    def __init__(self, nc):
        self.nc = nc
        self.th = {e: [] for e in ENGS}
        self.sem = {}
        self.cnt = {}
        self.waited = {e: {} for e in ENGS}

    def mksem(self, ctx, names):
        for n in names:
            self.sem[n] = ctx.enter_context(self.nc.semaphore(n))
            self.cnt[n] = 0

    def wait(self, eng, sname, val):
        if val is None or val <= 0:
            return
        if self.waited[eng].get(sname, 0) >= val:
            return
        self.waited[eng][sname] = val
        h = self.sem[sname]
        self.th[eng].append(lambda E, h=h, v=val: E.wait_ge(h, v))

    def _wait_tok(self, eng, tok):
        self.wait(eng, tok[0], tok[1])

    def barrier_read(self, eng, reads):
        for r in reads:
            if r.w is not None:
                self._wait_tok(eng, r.w)

    def warw(self, eng, res):
        for rr in res.r:
            self._wait_tok(eng, rr)
        if res.w is not None:
            self._wait_tok(eng, res.w)

    def raw(self, eng, fn):
        self.th[eng].append(fn)

    def inc(self, eng, fn, dma=False):
        if dma:
            sname = {"y": "dy", "p": "dp", "s": "ds", "v": "dv"}[eng]
            step = 16
        else:
            sname = eng
            step = 1
        h = self.sem[sname]
        self.cnt[sname] += step
        self.th[eng].append(lambda E, fn=fn, h=h, s=step: fn(E).then_inc(h, s))
        return (sname, self.cnt[sname])

    def emit(self, eng, fn, reads=(), writes=(), dma=False):
        for r in reads:
            if r.w is not None:
                self._wait_tok(eng, r.w)
        for w in writes:
            self.warw(eng, w)
        tok = self.inc(eng, fn, dma=dma)
        for r in reads:
            r.r.append(tok)
        for w in writes:
            w.w = tok
            w.r = []
        return tok


def _np_rope_tables():
    inv_freq = 1.0 / 10000.0 ** (np.arange(0, HD, 2, dtype=np.float32) / HD)
    t = np.arange(S, dtype=np.float32)
    freqs = np.einsum("i,j->ij", t, inv_freq)
    emb = np.concatenate((freqs, freqs), axis=-1)
    cosT = np.ascontiguousarray(np.cos(emb).astype(np.float32).T)
    sinT = np.ascontiguousarray(np.sin(emb).astype(np.float32).T)
    sin_rope = sinT.copy()
    sin_rope[:64, :] *= -1.0
    return cosT, sin_rope


def host_consts():
    ident = np.eye(P, dtype=np.float32).astype(ml_dtypes.bfloat16)
    cosT, sinT = _np_rope_tables()
    mask = np.where(np.arange(P)[:, None] <= np.arange(P)[None, :], 1.0, 0.0)
    maskc = mask.astype(np.float32).astype(ml_dtypes.bfloat16)
    rev = np.broadcast_to(np.arange(NE, 0, -1, dtype=np.float32)[None, :], (P, NE)).copy()
    ones1 = np.ones((1, P), dtype=np.float32)
    ones1b = np.ones((1, P), dtype=np.float32).astype(ml_dtypes.bfloat16)
    return {"c_ident": ident, "c_cos": cosT, "c_sin": sinT, "c_mask": maskc,
            "c_rev": rev, "c_ones1": ones1, "c_ones1b": ones1b}


def build(debug=False):
    nc = bass.Bass("TRN2", target_bir_lowering=False, dynamic_dma_scratch_size=16384)
    dt = nc.dram_tensor
    kind_sp = "ExternalOutput" if debug else "Internal"

    x0_h = dt("x0", [S, DM], F32, kind="ExternalInput").ap()
    wq_h = dt("wq", [DM, DM], F32, kind="ExternalInput").ap()
    wk_h = dt("wk", [DM, DM], F32, kind="ExternalInput").ap()
    wv_h = dt("wv", [DM, DM], F32, kind="ExternalInput").ap()
    wo_h = dt("wo", [DM, DM], F32, kind="ExternalInput").ap()
    wg_h = dt("wg", [NE, DM, FF], F32, kind="ExternalInput").ap()
    wu_h = dt("wu", [NE, DM, FF], F32, kind="ExternalInput").ap()
    wd_h = dt("wd", [NE, FF, DM], F32, kind="ExternalInput").ap()
    wr_h = dt("wr", [DM, NE], F32, kind="ExternalInput").ap()
    temp_h = dt("temp", [1, 1], F32, kind="ExternalInput").ap()
    c_ident = dt("c_ident", [P, P], BF16, kind="ExternalInput").ap()
    c_cos = dt("c_cos", [P, S], F32, kind="ExternalInput").ap()
    c_sin = dt("c_sin", [P, S], F32, kind="ExternalInput").ap()
    c_mask = dt("c_mask", [P, P], BF16, kind="ExternalInput").ap()
    c_rev = dt("c_rev", [P, NE], F32, kind="ExternalInput").ap()
    c_ones1 = dt("c_ones1", [1, P], F32, kind="ExternalInput").ap()
    c_ones1b = dt("c_ones1b", [1, P], BF16, kind="ExternalInput").ap()

    out_h = dt("out", [S, DM], F32, kind="ExternalOutput").ap()

    dbg1 = dt("dbg_ln1bf", [P, 2048], BF16, kind=kind_sp).ap() if debug else None
    dbg2 = dt("dbg_x0f", [P, 2048], F32, kind=kind_sp).ap() if debug else None
    ln1t_sp = dt("ln1t_sp", [NT, P, ND, P], BF16, kind=kind_sp).ap()
    q_sp = dt("q_sp", [NH, P, S], BF16, kind=kind_sp).ap()
    ao_sp = dt("ao_sp", [NT, P, DM], BF16, kind=kind_sp).ap()
    x1_sp = dt("x1_sp", [S, DM], F32, kind=kind_sp).ap()
    ln2t_sp = dt("ln2t_sp", [2, P, ND, 1024], BF16, kind=kind_sp).ap()
    wg_c = dt("wg_c", [NE, DM, FF], BF16, kind="Internal").ap()
    wu_c = dt("wu_c", [NE, DM, FF], BF16, kind="Internal").ap()
    wd_c = dt("wd_c", [NE, FF, DM], BF16, kind="Internal").ap()

    with ExitStack() as ctx:
        en = ctx.enter_context
        A1 = en(nc.sbuf_tensor("A1", [P, 32768], BF16))
        A2 = en(nc.sbuf_tensor("A2", [P, 32768], BF16))
        A3 = en(nc.sbuf_tensor("A3", [P, 32768], BF16))
        wstage = en(nc.sbuf_tensor("wstage", [P, 2048], F32))
        cast_buf = en(nc.sbuf_tensor("cast_buf", [P, 2048], BF16))
        stt = en(nc.sbuf_tensor("stt", [P, 32], F32))
        sttC = en(nc.sbuf_tensor("sttC", [P, 32], F32))
        rt = en(nc.sbuf_tensor("rt", [P, 176], F32))
        w_all = en(nc.sbuf_tensor("w_all", [P, 128], F32))
        w_bfe = en(nc.sbuf_tensor("w_bfe", [P, 128], BF16))
        ident = en(nc.sbuf_tensor("ident", [P, P], BF16))
        maskc = en(nc.sbuf_tensor("maskc", [P, P], BF16))
        rev = en(nc.sbuf_tensor("rev", [P, NE], F32))
        ones1 = en(nc.sbuf_tensor("ones1", [1, P], F32))
        ones1b = en(nc.sbuf_tensor("ones1b", [1, P], BF16))
        wr_hi = en(nc.sbuf_tensor("wr_hi", [P, ND * NE], BF16))
        wr_lo = en(nc.sbuf_tensor("wr_lo", [P, ND * NE], BF16))
        wr_f = en(nc.sbuf_tensor("wr_f", [P, NE], F32))
        temp_s = en(nc.sbuf_tensor("temp_s", [1, 1], F32))
        pb0 = en(nc.psum_tensor("pb0", [P, 2048], F32))
        pbB = en(nc.psum_tensor("pbB", [P, 2048], F32))
        pb4 = pbB[:, 0:512]
        pb5 = pbB[:, 512:1024]
        pb6 = pbB[:, 1024:1536]
        pb7 = pbB[:, 1536:2048]
        tpb4 = pb4.bitcast(BF16)
        tpb5 = pb5.bitcast(BF16)
        wT = A3[0:8, 28672:30720]
        wrow = A3[0:1, 30720:31744]

        em = Em(nc)
        em.mksem(ctx, ["t", "v", "s", "p", "dy", "dp", "ds", "dv"])

        b = [Res() for _ in range(8)]
        r_consts = Res()
        r_tabs = Res()
        r_wr = Res()
        r_temp = Res()
        r_stt = Res()
        r_rt = Res()
        r_xfa = Res()
        r_wstage = [Res(), Res()]
        r_castbuf = [Res(), Res()]
        wslot_ctr = [0]
        r_spill = {k: Res() for k in
                   ("ln1t", "q", "ao", "x1", "ln2t", "out")}
        for nm in ("wg", "wu", "wd"):
            for ex_ in range(NE):
                r_spill[nm + str(ex_)] = Res()

        def mm_open(bank):
            em.warw("t", bank)

        def mm_raw(dst, lhs, rhs, start, stop):
            em.raw("t", lambda E, d=dst, l=lhs, r=rhs, st=start, sp=stop:
                   E.matmul(d, l, r, start=st, stop=sp))

        def mm_close(dst, lhs, rhs, start, bank, readers):
            tok = em.inc("t", lambda E, d=dst, l=lhs, r=rhs, st=start:
                         E.matmul(d, l, r, start=st, stop=True))
            bank.w = tok
            bank.r = []
            for r in readers:
                r.r.append(tok)
            return tok

        def tr_raw(dst, src):
            em.raw("t", lambda E, d=dst, s=src: E.transpose(d, s, ident[:]))

        def tr_close(dst, src, bank, readers):
            tok = em.inc("t", lambda E, d=dst, s=src: E.transpose(d, s, ident[:]))
            bank.w = tok
            bank.r = []
            for r in readers:
                r.r.append(tok)
            return tok

        def prep_chunk(w_rows_ap, dst_fn, dst_res):
            for hf in range(2):
                cb, cres = stage_load_cast(w_rows_ap[:, hf * 1024:(hf + 1) * 1024])
                em.barrier_read("t", [cres, r_consts])
                pb = tpb4 if hf == 0 else tpb5
                bank = b[4 + hf]
                mm_open(bank)
                for j in range(8):
                    s = cb[:, j * P:(j + 1) * P]
                    d = pb[:, j * P:(j + 1) * P]
                    if j < 7:
                        tr_raw(d, s)
                    else:
                        tr_close(d, s, bank, [cres])
                sp = pb[:, :].rearrange("p (c n) -> p c n", n=P)
                em.emit("v", lambda E, s=sp, d=dst_fn(hf): E.tensor_copy(d, s),
                        reads=[bank], writes=[dst_res])

        def transpose16(src_bf, src_res, dst_fn, dst_res, nhalf=2):
            em.barrier_read("t", [src_res, r_consts])
            for half in range(nhalf):
                pb = tpb4 if half % 2 == 0 else tpb5
                bank = b[4 + (half % 2)]
                mm_open(bank)
                for j in range(8):
                    dc = half * 8 + j
                    s = src_bf[:, dc * P:(dc + 1) * P]
                    d = pb[:, j * P:(j + 1) * P]
                    if j < 7:
                        tr_raw(d, s)
                    else:
                        tr_close(d, s, bank, [src_res])
                sp = pb[:, :].rearrange("p (c n) -> p c n", n=P)
                em.emit("v", lambda E, s=sp, d=dst_fn(half): E.tensor_copy(d, s),
                        reads=[bank], writes=[dst_res])

        for dst_ap, src_ap, res in ((ident[:], c_ident, r_consts), (maskc[:], c_mask, r_consts),
                                    (rev[:], c_rev, r_consts), (ones1[:], c_ones1, r_consts),
                                    (ones1b[:], c_ones1b, r_consts), (temp_s[:], temp_h, r_temp)):
            em.emit("y", lambda E, d=dst_ap, s=src_ap: E.dma_start(out=d, in_=s),
                    writes=[res], dma=True)
        cosT = A3[:, 0:4096].bitcast(F32)
        sinT = A3[:, 4096:8192].bitcast(F32)
        em.emit("y", lambda E: E.dma_start(out=cosT, in_=c_cos), writes=[r_tabs], dma=True)
        em.emit("y", lambda E: E.dma_start(out=sinT, in_=c_sin), writes=[r_tabs], dma=True)
        for dc_ in range(ND):
            em.emit("y", lambda E, s=wr_h[dc_ * P:(dc_ + 1) * P, :]: E.dma_start(out=wr_f[:], in_=s),
                    writes=[r_wr], dma=True)
            em.emit("v", lambda E, d=wr_hi[:, dc_ * NE:(dc_ + 1) * NE]: E.tensor_copy(d, wr_f[:]),
                    reads=[r_wr], writes=[r_wr])
            em.emit("v", lambda E, d=wr_lo[:, dc_ * NE:(dc_ + 1) * NE],
                    h=wr_hi[:, dc_ * NE:(dc_ + 1) * NE]:
                    E.tensor_tensor(d, wr_f[:], h, ALU.subtract),
                    reads=[r_wr], writes=[r_wr])
        em.emit("t", lambda E: E.matmul(pb6[:, 0:1], ones1[:], temp_s[:], start=True, stop=True),
                reads=[r_consts, r_temp], writes=[b[6]])
        em.emit("v", lambda E: E.tensor_copy(stt[:, 15:16], pb6[:, 0:1]), reads=[b[6]], writes=[r_stt])
        em.emit("v", lambda E: E.memset(stt[:, 23:24], EPS), writes=[r_stt])
        em.emit("v", lambda E: E.memset(stt[:, 30:31], -20.0), writes=[r_stt])
        em.emit("v", lambda E: E.reciprocal(stt[:, 9:10], stt[:, 15:16]), reads=[r_stt], writes=[r_stt])

        cache_jobs = []
        for ex in range(NE):
            for mat_h, mat_c, name in ((wg_h, wg_c, "wg"), (wu_h, wu_c, "wu"), (wd_h, wd_c, "wd")):
                for rc in range(ND):
                    for hf in range(2):
                        cache_jobs.append((mat_h[ex, rc * P:(rc + 1) * P, hf * 1024:(hf + 1) * 1024],
                                           mat_c[ex, rc * P:(rc + 1) * P, hf * 1024:(hf + 1) * 1024],
                                           name + str(ex)))
        cache_pos = [0]

        def stage_load_cast(src_ap):
            sl = wslot_ctr[0] % 2
            wslot_ctr[0] += 1
            wv = wstage[:, sl * 1024:(sl + 1) * 1024]
            cb = cast_buf[:, sl * 1024:(sl + 1) * 1024]
            qe = "y" if sl == 0 else "p"
            em.emit(qe, lambda E, d=wv, s=src_ap: E.dma_start(out=d, in_=s),
                    writes=[r_wstage[sl]], dma=True)
            if sl == 0:
                em.emit("s", lambda E, d=cb, s=wv: E.activation(out=d, in_=s, func=AF.Copy,
                                                                bias=0.0, scale=1.0),
                        reads=[r_wstage[sl]], writes=[r_castbuf[sl]])
            else:
                em.emit("v", lambda E, d=cb, s=wv: E.tensor_copy(d, s),
                        reads=[r_wstage[sl]], writes=[r_castbuf[sl]])
            return cb, r_castbuf[sl]

        def emit_cache_chunks(k):
            for _ in range(k):
                if cache_pos[0] >= len(cache_jobs):
                    return
                src, dst, name = cache_jobs[cache_pos[0]]
                cache_pos[0] += 1
                cb, cres = stage_load_cast(src)
                em.emit("s", lambda E, d=dst, s=cb: E.dma_start(out=d, in_=s),
                        reads=[cres], writes=[r_spill[name]], dma=True)

        def emit_ln(src_ap, src_res, dst_ap, dst_res, dump_ap, dump_writes, lo=None):
            em.emit("v", lambda E: E.reduce_sum(stt[:, 0:1], src_ap, axis=AX.X),
                    reads=[src_res], writes=[r_stt])
            em.emit("s", lambda E: E.mul(stt[:, 1:2], stt[:, 0:1], -1.0 / DM),
                    reads=[r_stt], writes=[r_stt])
            em.emit("s", lambda E: E.activation(out=dump_ap, in_=src_ap, func=AF.Square,
                                                bias=stt[:, 1:2], scale=1.0,
                                                accum_out=stt[:, 2:3]),
                    reads=[src_res, r_stt], writes=[r_stt] + list(dump_writes))
            em.emit("s", lambda E: E.activation(out=stt[:, 3:4], in_=stt[:, 2:3], func=AF.Ln,
                                                bias=stt[:, 23:24], scale=float(np.float32(1.0 / DM))),
                    reads=[r_stt], writes=[r_stt])
            em.emit("s", lambda E: E.activation(out=stt[:, 4:5], in_=stt[:, 3:4], func=AF.Exp,
                                                bias=0.0, scale=-0.5),
                    reads=[r_stt], writes=[r_stt])
            if lo is None:
                em.emit("v", lambda E: E.tensor_scalar(out=dst_ap, in0=src_ap,
                                                       scalar1=stt[:, 1:2], scalar2=stt[:, 4:5],
                                                       op0=ALU.add, op1=ALU.mult),
                        reads=[src_res, r_stt], writes=[dst_res])
            else:
                lnf_ap, lnf_res, lo_ap, lo_res = lo
                em.emit("v", lambda E: E.tensor_scalar(out=lnf_ap, in0=src_ap,
                                                       scalar1=stt[:, 1:2], scalar2=stt[:, 4:5],
                                                       op0=ALU.add, op1=ALU.mult),
                        reads=[src_res, r_stt], writes=[lnf_res])
                em.emit("s", lambda E: E.activation(out=dst_ap, in_=lnf_ap, func=AF.Copy,
                                                    bias=0.0, scale=1.0),
                        reads=[lnf_res], writes=[dst_res])
                em.emit("v", lambda E: E.tensor_tensor(lo_ap, lnf_ap, dst_ap, ALU.subtract),
                        reads=[lnf_res, dst_res], writes=[lo_res])

        r_ln1T = Res()
        r_A2s = [Res(), Res()]
        ln1bf2 = [A2[:, 0:2048], A2[:, 2048:4096]]
        xf_p1 = [A2[:, 4096:8192].bitcast(F32), A2[:, 8192:12288].bitcast(F32)]
        r_xf_p1 = [Res(), Res()]
        A1v = A1[:, :].rearrange("p (dc n) -> p dc n", n=2048)
        for tt in range(NT):
            xsl = tt % 2
            em.emit("y", lambda E, d=xf_p1[xsl][:], s=x0_h[tt * P:(tt + 1) * P, :]: E.dma_start(out=d, in_=s),
                    writes=[r_xf_p1[xsl]], dma=True)
            emit_ln(xf_p1[xsl][:], r_xf_p1[xsl], ln1bf2[xsl], r_A2s[xsl], pb0[:, 0:2048],
                    [b[0], b[1], b[2], b[3]])
            transpose16(ln1bf2[xsl], r_A2s[xsl],
                        lambda hf, tt=tt: A1v[:, hf * 8:hf * 8 + 8, tt * P:(tt + 1) * P],
                        r_ln1T)
            em.emit("p", lambda E, s=A1v[:, :, tt * P:(tt + 1) * P], d=ln1t_sp[tt]:
                    E.dma_start(out=d, in_=s),
                    reads=[r_ln1T], writes=[r_spill["ln1t"]], dma=True)
            emit_cache_chunks(12)

        r_t12 = Res()
        r_wqt = seed(r_A2s[0], r_A2s[1])
        r_wkt = Res()
        r_K = seed(r_wqt, r_xf_p1[0], r_xf_p1[1])
        t12f = A3[:, 8192:12288].bitcast(F32)
        t1 = t12f[:, 0:512]
        t2 = t12f[:, 512:1024]
        qbf2 = [A3[:, 12288:12800], A3[:, 12800:13312]]
        r_qbf = [Res(), Res()]
        wkt_st = A3[:, 13312:15360]

        def emit_proj(w_src_h, mode):
            for oc in range(NH):
                if mode == "q":
                    wt_res, wt_buf = r_wqt, A2[:, 0:2048]
                else:
                    wt_res, wt_buf = r_wkt, wkt_st
                wtv = wt_buf.rearrange("p (c n) -> p c n", n=P)
                prep_chunk(w_src_h[oc * P:(oc + 1) * P, :],
                           lambda hf, wtv=wtv: wtv[:, hf * 8:hf * 8 + 8, :], wt_res)
                em.barrier_read("t", [wt_res, r_ln1T])
                for tb in range(4):
                    bank = b[tb]
                    for dc in range(ND):
                        lhs = wt_buf[:, dc * P:(dc + 1) * P]
                        rhs = A1[:, dc * 2048 + tb * 512: dc * 2048 + (tb + 1) * 512]
                        dst = pb0[:, tb * 512:(tb + 1) * 512]
                        if dc == 0:
                            mm_open(bank)
                            mm_raw(dst, lhs, rhs, True, False)
                        elif dc == ND - 1:
                            mm_close(dst, lhs, rhs, False, bank, [wt_res, r_ln1T])
                        else:
                            mm_raw(dst, lhs, rhs, False, False)
                    ps = pb0[:, tb * 512:(tb + 1) * 512]
                    ps_lo = pb0[64:128, tb * 512:(tb + 1) * 512]
                    ps_hi = pb0[0:64, tb * 512:(tb + 1) * 512]
                    cs = cosT[:, tb * 512:(tb + 1) * 512]
                    sn = sinT[:, tb * 512:(tb + 1) * 512]
                    em.emit("v", lambda E, p_=ps, c_=cs: E.tensor_tensor(t1[:, :], p_, c_, ALU.mult),
                            reads=[bank, r_tabs], writes=[r_t12])
                    em.emit("v", lambda E, p_=ps_lo, s_=sn[0:64, :]:
                            E.tensor_tensor(t2[0:64, :], p_, s_, ALU.mult),
                            reads=[bank, r_tabs], writes=[r_t12])
                    em.emit("v", lambda E, p_=ps_hi, s_=sn[64:128, :]:
                            E.tensor_tensor(t2[64:128, :], p_, s_, ALU.mult),
                            reads=[bank, r_tabs], writes=[r_t12])
                    if mode == "q":
                        qsl = (oc * 4 + tb) % 2
                        em.emit("v", lambda E, d=qbf2[qsl]: E.tensor_tensor(d, t1[:, :], t2[:, :], ALU.add),
                                reads=[r_t12], writes=[r_qbf[qsl]])
                        em.emit("p", lambda E, d=q_sp[oc, :, tb * 512:(tb + 1) * 512], s=qbf2[qsl]:
                                E.dma_start(out=d, in_=s),
                                reads=[r_qbf[qsl]], writes=[r_spill["q"]], dma=True)
                    else:
                        dstk = A2[:, oc * 2048 + tb * 512: oc * 2048 + (tb + 1) * 512]
                        em.emit("v", lambda E, d=dstk: E.tensor_tensor(d, t1[:, :], t2[:, :], ALU.add),
                                reads=[r_t12], writes=[r_K])
                emit_cache_chunks(8)

        emit_proj(wq_h, "q")
        emit_proj(wk_h, "k")

        VW = NH * 129
        r_wvt = [seed(r_ln1T), seed(r_ln1T)]
        r_slice = [seed(r_ln1T), seed(r_ln1T)]
        r_V = seed(r_tabs, r_t12, r_wkt, r_qbf[0], r_qbf[1], r_ln1T)

        def vtile(tt):
            if tt < 15:
                return A3[:, tt * VW:(tt + 1) * VW]
            return A1[:, 20480:20480 + VW]
        for ob in range(4):
            slot = ob % 2
            wv_slot = A1[:, slot * 8192:(slot + 1) * 8192]
            wvv = wv_slot.rearrange("p (dc n) -> p dc n", n=512)
            for j in range(4):
                oc = ob * 4 + j
                prep_chunk(wv_h[oc * P:(oc + 1) * P, :],
                           lambda hf, wvv=wvv, j=j: wvv[:, hf * 8:hf * 8 + 8, j * P:(j + 1) * P],
                           r_wvt[slot])
            for tt in range(NT):
                sl = tt % 2
                sbuf = A1[:, 16384 + sl * 2048: 16384 + (sl + 1) * 2048]
                em.emit("y", lambda E, s=ln1t_sp[tt], d=sbuf.rearrange("p (c n) -> p c n", n=P):
                        E.dma_start(out=d, in_=s),
                        reads=[r_spill["ln1t"]], writes=[r_slice[sl]], dma=True)
                bank = b[tt % 4]
                em.barrier_read("t", [r_slice[sl], r_wvt[slot]])
                dst = pb0[:, (tt % 4) * 512:((tt % 4) + 1) * 512]
                for dc in range(ND):
                    lhs = sbuf[:, dc * P:(dc + 1) * P]
                    rhs = wv_slot[:, dc * 512:(dc + 1) * 512]
                    if dc == 0:
                        mm_open(bank)
                        mm_raw(dst, lhs, rhs, True, False)
                    elif dc == ND - 1:
                        mm_close(dst, lhs, rhs, False, bank, [r_slice[sl], r_wvt[slot]])
                    else:
                        mm_raw(dst, lhs, rhs, False, False)
                vdst = vtile(tt).rearrange("p (h n) -> p h n", n=129)[:, ob * 4:(ob + 1) * 4, 0:128]
                em.emit("s", lambda E, d=vdst, s=dst.rearrange("p (h n) -> p h n", n=P):
                        E.activation(out=d, in_=s, func=AF.Copy, bias=0.0, scale=1.0),
                        reads=[bank], writes=[r_V])
                if tt % 4 == 3:
                    emit_cache_chunks(2)
        for tt in range(NT):
            em.emit("v", lambda E, d=vtile(tt).rearrange("p (h n) -> p h n", n=129)[:, :, 128:129]:
                    E.memset(d, 1.0), writes=[r_V])

        r_qh = [seed(r_wvt[0], r_wvt[1]), seed(r_wvt[0], r_wvt[1])]
        r_expc = [seed(r_wvt[0], r_wvt[1]) for _ in range(3)]
        r_stage = [seed(r_wvt[0], r_wvt[1]), seed(r_wvt[0], r_wvt[1])]
        r_sC = Res()
        qh_buf = [A1[:, 0:2048], A1[:, 2048:4096]]
        expc = [A1[:, 4096:4608], A1[:, 4608:5120], A1[:, 5120:5632]]
        stg = [A1[:, 5632:6144], A1[:, 6144:6656]]
        ec_ctr = [0]

        def qh_load(hh):
            sl = hh % 2
            em.emit("y", lambda E, d=qh_buf[sl][:], s=q_sp[hh]: E.dma_start(out=d, in_=s),
                    reads=[r_spill["q"]], writes=[r_qh[sl]], dma=True)

        def score_exp(hh, Q, kk):
            sl = hh % 2
            ep = ec_ctr[0] % 3
            ep2 = ec_ctr[0] % 2
            ec_ctr[0] += 1
            sb = b[6 + ep2]
            spp = pb6 if ep2 == 0 else pb7
            em.barrier_read("t", [r_K, r_qh[sl]])
            em.warw("t", sb)
            tok = em.inc("t", lambda E, d=spp[:, 0:512],
                         l=A2[:, hh * 2048 + kk * P: hh * 2048 + (kk + 1) * P],
                         r=qh_buf[sl][:, Q * 512:(Q + 1) * 512]:
                         E.matmul(d, l, r, start=True, stop=True))
            sb.w = tok
            sb.r = []
            r_K.r.append(tok)
            r_qh[sl].r.append(tok)
            em.emit("s", lambda E, s=spp[:, 0:512], d=expc[ep][:]:
                    E.activation(out=d, in_=s, func=AF.Exp, bias=stt[:, 30:31],
                                 scale=SOFT_SCALE),
                    reads=[sb, r_stt], writes=[r_expc[ep]])
            if kk >= 4 * Q:
                jd = kk - 4 * Q
                em.emit("v", lambda E, d=expc[ep][:, jd * P:(jd + 1) * P]:
                        E.tensor_tensor(d, d, maskc[:], ALU.mult),
                        reads=[r_consts], writes=[r_expc[ep]])
            return ep

        def avs(hh, Q, kk, ep):
            sg = Q % 2
            jls = list(range(max(kk - 4 * Q, 0), 4))
            em.barrier_read("t", [r_expc[ep], r_V])
            for idx, jl in enumerate(jls):
                qt = 4 * Q + jl
                bank = b[jl]
                dst = pb0[:, jl * 512: jl * 512 + 129]
                lhs = expc[ep][:, jl * P:(jl + 1) * P]
                rhs = vtile(kk)[:, hh * 129:(hh + 1) * 129]
                if kk == 0:
                    em.warw("t", bank)
                if kk == qt or idx == len(jls) - 1:
                    tok = em.inc("t", lambda E, d=dst, l=lhs, r=rhs,
                                 s1=(kk == 0), s2=(kk == qt):
                                 E.matmul(d, l, r, start=s1, stop=s2))
                    r_expc[ep].r.append(tok)
                    r_V.r.append(tok)
                    if kk == qt:
                        bank.w = tok
                        bank.r = []
                        sc = (Q % 2) * 4 + jl
                        em.emit("v", lambda E, d=sttC[:, sc:sc + 1],
                                s=pb0[:, jl * 512 + 128: jl * 512 + 129]:
                                E.reciprocal(d, s),
                                reads=[bank], writes=[r_sC])
                        em.emit("v", lambda E, d=stg[sg][:, jl * P:(jl + 1) * P],
                                s=pb0[:, jl * 512: jl * 512 + 128],
                                r_=sttC[:, sc:sc + 1]:
                                E.tensor_scalar(out=d, in0=s, scalar1=r_,
                                                scalar2=None, op0=ALU.mult),
                                reads=[bank, r_sC], writes=[r_stage[sg]])
                else:
                    mm_raw(dst, lhs, rhs, (kk == 0), False)

        def stage_dma(hh, Q):
            sg = Q % 2
            em.emit("p", lambda E, s=stg[sg].rearrange("p (t n) -> p t n", n=P),
                    d=ao_sp[4 * Q:4 * Q + 4, :, hh * P:(hh + 1) * P].rearrange("t p n -> p t n"):
                    E.dma_start(out=d, in_=s),
                    reads=[r_stage[sg]], writes=[r_spill["ao"]], dma=True)

        qh_load(0)
        qh_load(1)
        for hh in range(NH):
            chunks = [(Q, kk) for Q in range(4) for kk in range(4 * Q + 4)]
            eps = {0: score_exp(hh, *chunks[0]), 1: score_exp(hh, *chunks[1])}
            for i, (Q, kk) in enumerate(chunks):
                if i + 2 < len(chunks):
                    eps[i + 2] = score_exp(hh, *chunks[i + 2])
                avs(hh, Q, kk, eps[i])
                if kk == 4 * Q + 3:
                    stage_dma(hh, Q)
            if hh + 2 < NH:
                qh_load(hh + 2)
            emit_cache_chunks(12)

        r_WoT = seed(r_qh[0], r_qh[1], r_expc[0], r_expc[1], r_stage[0], r_stage[1],
                     r_slice[0], r_slice[1], r_V)
        r_ao = [seed(r_K), seed(r_K)]
        r_aot = seed(r_K)
        r_ln2hi = seed(r_K)
        r_ln2lo = seed(r_K)
        r_hiT = seed(r_K)
        r_loT = seed(r_K)
        r_lnf = seed(r_V)
        r_sqd = seed(r_V)
        r_wall = Res()
        r_wT = seed(r_V)

        A1w = A1[:, :].rearrange("p (hc n) -> p hc n", n=2048)
        for oc in range(NH):
            prep_chunk(wo_h[oc * P:(oc + 1) * P, :],
                       lambda hf, oc=oc: A1w[:, hf * 8:hf * 8 + 8, oc * P:(oc + 1) * P],
                       r_WoT)

        aosl = [A2[:, 0:2048], A2[:, 2048:4096]]
        aotb = A2[:, 4096:6144]
        ln2hi = A2[:, 6144:8192]
        ln2lo = A2[:, 8192:10240]
        hiT = A2[:, 10240:12288]
        loT = A2[:, 12288:14336]
        lnf = A3[:, 0:4096].bitcast(F32)
        sqd = A3[:, 8192:12288].bitcast(F32)[:, 0:2048]
        xfa2 = [A2[:, 14336:18432].bitcast(F32), A2[:, 18432:22528].bitcast(F32)]
        r_xfa2 = [seed(r_K), seed(r_K)]
        aotv = aotb.rearrange("p (c n) -> p c n", n=P)
        hiTv = hiT.rearrange("p (c n) -> p c n", n=P)
        loTv = loT.rearrange("p (c n) -> p c n", n=P)

        def c8(i):
            return rt[:, i * NE:(i + 1) * NE]

        (RZ, REX, RINH, RD0, RDIST, REX1, RFB, RZONE, RWT, RSELR, RFB2,
         RFBM, RFBW, RG1, RG2, RW0, RDIFF, RDSC, RINV, RSEL2R, RT1, RT2) = range(22)

        def c2_stageA(tt):
            sl = tt % 2
            xf = xfa2[sl]
            rxf = r_xfa2[sl]
            em.emit("y", lambda E, s=ao_sp[tt], d=aosl[sl]: E.dma_start(out=d, in_=s),
                    reads=[r_spill["ao"]], writes=[r_ao[sl]], dma=True)
            em.emit("y", lambda E, d=xf[:], s=x0_h[tt * P:(tt + 1) * P, :]: E.dma_start(out=d, in_=s),
                    writes=[rxf], dma=True)
            transpose16(aosl[sl], r_ao[sl],
                        lambda hf: aotv[:, hf * 8:hf * 8 + 8, :], r_aot)
            em.barrier_read("t", [r_aot, r_WoT])
            for hc in range(NH):
                for ob in range(4):
                    bank = b[ob]
                    lhs = aotb[:, hc * P:(hc + 1) * P]
                    rhs = A1[:, hc * 2048 + ob * 512: hc * 2048 + (ob + 1) * 512]
                    dst = pb0[:, ob * 512:(ob + 1) * 512]
                    if hc == 0:
                        mm_open(bank)
                        mm_raw(dst, lhs, rhs, True, False)
                    elif hc == NH - 1:
                        mm_close(dst, lhs, rhs, False, bank, [r_aot, r_WoT])
                    else:
                        mm_raw(dst, lhs, rhs, False, False)
            for ob in range(4):
                em.emit("v", lambda E, s=pb0[:, ob * 512:(ob + 1) * 512],
                        x=xf[:, ob * 512:(ob + 1) * 512]:
                        E.scalar_tensor_tensor(out=x, in0=s, scalar=1.0, in1=x,
                                               op0=ALU.mult, op1=ALU.add),
                        reads=[b[ob]], writes=[rxf])
            em.emit("p", lambda E, d=x1_sp[tt * P:(tt + 1) * P, :], s=xf[:]: E.dma_start(out=d, in_=s),
                    reads=[rxf], writes=[r_spill["x1"]], dma=True)

        def c2_stageB(tt):
            sl = tt % 2
            xf = xfa2[sl]
            rxf = r_xfa2[sl]
            emit_ln(xf[:], rxf, ln2hi, r_ln2hi, sqd, [r_sqd],
                    lo=(lnf[:, 0:2048], r_lnf, ln2lo, r_ln2lo))
            transpose16(ln2hi, r_ln2hi, lambda hf: hiTv[:, hf * 8:hf * 8 + 8, :], r_hiT)
            transpose16(ln2lo, r_ln2lo, lambda hf: loTv[:, hf * 8:hf * 8 + 8, :], r_loT)
            tb2, off = tt // 8, (tt % 8) * P
            em.emit("p", lambda E, d=ln2t_sp[tb2, :, :, off:off + P], s=hiTv:
                    E.dma_start(out=d, in_=s),
                    reads=[r_hiT], writes=[r_spill["ln2t"]], dma=True)
            em.barrier_read("t", [r_hiT, r_loT, r_wr])
            mm_open(b[6])
            k = 0
            nmm = ND * 3
            for dc in range(ND):
                for (lb, rb) in ((hiT, wr_hi), (loT, wr_hi), (hiT, wr_lo)):
                    lhs = lb[:, dc * P:(dc + 1) * P]
                    rhs = rb[:, dc * NE:(dc + 1) * NE]
                    if k == nmm - 1:
                        mm_close(pb6[:, 0:NE], lhs, rhs, False, b[6], [r_hiT, r_loT, r_wr])
                    else:
                        mm_raw(pb6[:, 0:NE], lhs, rhs, (k == 0), False)
                    k += 1
            em.emit("v", lambda E: E.tensor_scalar(out=c8(RZ), in0=pb6[:, 0:NE],
                                                   scalar1=stt[:, 9:10], scalar2=None, op0=ALU.mult),
                    reads=[b[6], r_stt], writes=[r_rt])
            em.emit("s", lambda E: E.activation(out=c8(REX), in_=c8(RZ), func=AF.Exp,
                                                bias=0.0, scale=-1.0),
                    reads=[r_rt], writes=[r_rt])
            em.emit("v", lambda E: E.tensor_scalar(out=c8(RT1), in0=c8(REX), scalar1=1.0,
                                                   scalar2=None, op0=ALU.add),
                    reads=[r_rt], writes=[r_rt])
            em.emit("v", lambda E: E.reciprocal(c8(RINH), c8(RT1)), reads=[r_rt], writes=[r_rt])
            em.emit("v", lambda E: E.tensor_scalar(out=c8(RD0), in0=c8(RINH), scalar1=GOLD_C,
                                                   scalar2=None, op0=ALU.subtract),
                    reads=[r_rt], writes=[r_rt])
            em.emit("s", lambda E: E.activation(out=c8(RDIST), in_=c8(RD0), func=AF.Abs,
                                                bias=0.0, scale=1.0),
                    reads=[r_rt], writes=[r_rt])
            em.emit("s", lambda E: E.activation(out=c8(REX1), in_=c8(RDIST), func=AF.Exp,
                                                bias=0.0, scale=-10.0),
                    reads=[r_rt], writes=[r_rt])
            em.emit("s", lambda E: E.activation(out=c8(RFB), in_=c8(RDIST), func=AF.Exp,
                                                bias=0.0, scale=float(np.float32(-1.0 / 0.3))),
                    reads=[r_rt], writes=[r_rt])
            em.emit("v", lambda E: E.tensor_single_scalar(c8(RG1), c8(RINH), GOLD_L, ALU.is_ge),
                    reads=[r_rt], writes=[r_rt])
            em.emit("v", lambda E: E.tensor_single_scalar(c8(RG2), c8(RINH), GOLD_U, ALU.is_le),
                    reads=[r_rt], writes=[r_rt])
            em.emit("v", lambda E: E.tensor_tensor(c8(RZONE), c8(RG1), c8(RG2), ALU.mult),
                    reads=[r_rt], writes=[r_rt])
            em.emit("v", lambda E: E.tensor_tensor(c8(RWT), c8(REX1), c8(RZONE), ALU.mult),
                    reads=[r_rt], writes=[r_rt])
            em.emit("v", lambda E: E.reduce_sum(stt[:, 10:11], c8(RWT), axis=AX.X),
                    reads=[r_rt], writes=[r_stt])
            em.emit("v", lambda E: E.tensor_single_scalar(stt[:, 11:12], stt[:, 10:11],
                                                          1e-8, ALU.is_lt),
                    reads=[r_stt], writes=[r_stt])
            em.emit("v", lambda E: E.reduce_max(stt[:, 12:13], c8(RFB), axis=AX.X),
                    reads=[r_rt], writes=[r_stt])
            em.emit("v", lambda E: E.tensor_scalar(out=c8(RT1), in0=c8(RFB), scalar1=stt[:, 12:13],
                                                   scalar2=None, op0=ALU.is_ge),
                    reads=[r_rt, r_stt], writes=[r_rt])
            em.emit("v", lambda E: E.tensor_tensor(c8(RSELR), c8(RT1), rev[:], ALU.mult),
                    reads=[r_rt, r_consts], writes=[r_rt])
            em.emit("v", lambda E: E.reduce_max(stt[:, 13:14], c8(RSELR), axis=AX.X),
                    reads=[r_rt], writes=[r_stt])
            em.emit("v", lambda E: E.tensor_scalar(out=c8(RT1), in0=c8(RSELR), scalar1=stt[:, 13:14],
                                                   scalar2=None, op0=ALU.is_ge),
                    reads=[r_rt, r_stt], writes=[r_rt])
            em.emit("v", lambda E: E.tensor_scalar(out=c8(RINV), in0=c8(RT1), scalar1=-1.0,
                                                   scalar2=1.0, op0=ALU.mult, op1=ALU.add),
                    reads=[r_rt], writes=[r_rt])
            em.emit("v", lambda E: E.tensor_tensor(c8(RFB2), c8(RFB), c8(RINV), ALU.mult),
                    reads=[r_rt], writes=[r_rt])
            em.emit("v", lambda E: E.reduce_max(stt[:, 14:15], c8(RFB2), axis=AX.X),
                    reads=[r_rt], writes=[r_stt])
            em.emit("v", lambda E: E.tensor_scalar(out=c8(RT2), in0=c8(RFB2), scalar1=stt[:, 14:15],
                                                   scalar2=None, op0=ALU.is_ge),
                    reads=[r_rt, r_stt], writes=[r_rt])
            em.emit("v", lambda E: E.tensor_tensor(c8(RSEL2R), c8(RT2), rev[:], ALU.mult),
                    reads=[r_rt, r_consts], writes=[r_rt])
            em.emit("v", lambda E: E.reduce_max(stt[:, 16:17], c8(RSEL2R), axis=AX.X),
                    reads=[r_rt], writes=[r_stt])
            em.emit("v", lambda E: E.tensor_scalar(out=c8(RT2), in0=c8(RSEL2R), scalar1=stt[:, 16:17],
                                                   scalar2=None, op0=ALU.is_ge),
                    reads=[r_rt, r_stt], writes=[r_rt])
            em.emit("v", lambda E: E.tensor_tensor(c8(RFBM), c8(RT1), c8(RT2), ALU.add),
                    reads=[r_rt], writes=[r_rt])
            em.emit("v", lambda E: E.tensor_tensor(c8(RFBW), c8(RFB), c8(RFBM), ALU.mult),
                    reads=[r_rt], writes=[r_rt])
            em.emit("v", lambda E: E.reduce_sum(stt[:, 17:18], c8(RFBW), axis=AX.X),
                    reads=[r_rt], writes=[r_stt])
            em.emit("v", lambda E: E.tensor_single_scalar(stt[:, 18:19], stt[:, 17:18],
                                                          1e-8, ALU.max),
                    reads=[r_stt], writes=[r_stt])
            em.emit("v", lambda E: E.reciprocal(stt[:, 19:20], stt[:, 18:19]),
                    reads=[r_stt], writes=[r_stt])
            em.emit("v", lambda E: E.tensor_scalar(out=c8(RFBW), in0=c8(RFBW), scalar1=stt[:, 19:20],
                                                   scalar2=None, op0=ALU.mult),
                    reads=[r_stt], writes=[r_rt])
            em.emit("v", lambda E: E.tensor_tensor(c8(RDIFF), c8(RFBW), c8(RWT), ALU.subtract),
                    reads=[r_rt], writes=[r_rt])
            em.emit("v", lambda E: E.tensor_scalar(out=c8(RDSC), in0=c8(RDIFF), scalar1=stt[:, 11:12],
                                                   scalar2=None, op0=ALU.mult),
                    reads=[r_rt, r_stt], writes=[r_rt])
            em.emit("v", lambda E: E.tensor_tensor(c8(RW0), c8(RWT), c8(RDSC), ALU.add),
                    reads=[r_rt], writes=[r_rt])
            em.emit("v", lambda E: E.reduce_sum(stt[:, 20:21], c8(RW0), axis=AX.X),
                    reads=[r_rt], writes=[r_stt])
            em.emit("v", lambda E: E.tensor_single_scalar(stt[:, 21:22], stt[:, 20:21],
                                                          1e-8, ALU.max),
                    reads=[r_stt], writes=[r_stt])
            em.emit("v", lambda E: E.reciprocal(stt[:, 22:23], stt[:, 21:22]),
                    reads=[r_stt], writes=[r_stt])
            em.emit("v", lambda E, d=w_all[:, tt * NE:(tt + 1) * NE]:
                    E.tensor_scalar(out=d, in0=c8(RW0), scalar1=stt[:, 22:23],
                                    scalar2=None, op0=ALU.mult),
                    reads=[r_rt, r_stt], writes=[r_wall])
            emit_cache_chunks(4)

        c2_stageA(0)
        for tt in range(NT):
            if tt + 1 < NT:
                c2_stageA(tt + 1)
            c2_stageB(tt)

        em.emit("v", lambda E: E.tensor_copy(w_bfe[:], w_all[:]), reads=[r_wall], writes=[r_wall])
        for tt in range(NT):
            em.barrier_read("t", [r_wall, r_consts])
            em.warw("t", b[4])
            tok = em.inc("t", lambda E, s=w_bfe[:, tt * NE:(tt + 1) * NE]:
                         E.transpose(tpb4[0:NE, 0:P], s, ident[:]))
            b[4].w = tok
            b[4].r = []
            r_wall.r.append(tok)
            em.emit("v", lambda E, d=wT[:, tt * P:(tt + 1) * P]: E.tensor_copy(d, tpb4[0:NE, 0:P]),
                    reads=[b[4]], writes=[r_wT])

        r_wgwu = [seed(r_WoT), seed(r_WoT)]
        r_wd = [seed(r_WoT), seed(r_WoT)]
        r_ln2blk = seed(r_ao[0], r_ao[1], r_aot, r_ln2hi, r_ln2lo, r_hiT, r_loT, r_xfa2[0], r_xfa2[1])
        r_h = seed(r_ln2blk)
        r_part = seed(r_lnf, r_sqd, r_V)
        r_wbc = seed(r_lnf, r_sqd, r_V)
        r_x1b = [seed(r_lnf, r_sqd, r_V), seed(r_lnf, r_sqd, r_V)]
        r_outst = [seed(r_lnf, r_sqd, r_V), seed(r_lnf, r_sqd, r_V)]
        r_s12 = [seed(r_lnf, r_sqd, r_V), seed(r_lnf, r_sqd, r_V)]
        r_wrow = seed(r_V)

        ln2blk = A2[:, 0:16384]
        hbuf = A2[:, 16384:32768]
        gsl_b = A1[:, 0:8192]
        usl_b = A1[:, 8192:16384]
        wdsl = [A1[:, 16384:24576], A1[:, 24576:32768]]
        partial = A3[:, 0:16384]
        wbc = A3[:, 16384:20480].bitcast(F32)[:, 0:1024]
        x1b = A3[:, 20480:22528].bitcast(F32)
        outst = A3[:, 22528:24576].bitcast(F32)
        s1b = [A3[:, 24576:25600], A3[:, 25600:26624]]
        s2b = [A3[:, 26624:27648], A3[:, 27648:28672]]

        emit_cache_chunks(len(cache_jobs))

        for tb in range(2):
            em.emit("y", lambda E, s=ln2t_sp[tb], d=ln2blk.rearrange("p (c n) -> p c n", n=1024):
                    E.dma_start(out=d, in_=s),
                    reads=[r_spill["ln2t"]], writes=[r_ln2blk], dma=True)
            for ex in range(NE):
                em.emit("y", lambda E, s=wT[ex:ex + 1, tb * 1024:(tb + 1) * 1024]:
                        E.dma_start(out=wrow, in_=s),
                        reads=[r_wT], writes=[r_wrow], dma=True)
                for half in range(2):
                    pbx = pb6 if half == 0 else pb7
                    bankx = b[6 + half]
                    em.emit("t", lambda E, d=pbx[:, 0:512], r=wrow[:, half * 512:(half + 1) * 512]:
                            E.matmul(d, ones1b[:], r, start=True, stop=True),
                            reads=[r_wrow, r_consts], writes=[bankx])
                    em.emit("v", lambda E, d=wbc[:, half * 512:(half + 1) * 512], s=pbx[:, 0:512]:
                            E.tensor_copy(d, s),
                            reads=[bankx], writes=[r_wbc])
                for fb in range(4):
                    em.emit("y", lambda E, s=wg_c[ex, :, fb * 512:(fb + 1) * 512]
                            .rearrange("(dc p) n -> p dc n", p=P),
                            d=gsl_b.rearrange("p (dc n) -> p dc n", n=512):
                            E.dma_start(out=d, in_=s),
                            reads=[r_spill["wg" + str(ex)]], writes=[r_wgwu[0]], dma=True)
                    em.emit("p", lambda E, s=wu_c[ex, :, fb * 512:(fb + 1) * 512]
                            .rearrange("(dc p) n -> p dc n", p=P),
                            d=usl_b.rearrange("p (dc n) -> p dc n", n=512):
                            E.dma_start(out=d, in_=s),
                            reads=[r_spill["wu" + str(ex)]], writes=[r_wgwu[1]], dma=True)
                    for fc4 in range(4):
                        fc = fb * 4 + fc4
                        gb = (fc % 2) * 4
                        pset = pb0 if gb == 0 else pbB
                        sp12 = fc % 2
                        em.barrier_read("t", [r_wgwu[0], r_wgwu[1], r_ln2blk, r_wbc])
                        for which in range(2):
                            wslot = gsl_b if which == 0 else usl_b
                            bps = 0 if which == 0 else 2
                            for dc in range(ND):
                                lhs = wslot[:, dc * 512 + fc4 * P: dc * 512 + (fc4 + 1) * P]
                                for th in range(2):
                                    bank = b[gb + bps + th]
                                    rhs = ln2blk[:, dc * 1024 + th * 512: dc * 1024 + (th + 1) * 512]
                                    dst = pset[:, (bps + th) * 512:(bps + th + 1) * 512]
                                    if dc == 0:
                                        mm_open(bank)
                                        mm_raw(dst, lhs, rhs, True, False)
                                    elif dc == ND - 1:
                                        mm_close(dst, lhs, rhs, False, bank,
                                                 [r_wgwu[which], r_ln2blk])
                                    else:
                                        mm_raw(dst, lhs, rhs, False, False)
                        em.emit("s", lambda E, o=s1b[sp12], i=pset[:, 0:1024]:
                                E.activation(out=o, in_=i, func=AF.Silu, bias=0.0, scale=1.0),
                                reads=[b[gb], b[gb + 1]], writes=[r_s12[sp12]])
                        em.emit("v", lambda E, o=s2b[sp12], i=pset[:, 1024:2048]:
                                E.tensor_tensor(o, i, wbc[:, 0:1024], ALU.mult),
                                reads=[b[gb + 2], b[gb + 3], r_wbc], writes=[r_s12[sp12]])
                        em.emit("p", lambda E, d=hbuf[:, fc * 1024:(fc + 1) * 1024],
                                a=s1b[sp12], c=s2b[sp12]:
                                E.tensor_tensor(d, a, c, ALU.mult),
                                reads=[r_s12[sp12]], writes=[r_h])
                for dblk in range(4):
                    dsl = wdsl[dblk % 2]
                    em.emit("s", lambda E, s=wd_c[ex, :, dblk * 512:(dblk + 1) * 512]
                            .rearrange("(fc p) n -> p fc n", p=P),
                            d=dsl.rearrange("p (fc n) -> p fc n", n=512):
                            E.dma_start(out=d, in_=s),
                            reads=[r_spill["wd" + str(ex)]], writes=[r_wd[dblk % 2]], dma=True)
                    em.barrier_read("t", [r_wd[dblk % 2], r_h])
                    for t128 in range(8):
                        avp = pb6 if t128 % 2 == 0 else pb7
                        bank = b[6 + (t128 % 2)]
                        mm_open(bank)
                        for fc in range(16):
                            lhs = hbuf[:, fc * 1024 + t128 * P: fc * 1024 + (t128 + 1) * P]
                            rhs = dsl[:, fc * 512:(fc + 1) * 512]
                            if fc == 15:
                                mm_close(avp[:, 0:512], lhs, rhs, False, bank,
                                         [r_wd[dblk % 2], r_h])
                            else:
                                mm_raw(avp[:, 0:512], lhs, rhs, (fc == 0), False)
                        pslice = partial[:, t128 * 2048 + dblk * 512: t128 * 2048 + (dblk + 1) * 512]
                        if ex == 0:
                            em.emit("v", lambda E, d=pslice, s=avp[:, 0:512]: E.tensor_copy(d, s),
                                    reads=[bank], writes=[r_part])
                        elif ex < NE - 1:
                            em.emit("v", lambda E, d=pslice, s=avp[:, 0:512]:
                                    E.tensor_tensor(d, s, d, ALU.add),
                                    reads=[bank, r_part], writes=[r_part])
                        else:
                            rows = tb * 1024 + t128 * P
                            xsl = x1b[:, (t128 % 2) * 512:((t128 % 2) + 1) * 512]
                            osl = outst[:, (t128 % 2) * 512:((t128 % 2) + 1) * 512]
                            em.emit("y", lambda E, d=xsl,
                                    s=x1_sp[rows:rows + P, dblk * 512:(dblk + 1) * 512]:
                                    E.dma_start(out=d, in_=s),
                                    reads=[r_spill["x1"]], writes=[r_x1b[t128 % 2]], dma=True)
                            em.emit("v", lambda E, d=osl, s=avp[:, 0:512], x=xsl:
                                    E.tensor_tensor(d, s, x, ALU.add),
                                    reads=[bank, r_x1b[t128 % 2]], writes=[r_outst[t128 % 2]])
                            em.emit("v", lambda E, d=osl, p_=pslice:
                                    E.tensor_tensor(d, d, p_, ALU.add),
                                    reads=[r_part], writes=[r_outst[t128 % 2]])
                            em.emit("p", lambda E, s=osl,
                                    d=out_h[rows:rows + P, dblk * 512:(dblk + 1) * 512]:
                                    E.dma_start(out=d, in_=s),
                                    reads=[r_outst[t128 % 2]], writes=[r_spill["out"]], dma=True)

        em.wait("p", "dp", em.cnt["dp"])

        with nc.Block() as block:
            block.sync(lambda E: [f(E) for f in em.th["y"]])
            block.tensor(lambda E: [f(E) for f in em.th["t"]])
            block.vector(lambda E: [f(E) for f in em.th["v"]])
            block.scalar(lambda E: [f(E) for f in em.th["s"]])
            block.gpsimd(lambda E: [f(E) for f in em.th["p"]])

    return nc


_NC_CACHE = {}


def kernel(L0_attn_hidden_states, L0_attn_Wq, L0_attn_Wk, L0_attn_Wv, L0_attn_Wo,
           L0_ffn_Wg, L0_ffn_Wu, L0_ffn_Wd, L0_ffn_Wr, L0_ffn_temperature):
    x = np.ascontiguousarray(np.asarray(L0_attn_hidden_states, dtype=np.float32))
    base = {
        "wq": np.ascontiguousarray(np.asarray(L0_attn_Wq, dtype=np.float32)),
        "wk": np.ascontiguousarray(np.asarray(L0_attn_Wk, dtype=np.float32)),
        "wv": np.ascontiguousarray(np.asarray(L0_attn_Wv, dtype=np.float32)),
        "wo": np.ascontiguousarray(np.asarray(L0_attn_Wo, dtype=np.float32)),
        "wg": np.ascontiguousarray(np.asarray(L0_ffn_Wg, dtype=np.float32)),
        "wu": np.ascontiguousarray(np.asarray(L0_ffn_Wu, dtype=np.float32)),
        "wd": np.ascontiguousarray(np.asarray(L0_ffn_Wd, dtype=np.float32)),
        "wr": np.ascontiguousarray(np.asarray(L0_ffn_Wr, dtype=np.float32)),
        "temp": np.asarray(L0_ffn_temperature, dtype=np.float32).reshape(1, 1),
    }
    base.update(host_consts())
    if "nc" not in _NC_CACHE:
        _NC_CACHE["nc"] = build(debug=False)
    nc = _NC_CACHE["nc"]
    in_maps = [dict(base, x0=x[c]) for c in range(8)]
    res = run_bass_kernel_spmd(nc, in_maps, core_ids=list(range(8)))
    return np.stack([r["out"] for r in res.results], axis=0).astype(np.float32)


# version 14
# speedup vs baseline: 1.4569x; 1.1060x over previous
import math
import numpy as np
import ml_dtypes
from contextlib import ExitStack

import concourse.bass as bass
import concourse.mybir as mybir
from concourse.bass_utils import run_bass_kernel_spmd

F32 = mybir.dt.float32
BF16 = mybir.dt.bfloat16
AF = mybir.ActivationFunctionType
ALU = mybir.AluOpType
AX = mybir.AxisListType

P = 128
S = 2048
DM = 2048
NH = 16
HD = 128
NE = 8
FF = 2048
NT = S // P
ND = DM // P
SOFT_SCALE = float(np.float32(1.0 / math.sqrt(HD)))
EPS = float(np.float32(1e-5))
GOLD_C = float(np.float32(1.0 / np.e))
GOLD_L = float(np.float32(0.5 - np.log(4.0 / 3.0)))
GOLD_U = float(np.float32(0.5))

ENGS = ("t", "v", "s", "p", "y")


class Res:
    __slots__ = ("w", "r")

    def __init__(self):
        self.w = None
        self.r = []


def seed(*olds):
    n = Res()
    toks = []
    for o in olds:
        toks.extend(o.r)
        if o.w is not None:
            toks.append(o.w)
    n.r = toks
    return n


class Em:
    def __init__(self, nc):
        self.nc = nc
        self.th = {e: [] for e in ENGS}
        self.sem = {}
        self.cnt = {}
        self.waited = {e: {} for e in ENGS}

    def mksem(self, ctx, names):
        for n in names:
            self.sem[n] = ctx.enter_context(self.nc.semaphore(n))
            self.cnt[n] = 0

    def wait(self, eng, sname, val):
        if val is None or val <= 0:
            return
        if self.waited[eng].get(sname, 0) >= val:
            return
        self.waited[eng][sname] = val
        h = self.sem[sname]
        self.th[eng].append(lambda E, h=h, v=val: E.wait_ge(h, v))

    def _wait_tok(self, eng, tok):
        self.wait(eng, tok[0], tok[1])

    def barrier_read(self, eng, reads):
        for r in reads:
            if r.w is not None:
                self._wait_tok(eng, r.w)

    def warw(self, eng, res):
        for rr in res.r:
            self._wait_tok(eng, rr)
        if res.w is not None:
            self._wait_tok(eng, res.w)

    def raw(self, eng, fn):
        self.th[eng].append(fn)

    def inc(self, eng, fn, dma=False):
        if dma:
            sname = {"y": "dy", "p": "dp", "s": "ds", "v": "dv"}[eng]
            step = 16
        else:
            sname = eng
            step = 1
        h = self.sem[sname]
        self.cnt[sname] += step
        self.th[eng].append(lambda E, fn=fn, h=h, s=step: fn(E).then_inc(h, s))
        return (sname, self.cnt[sname])

    def emit(self, eng, fn, reads=(), writes=(), dma=False):
        for r in reads:
            if r.w is not None:
                self._wait_tok(eng, r.w)
        for w in writes:
            self.warw(eng, w)
        tok = self.inc(eng, fn, dma=dma)
        for r in reads:
            r.r.append(tok)
        for w in writes:
            w.w = tok
            w.r = []
        return tok


def _np_rope_tables():
    inv_freq = 1.0 / 10000.0 ** (np.arange(0, HD, 2, dtype=np.float32) / HD)
    t = np.arange(S, dtype=np.float32)
    freqs = np.einsum("i,j->ij", t, inv_freq)
    emb = np.concatenate((freqs, freqs), axis=-1)
    cosT = np.ascontiguousarray(np.cos(emb).astype(np.float32).T)
    sinT = np.ascontiguousarray(np.sin(emb).astype(np.float32).T)
    sin_rope = sinT.copy()
    sin_rope[:64, :] *= -1.0
    return cosT, sin_rope


def host_consts():
    ident = np.eye(P, dtype=np.float32).astype(ml_dtypes.bfloat16)
    cosT, sinT = _np_rope_tables()
    mask = np.where(np.arange(P)[:, None] <= np.arange(P)[None, :], 1.0, 0.0)
    maskc = mask.astype(np.float32).astype(ml_dtypes.bfloat16)
    rev = np.broadcast_to(np.arange(NE, 0, -1, dtype=np.float32)[None, :], (P, NE)).copy()
    ones1 = np.ones((1, P), dtype=np.float32)
    ones1b = np.ones((1, P), dtype=np.float32).astype(ml_dtypes.bfloat16)
    return {"c_ident": ident, "c_cos": cosT, "c_sin": sinT, "c_mask": maskc,
            "c_rev": rev, "c_ones1": ones1, "c_ones1b": ones1b}


def build(debug=False):
    nc = bass.Bass("TRN2", target_bir_lowering=False, dynamic_dma_scratch_size=16384)
    dt = nc.dram_tensor
    kind_sp = "ExternalOutput" if debug else "Internal"

    x0_h = dt("x0", [S, DM], F32, kind="ExternalInput").ap()
    wq_h = dt("wq", [DM, DM], F32, kind="ExternalInput").ap()
    wk_h = dt("wk", [DM, DM], F32, kind="ExternalInput").ap()
    wv_h = dt("wv", [DM, DM], F32, kind="ExternalInput").ap()
    wo_h = dt("wo", [DM, DM], F32, kind="ExternalInput").ap()
    wg_h = dt("wg", [NE, DM, FF], F32, kind="ExternalInput").ap()
    wu_h = dt("wu", [NE, DM, FF], F32, kind="ExternalInput").ap()
    wd_h = dt("wd", [NE, FF, DM], F32, kind="ExternalInput").ap()
    wr_h = dt("wr", [DM, NE], F32, kind="ExternalInput").ap()
    temp_h = dt("temp", [1, 1], F32, kind="ExternalInput").ap()
    c_ident = dt("c_ident", [P, P], BF16, kind="ExternalInput").ap()
    c_cos = dt("c_cos", [P, S], F32, kind="ExternalInput").ap()
    c_sin = dt("c_sin", [P, S], F32, kind="ExternalInput").ap()
    c_mask = dt("c_mask", [P, P], BF16, kind="ExternalInput").ap()
    c_rev = dt("c_rev", [P, NE], F32, kind="ExternalInput").ap()
    c_ones1 = dt("c_ones1", [1, P], F32, kind="ExternalInput").ap()
    c_ones1b = dt("c_ones1b", [1, P], BF16, kind="ExternalInput").ap()

    out_h = dt("out", [S, DM], F32, kind="ExternalOutput").ap()

    dbg1 = dt("dbg_ln1bf", [P, 2048], BF16, kind=kind_sp).ap() if debug else None
    dbg2 = dt("dbg_x0f", [P, 2048], F32, kind=kind_sp).ap() if debug else None
    ln1t_sp = dt("ln1t_sp", [NT, P, ND, P], BF16, kind=kind_sp).ap()
    q_sp = dt("q_sp", [NH, P, S], BF16, kind=kind_sp).ap()
    ao_sp = dt("ao_sp", [NT, P, DM], BF16, kind=kind_sp).ap()
    x1_sp = dt("x1_sp", [S, DM], F32, kind=kind_sp).ap()
    ln2t_sp = dt("ln2t_sp", [2, P, ND, 1024], BF16, kind=kind_sp).ap()
    wg_c = dt("wg_c", [NE, DM, FF], BF16, kind="Internal").ap()
    wu_c = dt("wu_c", [NE, DM, FF], BF16, kind="Internal").ap()
    wd_c = dt("wd_c", [NE, FF, DM], BF16, kind="Internal").ap()

    with ExitStack() as ctx:
        en = ctx.enter_context
        A1 = en(nc.sbuf_tensor("A1", [P, 32768], BF16))
        A2 = en(nc.sbuf_tensor("A2", [P, 32768], BF16))
        A3 = en(nc.sbuf_tensor("A3", [P, 32768], BF16))
        wstage = en(nc.sbuf_tensor("wstage", [P, 2048], F32))
        cast_buf = en(nc.sbuf_tensor("cast_buf", [P, 2048], BF16))
        stt = en(nc.sbuf_tensor("stt", [P, 32], F32))
        sttC = en(nc.sbuf_tensor("sttC", [P, 32], F32))
        rt = en(nc.sbuf_tensor("rt", [P, 176], F32))
        w_all = en(nc.sbuf_tensor("w_all", [P, 128], F32))
        w_bfe = en(nc.sbuf_tensor("w_bfe", [P, 128], BF16))
        ident = en(nc.sbuf_tensor("ident", [P, P], BF16))
        maskc = en(nc.sbuf_tensor("maskc", [P, P], BF16))
        rev = en(nc.sbuf_tensor("rev", [P, NE], F32))
        ones1 = en(nc.sbuf_tensor("ones1", [1, P], F32))
        ones1b = en(nc.sbuf_tensor("ones1b", [1, P], BF16))
        wr_hi = en(nc.sbuf_tensor("wr_hi", [P, ND * NE], BF16))
        wr_lo = en(nc.sbuf_tensor("wr_lo", [P, ND * NE], BF16))
        wr_f = en(nc.sbuf_tensor("wr_f", [P, NE], F32))
        temp_s = en(nc.sbuf_tensor("temp_s", [1, 1], F32))
        pb0 = en(nc.psum_tensor("pb0", [P, 2048], F32))
        pbB = en(nc.psum_tensor("pbB", [P, 2048], F32))
        pb4 = pbB[:, 0:512]
        pb5 = pbB[:, 512:1024]
        pb6 = pbB[:, 1024:1536]
        pb7 = pbB[:, 1536:2048]
        tpb4 = pb4.bitcast(BF16)
        tpb5 = pb5.bitcast(BF16)
        wT = A3[0:8, 28672:30720]
        wrow = A3[0:1, 30720:31744]

        em = Em(nc)
        em.mksem(ctx, ["t", "v", "s", "p", "dy", "dp", "ds", "dv"])

        b = [Res() for _ in range(8)]
        r_consts = Res()
        r_tabs = Res()
        r_wr = Res()
        r_temp = Res()
        r_stt = Res()
        r_rt = Res()
        r_xfa = Res()
        r_wstage = [Res(), Res()]
        r_castbuf = [Res(), Res()]
        wslot_ctr = [0]
        r_spill = {k: Res() for k in
                   ("ln1t", "q", "ao", "x1", "ln2t", "out")}
        for nm in ("wg", "wu", "wd"):
            for ex_ in range(NE):
                r_spill[nm + str(ex_)] = Res()

        def mm_open(bank):
            em.warw("t", bank)

        def mm_raw(dst, lhs, rhs, start, stop):
            em.raw("t", lambda E, d=dst, l=lhs, r=rhs, st=start, sp=stop:
                   E.matmul(d, l, r, start=st, stop=sp))

        def mm_close(dst, lhs, rhs, start, bank, readers):
            tok = em.inc("t", lambda E, d=dst, l=lhs, r=rhs, st=start:
                         E.matmul(d, l, r, start=st, stop=True))
            bank.w = tok
            bank.r = []
            for r in readers:
                r.r.append(tok)
            return tok

        def tr_raw(dst, src):
            em.raw("t", lambda E, d=dst, s=src: E.transpose(d, s, ident[:]))

        def tr_close(dst, src, bank, readers):
            tok = em.inc("t", lambda E, d=dst, s=src: E.transpose(d, s, ident[:]))
            bank.w = tok
            bank.r = []
            for r in readers:
                r.r.append(tok)
            return tok

        def prep_chunk(w_rows_ap, dst_fn, dst_res):
            for hf in range(2):
                cb, cres = stage_load_cast(w_rows_ap[:, hf * 1024:(hf + 1) * 1024])
                em.barrier_read("t", [cres, r_consts])
                pb = tpb4 if hf == 0 else tpb5
                bank = b[4 + hf]
                mm_open(bank)
                for j in range(8):
                    s = cb[:, j * P:(j + 1) * P]
                    d = pb[:, j * P:(j + 1) * P]
                    if j < 7:
                        tr_raw(d, s)
                    else:
                        tr_close(d, s, bank, [cres])
                sp = pb[:, :].rearrange("p (c n) -> p c n", n=P)
                em.emit("v", lambda E, s=sp, d=dst_fn(hf): E.tensor_copy(d, s),
                        reads=[bank], writes=[dst_res])

        def transpose16(src_bf, src_res, dst_fn, dst_res, nhalf=2):
            em.barrier_read("t", [src_res, r_consts])
            for half in range(nhalf):
                pb = tpb4 if half % 2 == 0 else tpb5
                bank = b[4 + (half % 2)]
                mm_open(bank)
                for j in range(8):
                    dc = half * 8 + j
                    s = src_bf[:, dc * P:(dc + 1) * P]
                    d = pb[:, j * P:(j + 1) * P]
                    if j < 7:
                        tr_raw(d, s)
                    else:
                        tr_close(d, s, bank, [src_res])
                sp = pb[:, :].rearrange("p (c n) -> p c n", n=P)
                em.emit("v", lambda E, s=sp, d=dst_fn(half): E.tensor_copy(d, s),
                        reads=[bank], writes=[dst_res])

        for dst_ap, src_ap, res in ((ident[:], c_ident, r_consts), (maskc[:], c_mask, r_consts),
                                    (rev[:], c_rev, r_consts), (ones1[:], c_ones1, r_consts),
                                    (ones1b[:], c_ones1b, r_consts), (temp_s[:], temp_h, r_temp)):
            em.emit("y", lambda E, d=dst_ap, s=src_ap: E.dma_start(out=d, in_=s),
                    writes=[res], dma=True)
        cosT = A3[:, 0:4096].bitcast(F32)
        sinT = A3[:, 4096:8192].bitcast(F32)
        em.emit("y", lambda E: E.dma_start(out=cosT, in_=c_cos), writes=[r_tabs], dma=True)
        em.emit("y", lambda E: E.dma_start(out=sinT, in_=c_sin), writes=[r_tabs], dma=True)
        for dc_ in range(ND):
            em.emit("y", lambda E, s=wr_h[dc_ * P:(dc_ + 1) * P, :]: E.dma_start(out=wr_f[:], in_=s),
                    writes=[r_wr], dma=True)
            em.emit("v", lambda E, d=wr_hi[:, dc_ * NE:(dc_ + 1) * NE]: E.tensor_copy(d, wr_f[:]),
                    reads=[r_wr], writes=[r_wr])
            em.emit("v", lambda E, d=wr_lo[:, dc_ * NE:(dc_ + 1) * NE],
                    h=wr_hi[:, dc_ * NE:(dc_ + 1) * NE]:
                    E.tensor_tensor(d, wr_f[:], h, ALU.subtract),
                    reads=[r_wr], writes=[r_wr])
        em.emit("t", lambda E: E.matmul(pb6[:, 0:1], ones1[:], temp_s[:], start=True, stop=True),
                reads=[r_consts, r_temp], writes=[b[6]])
        em.emit("v", lambda E: E.tensor_copy(stt[:, 15:16], pb6[:, 0:1]), reads=[b[6]], writes=[r_stt])
        em.emit("v", lambda E: E.memset(stt[:, 23:24], EPS), writes=[r_stt])
        em.emit("v", lambda E: E.memset(stt[:, 30:31], -20.0), writes=[r_stt])
        em.emit("v", lambda E: E.reciprocal(stt[:, 9:10], stt[:, 15:16]), reads=[r_stt], writes=[r_stt])

        cache_jobs = []
        for ex in range(NE):
            for mat_h, mat_c, name in ((wg_h, wg_c, "wg"), (wu_h, wu_c, "wu"), (wd_h, wd_c, "wd")):
                for rc in range(ND):
                    for hf in range(2):
                        cache_jobs.append((mat_h[ex, rc * P:(rc + 1) * P, hf * 1024:(hf + 1) * 1024],
                                           mat_c[ex, rc * P:(rc + 1) * P, hf * 1024:(hf + 1) * 1024],
                                           name + str(ex)))
        cache_pos = [0]

        def stage_load_cast(src_ap):
            sl = wslot_ctr[0] % 2
            wslot_ctr[0] += 1
            wv = wstage[:, sl * 1024:(sl + 1) * 1024]
            cb = cast_buf[:, sl * 1024:(sl + 1) * 1024]
            qe = "y" if sl == 0 else "p"
            em.emit(qe, lambda E, d=wv, s=src_ap: E.dma_start(out=d, in_=s),
                    writes=[r_wstage[sl]], dma=True)
            if sl == 0:
                em.emit("s", lambda E, d=cb, s=wv: E.activation(out=d, in_=s, func=AF.Copy,
                                                                bias=0.0, scale=1.0),
                        reads=[r_wstage[sl]], writes=[r_castbuf[sl]])
            else:
                em.emit("v", lambda E, d=cb, s=wv: E.tensor_copy(d, s),
                        reads=[r_wstage[sl]], writes=[r_castbuf[sl]])
            return cb, r_castbuf[sl]

        def emit_cache_chunks(k):
            for _ in range(k):
                if cache_pos[0] >= len(cache_jobs):
                    return
                src, dst, name = cache_jobs[cache_pos[0]]
                cache_pos[0] += 1
                cb, cres = stage_load_cast(src)
                em.emit("s", lambda E, d=dst, s=cb: E.dma_start(out=d, in_=s),
                        reads=[cres], writes=[r_spill[name]], dma=True)

        def emit_ln(src_ap, src_res, dst_ap, dst_res, dump_ap, dump_writes, lo=None):
            em.emit("v", lambda E: E.reduce_sum(stt[:, 0:1], src_ap, axis=AX.X),
                    reads=[src_res], writes=[r_stt])
            em.emit("s", lambda E: E.mul(stt[:, 1:2], stt[:, 0:1], -1.0 / DM),
                    reads=[r_stt], writes=[r_stt])
            em.emit("s", lambda E: E.activation(out=dump_ap, in_=src_ap, func=AF.Square,
                                                bias=stt[:, 1:2], scale=1.0,
                                                accum_out=stt[:, 2:3]),
                    reads=[src_res, r_stt], writes=[r_stt] + list(dump_writes))
            em.emit("s", lambda E: E.activation(out=stt[:, 3:4], in_=stt[:, 2:3], func=AF.Ln,
                                                bias=stt[:, 23:24], scale=float(np.float32(1.0 / DM))),
                    reads=[r_stt], writes=[r_stt])
            em.emit("s", lambda E: E.activation(out=stt[:, 4:5], in_=stt[:, 3:4], func=AF.Exp,
                                                bias=0.0, scale=-0.5),
                    reads=[r_stt], writes=[r_stt])
            if lo is None:
                em.emit("v", lambda E: E.tensor_scalar(out=dst_ap, in0=src_ap,
                                                       scalar1=stt[:, 1:2], scalar2=stt[:, 4:5],
                                                       op0=ALU.add, op1=ALU.mult),
                        reads=[src_res, r_stt], writes=[dst_res])
            else:
                lnf_ap, lnf_res, lo_ap, lo_res = lo
                em.emit("v", lambda E: E.tensor_scalar(out=lnf_ap, in0=src_ap,
                                                       scalar1=stt[:, 1:2], scalar2=stt[:, 4:5],
                                                       op0=ALU.add, op1=ALU.mult),
                        reads=[src_res, r_stt], writes=[lnf_res])
                em.emit("s", lambda E: E.activation(out=dst_ap, in_=lnf_ap, func=AF.Copy,
                                                    bias=0.0, scale=1.0),
                        reads=[lnf_res], writes=[dst_res])
                em.emit("v", lambda E: E.tensor_tensor(lo_ap, lnf_ap, dst_ap, ALU.subtract),
                        reads=[lnf_res, dst_res], writes=[lo_res])

        r_ln1T = Res()
        r_A2s = [Res(), Res()]
        ln1bf2 = [A2[:, 0:2048], A2[:, 2048:4096]]
        xf_p1 = [A2[:, 4096:8192].bitcast(F32), A2[:, 8192:12288].bitcast(F32)]
        r_xf_p1 = [Res(), Res()]
        A1v = A1[:, :].rearrange("p (dc n) -> p dc n", n=2048)
        for tt in range(NT):
            xsl = tt % 2
            em.emit("y", lambda E, d=xf_p1[xsl][:], s=x0_h[tt * P:(tt + 1) * P, :]: E.dma_start(out=d, in_=s),
                    writes=[r_xf_p1[xsl]], dma=True)
            emit_ln(xf_p1[xsl][:], r_xf_p1[xsl], ln1bf2[xsl], r_A2s[xsl], pb0[:, 0:2048],
                    [b[0], b[1], b[2], b[3]])
            transpose16(ln1bf2[xsl], r_A2s[xsl],
                        lambda hf, tt=tt: A1v[:, hf * 8:hf * 8 + 8, tt * P:(tt + 1) * P],
                        r_ln1T)
            em.emit("p", lambda E, s=A1v[:, :, tt * P:(tt + 1) * P], d=ln1t_sp[tt]:
                    E.dma_start(out=d, in_=s),
                    reads=[r_ln1T], writes=[r_spill["ln1t"]], dma=True)
            emit_cache_chunks(12)

        r_t12 = Res()
        r_wqt = seed(r_A2s[0], r_A2s[1])
        r_wkt = Res()
        r_K = seed(r_wqt, r_xf_p1[0], r_xf_p1[1])
        t12f = A3[:, 8192:12288].bitcast(F32)
        t1 = t12f[:, 0:512]
        t2 = t12f[:, 512:1024]
        qbf2 = [A3[:, 12288:12800], A3[:, 12800:13312]]
        r_qbf = [Res(), Res()]
        wkt_st = A3[:, 13312:15360]

        def emit_proj(w_src_h, mode):
            for oc in range(NH):
                if mode == "q":
                    wt_res, wt_buf = r_wqt, A2[:, 0:2048]
                else:
                    wt_res, wt_buf = r_wkt, wkt_st
                wtv = wt_buf.rearrange("p (c n) -> p c n", n=P)
                prep_chunk(w_src_h[oc * P:(oc + 1) * P, :],
                           lambda hf, wtv=wtv: wtv[:, hf * 8:hf * 8 + 8, :], wt_res)
                em.barrier_read("t", [wt_res, r_ln1T])
                for tb in range(4):
                    bank = b[tb]
                    for dc in range(ND):
                        lhs = wt_buf[:, dc * P:(dc + 1) * P]
                        rhs = A1[:, dc * 2048 + tb * 512: dc * 2048 + (tb + 1) * 512]
                        dst = pb0[:, tb * 512:(tb + 1) * 512]
                        if dc == 0:
                            mm_open(bank)
                            mm_raw(dst, lhs, rhs, True, False)
                        elif dc == ND - 1:
                            mm_close(dst, lhs, rhs, False, bank, [wt_res, r_ln1T])
                        else:
                            mm_raw(dst, lhs, rhs, False, False)
                    ps = pb0[:, tb * 512:(tb + 1) * 512]
                    ps_lo = pb0[64:128, tb * 512:(tb + 1) * 512]
                    ps_hi = pb0[0:64, tb * 512:(tb + 1) * 512]
                    cs = cosT[:, tb * 512:(tb + 1) * 512]
                    sn = sinT[:, tb * 512:(tb + 1) * 512]
                    em.emit("v", lambda E, p_=ps, c_=cs: E.tensor_tensor(t1[:, :], p_, c_, ALU.mult),
                            reads=[bank, r_tabs], writes=[r_t12])
                    em.emit("v", lambda E, p_=ps_lo, s_=sn[0:64, :]:
                            E.tensor_tensor(t2[0:64, :], p_, s_, ALU.mult),
                            reads=[bank, r_tabs], writes=[r_t12])
                    em.emit("v", lambda E, p_=ps_hi, s_=sn[64:128, :]:
                            E.tensor_tensor(t2[64:128, :], p_, s_, ALU.mult),
                            reads=[bank, r_tabs], writes=[r_t12])
                    if mode == "q":
                        qsl = (oc * 4 + tb) % 2
                        em.emit("v", lambda E, d=qbf2[qsl]: E.tensor_tensor(d, t1[:, :], t2[:, :], ALU.add),
                                reads=[r_t12], writes=[r_qbf[qsl]])
                        em.emit("p", lambda E, d=q_sp[oc, :, tb * 512:(tb + 1) * 512], s=qbf2[qsl]:
                                E.dma_start(out=d, in_=s),
                                reads=[r_qbf[qsl]], writes=[r_spill["q"]], dma=True)
                    else:
                        dstk = A2[:, oc * 2048 + tb * 512: oc * 2048 + (tb + 1) * 512]
                        em.emit("v", lambda E, d=dstk: E.tensor_tensor(d, t1[:, :], t2[:, :], ALU.add),
                                reads=[r_t12], writes=[r_K])
                emit_cache_chunks(8)

        emit_proj(wq_h, "q")
        emit_proj(wk_h, "k")

        VW = NH * 129
        r_wvt = [seed(r_ln1T), seed(r_ln1T)]
        r_slice = [seed(r_ln1T), seed(r_ln1T)]
        r_V = seed(r_tabs, r_t12, r_wkt, r_qbf[0], r_qbf[1], r_ln1T)

        def vtile(tt):
            if tt < 15:
                return A3[:, tt * VW:(tt + 1) * VW]
            return A1[:, 20480:20480 + VW]
        for ob in range(4):
            slot = ob % 2
            wv_slot = A1[:, slot * 8192:(slot + 1) * 8192]
            wvv = wv_slot.rearrange("p (dc n) -> p dc n", n=512)
            for j in range(4):
                oc = ob * 4 + j
                prep_chunk(wv_h[oc * P:(oc + 1) * P, :],
                           lambda hf, wvv=wvv, j=j: wvv[:, hf * 8:hf * 8 + 8, j * P:(j + 1) * P],
                           r_wvt[slot])
            for tt in range(NT):
                sl = tt % 2
                sbuf = A1[:, 16384 + sl * 2048: 16384 + (sl + 1) * 2048]
                em.emit("y", lambda E, s=ln1t_sp[tt], d=sbuf.rearrange("p (c n) -> p c n", n=P):
                        E.dma_start(out=d, in_=s),
                        reads=[r_spill["ln1t"]], writes=[r_slice[sl]], dma=True)
                bank = b[tt % 4]
                em.barrier_read("t", [r_slice[sl], r_wvt[slot]])
                dst = pb0[:, (tt % 4) * 512:((tt % 4) + 1) * 512]
                for dc in range(ND):
                    lhs = sbuf[:, dc * P:(dc + 1) * P]
                    rhs = wv_slot[:, dc * 512:(dc + 1) * 512]
                    if dc == 0:
                        mm_open(bank)
                        mm_raw(dst, lhs, rhs, True, False)
                    elif dc == ND - 1:
                        mm_close(dst, lhs, rhs, False, bank, [r_slice[sl], r_wvt[slot]])
                    else:
                        mm_raw(dst, lhs, rhs, False, False)
                vdst = vtile(tt).rearrange("p (h n) -> p h n", n=129)[:, ob * 4:(ob + 1) * 4, 0:128]
                em.emit("s", lambda E, d=vdst, s=dst.rearrange("p (h n) -> p h n", n=P):
                        E.activation(out=d, in_=s, func=AF.Copy, bias=0.0, scale=1.0),
                        reads=[bank], writes=[r_V])
                if tt % 4 == 3:
                    emit_cache_chunks(2)
        for tt in range(NT):
            em.emit("v", lambda E, d=vtile(tt).rearrange("p (h n) -> p h n", n=129)[:, :, 128:129]:
                    E.memset(d, 1.0), writes=[r_V])

        r_qh = [seed(r_wvt[0], r_wvt[1]), seed(r_wvt[0], r_wvt[1])]
        r_expc = [seed(r_wvt[0], r_wvt[1]) for _ in range(3)]
        r_stage = [seed(r_wvt[0], r_wvt[1]), seed(r_wvt[0], r_wvt[1])]
        r_sC = Res()
        qh_buf = [A1[:, 0:2048], A1[:, 2048:4096]]
        expc = [A1[:, 4096:4608], A1[:, 4608:5120], A1[:, 5120:5632]]
        stg = [A1[:, 5632:6144], A1[:, 6144:6656]]
        ec_ctr = [0]

        def qh_load(hh):
            sl = hh % 2
            em.emit("y", lambda E, d=qh_buf[sl][:], s=q_sp[hh]: E.dma_start(out=d, in_=s),
                    reads=[r_spill["q"]], writes=[r_qh[sl]], dma=True)

        def score_exp(hh, Q, kk):
            sl = hh % 2
            ep = ec_ctr[0] % 3
            ep2 = ec_ctr[0] % 2
            ec_ctr[0] += 1
            sb = b[6 + ep2]
            spp = pb6 if ep2 == 0 else pb7
            em.barrier_read("t", [r_K, r_qh[sl]])
            em.warw("t", sb)
            tok = em.inc("t", lambda E, d=spp[:, 0:512],
                         l=A2[:, hh * 2048 + kk * P: hh * 2048 + (kk + 1) * P],
                         r=qh_buf[sl][:, Q * 512:(Q + 1) * 512]:
                         E.matmul(d, l, r, start=True, stop=True))
            sb.w = tok
            sb.r = []
            r_K.r.append(tok)
            r_qh[sl].r.append(tok)
            em.emit("s", lambda E, s=spp[:, 0:512], d=expc[ep][:]:
                    E.activation(out=d, in_=s, func=AF.Exp, bias=stt[:, 30:31],
                                 scale=SOFT_SCALE),
                    reads=[sb, r_stt], writes=[r_expc[ep]])
            if kk >= 4 * Q:
                jd = kk - 4 * Q
                em.emit("v", lambda E, d=expc[ep][:, jd * P:(jd + 1) * P]:
                        E.tensor_tensor(d, d, maskc[:], ALU.mult),
                        reads=[r_consts], writes=[r_expc[ep]])
            return ep

        def avs(hh, Q, kk, ep):
            sg = Q % 2
            jls = list(range(max(kk - 4 * Q, 0), 4))
            em.barrier_read("t", [r_expc[ep], r_V])
            for idx, jl in enumerate(jls):
                qt = 4 * Q + jl
                bank = b[jl]
                dst = pb0[:, jl * 512: jl * 512 + 129]
                lhs = expc[ep][:, jl * P:(jl + 1) * P]
                rhs = vtile(kk)[:, hh * 129:(hh + 1) * 129]
                if kk == 0:
                    em.warw("t", bank)
                if kk == qt or idx == len(jls) - 1:
                    tok = em.inc("t", lambda E, d=dst, l=lhs, r=rhs,
                                 s1=(kk == 0), s2=(kk == qt):
                                 E.matmul(d, l, r, start=s1, stop=s2))
                    r_expc[ep].r.append(tok)
                    r_V.r.append(tok)
                    if kk == qt:
                        bank.w = tok
                        bank.r = []
                        sc = (Q % 2) * 4 + jl
                        em.emit("v", lambda E, d=sttC[:, sc:sc + 1],
                                s=pb0[:, jl * 512 + 128: jl * 512 + 129]:
                                E.reciprocal(d, s),
                                reads=[bank], writes=[r_sC])
                        em.emit("v", lambda E, d=stg[sg][:, jl * P:(jl + 1) * P],
                                s=pb0[:, jl * 512: jl * 512 + 128],
                                r_=sttC[:, sc:sc + 1]:
                                E.tensor_scalar(out=d, in0=s, scalar1=r_,
                                                scalar2=None, op0=ALU.mult),
                                reads=[bank, r_sC], writes=[r_stage[sg]])
                else:
                    mm_raw(dst, lhs, rhs, (kk == 0), False)

        def stage_dma(hh, Q):
            sg = Q % 2
            em.emit("p", lambda E, s=stg[sg].rearrange("p (t n) -> p t n", n=P),
                    d=ao_sp[4 * Q:4 * Q + 4, :, hh * P:(hh + 1) * P].rearrange("t p n -> p t n"):
                    E.dma_start(out=d, in_=s),
                    reads=[r_stage[sg]], writes=[r_spill["ao"]], dma=True)

        qh_load(0)
        qh_load(1)
        for hh in range(NH):
            chunks = [(Q, kk) for Q in range(4) for kk in range(4 * Q + 4)]
            eps = {0: score_exp(hh, *chunks[0]), 1: score_exp(hh, *chunks[1])}
            for i, (Q, kk) in enumerate(chunks):
                if i + 2 < len(chunks):
                    eps[i + 2] = score_exp(hh, *chunks[i + 2])
                avs(hh, Q, kk, eps[i])
                if kk == 4 * Q + 3:
                    stage_dma(hh, Q)
            if hh + 2 < NH:
                qh_load(hh + 2)
            emit_cache_chunks(12)

        r_WoT = seed(r_qh[0], r_qh[1], r_expc[0], r_expc[1], r_stage[0], r_stage[1],
                     r_slice[0], r_slice[1], r_V)
        r_ao = [seed(r_K), seed(r_K)]
        r_aot = seed(r_K)
        r_ln2hi = seed(r_K)
        r_ln2lo = seed(r_K)
        r_hiT = seed(r_K)
        r_loT = seed(r_K)
        r_lnf = seed(r_V)
        r_sqd = seed(r_V)
        r_wall = Res()
        r_wT = seed(r_V)

        A1w = A1[:, :].rearrange("p (hc n) -> p hc n", n=2048)
        for oc in range(NH):
            prep_chunk(wo_h[oc * P:(oc + 1) * P, :],
                       lambda hf, oc=oc: A1w[:, hf * 8:hf * 8 + 8, oc * P:(oc + 1) * P],
                       r_WoT)

        aosl = [A2[:, 0:2048], A2[:, 2048:4096]]
        aotb = A2[:, 4096:6144]
        ln2hi = A2[:, 6144:8192]
        ln2lo = A2[:, 8192:10240]
        hiT = A2[:, 10240:12288]
        loT = A2[:, 12288:14336]
        lnf = A3[:, 0:4096].bitcast(F32)
        sqd = A3[:, 8192:12288].bitcast(F32)[:, 0:2048]
        xfa2 = [A2[:, 14336:18432].bitcast(F32), A2[:, 18432:22528].bitcast(F32)]
        r_xfa2 = [seed(r_K), seed(r_K)]
        aotv = aotb.rearrange("p (c n) -> p c n", n=P)
        hiTv = hiT.rearrange("p (c n) -> p c n", n=P)
        loTv = loT.rearrange("p (c n) -> p c n", n=P)

        def c8(i):
            return rt[:, i * NE:(i + 1) * NE]

        (RZ, REX, RINH, RD0, RDIST, REX1, RFB, RZONE, RWT, RSELR, RFB2,
         RFBM, RFBW, RG1, RG2, RW0, RDIFF, RDSC, RINV, RSEL2R, RT1, RT2) = range(22)

        def c2_stageA(tt):
            sl = tt % 2
            xf = xfa2[sl]
            rxf = r_xfa2[sl]
            em.emit("y", lambda E, s=ao_sp[tt], d=aosl[sl]: E.dma_start(out=d, in_=s),
                    reads=[r_spill["ao"]], writes=[r_ao[sl]], dma=True)
            em.emit("y", lambda E, d=xf[:], s=x0_h[tt * P:(tt + 1) * P, :]: E.dma_start(out=d, in_=s),
                    writes=[rxf], dma=True)
            transpose16(aosl[sl], r_ao[sl],
                        lambda hf: aotv[:, hf * 8:hf * 8 + 8, :], r_aot)
            em.barrier_read("t", [r_aot, r_WoT])
            for hc in range(NH):
                for ob in range(4):
                    bank = b[ob]
                    lhs = aotb[:, hc * P:(hc + 1) * P]
                    rhs = A1[:, hc * 2048 + ob * 512: hc * 2048 + (ob + 1) * 512]
                    dst = pb0[:, ob * 512:(ob + 1) * 512]
                    if hc == 0:
                        mm_open(bank)
                        mm_raw(dst, lhs, rhs, True, False)
                    elif hc == NH - 1:
                        mm_close(dst, lhs, rhs, False, bank, [r_aot, r_WoT])
                    else:
                        mm_raw(dst, lhs, rhs, False, False)
            for ob in range(4):
                em.emit("v", lambda E, s=pb0[:, ob * 512:(ob + 1) * 512],
                        x=xf[:, ob * 512:(ob + 1) * 512]:
                        E.scalar_tensor_tensor(out=x, in0=s, scalar=1.0, in1=x,
                                               op0=ALU.mult, op1=ALU.add),
                        reads=[b[ob]], writes=[rxf])
            em.emit("p", lambda E, d=x1_sp[tt * P:(tt + 1) * P, :], s=xf[:]: E.dma_start(out=d, in_=s),
                    reads=[rxf], writes=[r_spill["x1"]], dma=True)

        def c2_stageB(tt):
            sl = tt % 2
            xf = xfa2[sl]
            rxf = r_xfa2[sl]
            emit_ln(xf[:], rxf, ln2hi, r_ln2hi, sqd, [r_sqd],
                    lo=(lnf[:, 0:2048], r_lnf, ln2lo, r_ln2lo))
            transpose16(ln2hi, r_ln2hi, lambda hf: hiTv[:, hf * 8:hf * 8 + 8, :], r_hiT)
            transpose16(ln2lo, r_ln2lo, lambda hf: loTv[:, hf * 8:hf * 8 + 8, :], r_loT)
            tb2, off = tt // 8, (tt % 8) * P
            em.emit("p", lambda E, d=ln2t_sp[tb2, :, :, off:off + P], s=hiTv:
                    E.dma_start(out=d, in_=s),
                    reads=[r_hiT], writes=[r_spill["ln2t"]], dma=True)
            em.barrier_read("t", [r_hiT, r_loT, r_wr])
            mm_open(b[6])
            k = 0
            nmm = ND * 3
            for dc in range(ND):
                for (lb, rb) in ((hiT, wr_hi), (loT, wr_hi), (hiT, wr_lo)):
                    lhs = lb[:, dc * P:(dc + 1) * P]
                    rhs = rb[:, dc * NE:(dc + 1) * NE]
                    if k == nmm - 1:
                        mm_close(pb6[:, 0:NE], lhs, rhs, False, b[6], [r_hiT, r_loT, r_wr])
                    else:
                        mm_raw(pb6[:, 0:NE], lhs, rhs, (k == 0), False)
                    k += 1
            em.emit("v", lambda E: E.tensor_scalar(out=c8(RZ), in0=pb6[:, 0:NE],
                                                   scalar1=stt[:, 9:10], scalar2=None, op0=ALU.mult),
                    reads=[b[6], r_stt], writes=[r_rt])
            em.emit("s", lambda E: E.activation(out=c8(REX), in_=c8(RZ), func=AF.Exp,
                                                bias=0.0, scale=-1.0),
                    reads=[r_rt], writes=[r_rt])
            em.emit("v", lambda E: E.tensor_scalar(out=c8(RT1), in0=c8(REX), scalar1=1.0,
                                                   scalar2=None, op0=ALU.add),
                    reads=[r_rt], writes=[r_rt])
            em.emit("v", lambda E: E.reciprocal(c8(RINH), c8(RT1)), reads=[r_rt], writes=[r_rt])
            em.emit("v", lambda E: E.tensor_scalar(out=c8(RD0), in0=c8(RINH), scalar1=GOLD_C,
                                                   scalar2=None, op0=ALU.subtract),
                    reads=[r_rt], writes=[r_rt])
            em.emit("s", lambda E: E.activation(out=c8(RDIST), in_=c8(RD0), func=AF.Abs,
                                                bias=0.0, scale=1.0),
                    reads=[r_rt], writes=[r_rt])
            em.emit("s", lambda E: E.activation(out=c8(REX1), in_=c8(RDIST), func=AF.Exp,
                                                bias=0.0, scale=-10.0),
                    reads=[r_rt], writes=[r_rt])
            em.emit("s", lambda E: E.activation(out=c8(RFB), in_=c8(RDIST), func=AF.Exp,
                                                bias=0.0, scale=float(np.float32(-1.0 / 0.3))),
                    reads=[r_rt], writes=[r_rt])
            em.emit("v", lambda E: E.tensor_single_scalar(c8(RG1), c8(RINH), GOLD_L, ALU.is_ge),
                    reads=[r_rt], writes=[r_rt])
            em.emit("v", lambda E: E.tensor_single_scalar(c8(RG2), c8(RINH), GOLD_U, ALU.is_le),
                    reads=[r_rt], writes=[r_rt])
            em.emit("v", lambda E: E.tensor_tensor(c8(RZONE), c8(RG1), c8(RG2), ALU.mult),
                    reads=[r_rt], writes=[r_rt])
            em.emit("v", lambda E: E.tensor_tensor(c8(RWT), c8(REX1), c8(RZONE), ALU.mult),
                    reads=[r_rt], writes=[r_rt])
            em.emit("v", lambda E: E.reduce_sum(stt[:, 10:11], c8(RWT), axis=AX.X),
                    reads=[r_rt], writes=[r_stt])
            em.emit("v", lambda E: E.tensor_single_scalar(stt[:, 11:12], stt[:, 10:11],
                                                          1e-8, ALU.is_lt),
                    reads=[r_stt], writes=[r_stt])
            em.emit("v", lambda E: E.reduce_max(stt[:, 12:13], c8(RFB), axis=AX.X),
                    reads=[r_rt], writes=[r_stt])
            em.emit("v", lambda E: E.tensor_scalar(out=c8(RT1), in0=c8(RFB), scalar1=stt[:, 12:13],
                                                   scalar2=None, op0=ALU.is_ge),
                    reads=[r_rt, r_stt], writes=[r_rt])
            em.emit("v", lambda E: E.tensor_tensor(c8(RSELR), c8(RT1), rev[:], ALU.mult),
                    reads=[r_rt, r_consts], writes=[r_rt])
            em.emit("v", lambda E: E.reduce_max(stt[:, 13:14], c8(RSELR), axis=AX.X),
                    reads=[r_rt], writes=[r_stt])
            em.emit("v", lambda E: E.tensor_scalar(out=c8(RT1), in0=c8(RSELR), scalar1=stt[:, 13:14],
                                                   scalar2=None, op0=ALU.is_ge),
                    reads=[r_rt, r_stt], writes=[r_rt])
            em.emit("v", lambda E: E.tensor_scalar(out=c8(RINV), in0=c8(RT1), scalar1=-1.0,
                                                   scalar2=1.0, op0=ALU.mult, op1=ALU.add),
                    reads=[r_rt], writes=[r_rt])
            em.emit("v", lambda E: E.tensor_tensor(c8(RFB2), c8(RFB), c8(RINV), ALU.mult),
                    reads=[r_rt], writes=[r_rt])
            em.emit("v", lambda E: E.reduce_max(stt[:, 14:15], c8(RFB2), axis=AX.X),
                    reads=[r_rt], writes=[r_stt])
            em.emit("v", lambda E: E.tensor_scalar(out=c8(RT2), in0=c8(RFB2), scalar1=stt[:, 14:15],
                                                   scalar2=None, op0=ALU.is_ge),
                    reads=[r_rt, r_stt], writes=[r_rt])
            em.emit("v", lambda E: E.tensor_tensor(c8(RSEL2R), c8(RT2), rev[:], ALU.mult),
                    reads=[r_rt, r_consts], writes=[r_rt])
            em.emit("v", lambda E: E.reduce_max(stt[:, 16:17], c8(RSEL2R), axis=AX.X),
                    reads=[r_rt], writes=[r_stt])
            em.emit("v", lambda E: E.tensor_scalar(out=c8(RT2), in0=c8(RSEL2R), scalar1=stt[:, 16:17],
                                                   scalar2=None, op0=ALU.is_ge),
                    reads=[r_rt, r_stt], writes=[r_rt])
            em.emit("v", lambda E: E.tensor_tensor(c8(RFBM), c8(RT1), c8(RT2), ALU.add),
                    reads=[r_rt], writes=[r_rt])
            em.emit("v", lambda E: E.tensor_tensor(c8(RFBW), c8(RFB), c8(RFBM), ALU.mult),
                    reads=[r_rt], writes=[r_rt])
            em.emit("v", lambda E: E.reduce_sum(stt[:, 17:18], c8(RFBW), axis=AX.X),
                    reads=[r_rt], writes=[r_stt])
            em.emit("v", lambda E: E.tensor_single_scalar(stt[:, 18:19], stt[:, 17:18],
                                                          1e-8, ALU.max),
                    reads=[r_stt], writes=[r_stt])
            em.emit("v", lambda E: E.reciprocal(stt[:, 19:20], stt[:, 18:19]),
                    reads=[r_stt], writes=[r_stt])
            em.emit("v", lambda E: E.tensor_scalar(out=c8(RFBW), in0=c8(RFBW), scalar1=stt[:, 19:20],
                                                   scalar2=None, op0=ALU.mult),
                    reads=[r_stt], writes=[r_rt])
            em.emit("v", lambda E: E.tensor_tensor(c8(RDIFF), c8(RFBW), c8(RWT), ALU.subtract),
                    reads=[r_rt], writes=[r_rt])
            em.emit("v", lambda E: E.tensor_scalar(out=c8(RDSC), in0=c8(RDIFF), scalar1=stt[:, 11:12],
                                                   scalar2=None, op0=ALU.mult),
                    reads=[r_rt, r_stt], writes=[r_rt])
            em.emit("v", lambda E: E.tensor_tensor(c8(RW0), c8(RWT), c8(RDSC), ALU.add),
                    reads=[r_rt], writes=[r_rt])
            em.emit("v", lambda E: E.reduce_sum(stt[:, 20:21], c8(RW0), axis=AX.X),
                    reads=[r_rt], writes=[r_stt])
            em.emit("v", lambda E: E.tensor_single_scalar(stt[:, 21:22], stt[:, 20:21],
                                                          1e-8, ALU.max),
                    reads=[r_stt], writes=[r_stt])
            em.emit("v", lambda E: E.reciprocal(stt[:, 22:23], stt[:, 21:22]),
                    reads=[r_stt], writes=[r_stt])
            em.emit("v", lambda E, d=w_all[:, tt * NE:(tt + 1) * NE]:
                    E.tensor_scalar(out=d, in0=c8(RW0), scalar1=stt[:, 22:23],
                                    scalar2=None, op0=ALU.mult),
                    reads=[r_rt, r_stt], writes=[r_wall])
            emit_cache_chunks(4)

        c2_stageA(0)
        for tt in range(NT):
            if tt + 1 < NT:
                c2_stageA(tt + 1)
            c2_stageB(tt)

        em.emit("v", lambda E: E.tensor_copy(w_bfe[:], w_all[:]), reads=[r_wall], writes=[r_wall])
        for tt in range(NT):
            em.barrier_read("t", [r_wall, r_consts])
            em.warw("t", b[4])
            tok = em.inc("t", lambda E, s=w_bfe[:, tt * NE:(tt + 1) * NE]:
                         E.transpose(tpb4[0:NE, 0:P], s, ident[:]))
            b[4].w = tok
            b[4].r = []
            r_wall.r.append(tok)
            em.emit("v", lambda E, d=wT[:, tt * P:(tt + 1) * P]: E.tensor_copy(d, tpb4[0:NE, 0:P]),
                    reads=[b[4]], writes=[r_wT])

        r_slot = [seed(r_WoT) for _ in range(4)]
        r_ln2blk = seed(r_ao[0], r_ao[1], r_aot, r_ln2hi, r_ln2lo, r_hiT, r_loT, r_xfa2[0], r_xfa2[1])
        r_h = seed(r_ln2blk)
        r_part = seed(r_lnf, r_sqd, r_V)
        r_wbc = seed(r_lnf, r_sqd, r_V)
        r_x1b = [seed(r_lnf, r_sqd, r_V), seed(r_lnf, r_sqd, r_V)]
        r_outst = [seed(r_lnf, r_sqd, r_V), seed(r_lnf, r_sqd, r_V)]
        r_s12 = [seed(r_lnf, r_sqd, r_V), seed(r_lnf, r_sqd, r_V)]
        r_wrow = seed(r_V)

        ln2blk = A2[:, 0:16384]
        hbuf = A2[:, 16384:32768]
        S4 = [A1[:, 0:8192], A1[:, 8192:16384], A1[:, 16384:24576], A1[:, 24576:32768]]
        partial = A3[:, 0:16384]
        wbc = A3[:, 16384:20480].bitcast(F32)[:, 0:1024]
        x1b = A3[:, 20480:22528].bitcast(F32)
        outst = A3[:, 22528:24576].bitcast(F32)
        s1b = [A3[:, 24576:25600], A3[:, 25600:26624]]
        s2b = [A3[:, 26624:27648], A3[:, 27648:28672]]

        emit_cache_chunks(len(cache_jobs))

        for tb in range(2):
            em.emit("y", lambda E, s=ln2t_sp[tb], d=ln2blk.rearrange("p (c n) -> p c n", n=1024):
                    E.dma_start(out=d, in_=s),
                    reads=[r_spill["ln2t"]], writes=[r_ln2blk], dma=True)
            loads = []
            for ex_ in range(NE):
                for fb_ in range(4):
                    loads.append(("y", "wg", ex_, fb_))
                    loads.append(("p", "wu", ex_, fb_))
                for db_ in range(4):
                    loads.append(("s", "wd", ex_, db_))
            lpos = [0]

            def issue_to(n, loads=loads, lpos=lpos):
                while lpos[0] < min(n, len(loads)):
                    i = lpos[0]
                    lpos[0] += 1
                    qe, kind, ex_, j = loads[i]
                    sl = i % 4
                    if kind == "wg":
                        src = wg_c[ex_, :, j * 512:(j + 1) * 512].rearrange("(dc p) n -> p dc n", p=P)
                    elif kind == "wu":
                        src = wu_c[ex_, :, j * 512:(j + 1) * 512].rearrange("(dc p) n -> p dc n", p=P)
                    else:
                        src = wd_c[ex_, :, j * 512:(j + 1) * 512].rearrange("(fc p) n -> p fc n", p=P)
                    em.emit(qe, lambda E, s=src, d=S4[sl].rearrange("p (c n) -> p c n", n=512):
                            E.dma_start(out=d, in_=s),
                            reads=[r_spill[kind + str(ex_)]], writes=[r_slot[sl]], dma=True)

            issue_to(4)
            for ex in range(NE):
                em.emit("y", lambda E, s=wT[ex:ex + 1, tb * 1024:(tb + 1) * 1024]:
                        E.dma_start(out=wrow, in_=s),
                        reads=[r_wT], writes=[r_wrow], dma=True)
                for half in range(2):
                    pbx = pb6 if half == 0 else pb7
                    bankx = b[6 + half]
                    em.emit("t", lambda E, d=pbx[:, 0:512], r=wrow[:, half * 512:(half + 1) * 512]:
                            E.matmul(d, ones1b[:], r, start=True, stop=True),
                            reads=[r_wrow, r_consts], writes=[bankx])
                    em.emit("v", lambda E, d=wbc[:, half * 512:(half + 1) * 512], s=pbx[:, 0:512]:
                            E.tensor_copy(d, s),
                            reads=[bankx], writes=[r_wbc])
                for fb in range(4):
                    gi = ex * 12 + 2 * fb
                    gsl_b = S4[gi % 4]
                    usl_b = S4[(gi + 1) % 4]
                    rg = r_slot[gi % 4]
                    ru = r_slot[(gi + 1) % 4]
                    for fc4 in range(4):
                        fc = fb * 4 + fc4
                        gb = (fc % 2) * 4
                        pset = pb0 if gb == 0 else pbB
                        sp12 = fc % 2
                        em.barrier_read("t", [rg, ru, r_ln2blk, r_wbc])
                        for which in range(2):
                            wslot = gsl_b if which == 0 else usl_b
                            rw = rg if which == 0 else ru
                            bps = 0 if which == 0 else 2
                            for dc in range(ND):
                                lhs = wslot[:, dc * 512 + fc4 * P: dc * 512 + (fc4 + 1) * P]
                                for th in range(2):
                                    bank = b[gb + bps + th]
                                    rhs = ln2blk[:, dc * 1024 + th * 512: dc * 1024 + (th + 1) * 512]
                                    dst = pset[:, (bps + th) * 512:(bps + th + 1) * 512]
                                    if dc == 0:
                                        mm_open(bank)
                                        mm_raw(dst, lhs, rhs, True, False)
                                    elif dc == ND - 1:
                                        mm_close(dst, lhs, rhs, False, bank,
                                                 [rw, r_ln2blk])
                                    else:
                                        mm_raw(dst, lhs, rhs, False, False)
                        em.emit("s", lambda E, o=s1b[sp12], i=pset[:, 0:1024]:
                                E.activation(out=o, in_=i, func=AF.Silu, bias=0.0, scale=1.0),
                                reads=[b[gb], b[gb + 1]], writes=[r_s12[sp12]])
                        em.emit("v", lambda E, o=s2b[sp12], i=pset[:, 1024:2048]:
                                E.tensor_tensor(o, i, wbc[:, 0:1024], ALU.mult),
                                reads=[b[gb + 2], b[gb + 3], r_wbc], writes=[r_s12[sp12]])
                        em.emit("p", lambda E, d=hbuf[:, fc * 1024:(fc + 1) * 1024],
                                a=s1b[sp12], c=s2b[sp12]:
                                E.tensor_tensor(d, a, c, ALU.mult),
                                reads=[r_s12[sp12]], writes=[r_h])
                    issue_to(gi + 2 + 4)
                for dblk in range(4):
                    di = ex * 12 + 8 + dblk
                    dsl = S4[di % 4]
                    rd = r_slot[di % 4]
                    em.barrier_read("t", [rd, r_h])
                    for t128 in range(8):
                        avp = pb6 if t128 % 2 == 0 else pb7
                        bank = b[6 + (t128 % 2)]
                        mm_open(bank)
                        for fc in range(16):
                            lhs = hbuf[:, fc * 1024 + t128 * P: fc * 1024 + (t128 + 1) * P]
                            rhs = dsl[:, fc * 512:(fc + 1) * 512]
                            if fc == 15:
                                mm_close(avp[:, 0:512], lhs, rhs, False, bank,
                                         [rd, r_h])
                            else:
                                mm_raw(avp[:, 0:512], lhs, rhs, (fc == 0), False)
                        pslice = partial[:, t128 * 2048 + dblk * 512: t128 * 2048 + (dblk + 1) * 512]
                        if ex == 0:
                            em.emit("v", lambda E, d=pslice, s=avp[:, 0:512]: E.tensor_copy(d, s),
                                    reads=[bank], writes=[r_part])
                        elif ex < NE - 1:
                            em.emit("v", lambda E, d=pslice, s=avp[:, 0:512]:
                                    E.tensor_tensor(d, s, d, ALU.add),
                                    reads=[bank, r_part], writes=[r_part])
                        else:
                            rows = tb * 1024 + t128 * P
                            xsl = x1b[:, (t128 % 2) * 512:((t128 % 2) + 1) * 512]
                            osl = outst[:, (t128 % 2) * 512:((t128 % 2) + 1) * 512]
                            em.emit("y", lambda E, d=xsl,
                                    s=x1_sp[rows:rows + P, dblk * 512:(dblk + 1) * 512]:
                                    E.dma_start(out=d, in_=s),
                                    reads=[r_spill["x1"]], writes=[r_x1b[t128 % 2]], dma=True)
                            em.emit("v", lambda E, d=osl, s=avp[:, 0:512], x=xsl:
                                    E.tensor_tensor(d, s, x, ALU.add),
                                    reads=[bank, r_x1b[t128 % 2]], writes=[r_outst[t128 % 2]])
                            em.emit("v", lambda E, d=osl, p_=pslice:
                                    E.tensor_tensor(d, d, p_, ALU.add),
                                    reads=[r_part], writes=[r_outst[t128 % 2]])
                            em.emit("p", lambda E, s=osl,
                                    d=out_h[rows:rows + P, dblk * 512:(dblk + 1) * 512]:
                                    E.dma_start(out=d, in_=s),
                                    reads=[r_outst[t128 % 2]], writes=[r_spill["out"]], dma=True)
                    issue_to(di + 1 + 4)

        em.wait("p", "dp", em.cnt["dp"])

        with nc.Block() as block:
            block.sync(lambda E: [f(E) for f in em.th["y"]])
            block.tensor(lambda E: [f(E) for f in em.th["t"]])
            block.vector(lambda E: [f(E) for f in em.th["v"]])
            block.scalar(lambda E: [f(E) for f in em.th["s"]])
            block.gpsimd(lambda E: [f(E) for f in em.th["p"]])

    return nc


_NC_CACHE = {}


def kernel(L0_attn_hidden_states, L0_attn_Wq, L0_attn_Wk, L0_attn_Wv, L0_attn_Wo,
           L0_ffn_Wg, L0_ffn_Wu, L0_ffn_Wd, L0_ffn_Wr, L0_ffn_temperature):
    x = np.ascontiguousarray(np.asarray(L0_attn_hidden_states, dtype=np.float32))
    base = {
        "wq": np.ascontiguousarray(np.asarray(L0_attn_Wq, dtype=np.float32)),
        "wk": np.ascontiguousarray(np.asarray(L0_attn_Wk, dtype=np.float32)),
        "wv": np.ascontiguousarray(np.asarray(L0_attn_Wv, dtype=np.float32)),
        "wo": np.ascontiguousarray(np.asarray(L0_attn_Wo, dtype=np.float32)),
        "wg": np.ascontiguousarray(np.asarray(L0_ffn_Wg, dtype=np.float32)),
        "wu": np.ascontiguousarray(np.asarray(L0_ffn_Wu, dtype=np.float32)),
        "wd": np.ascontiguousarray(np.asarray(L0_ffn_Wd, dtype=np.float32)),
        "wr": np.ascontiguousarray(np.asarray(L0_ffn_Wr, dtype=np.float32)),
        "temp": np.asarray(L0_ffn_temperature, dtype=np.float32).reshape(1, 1),
    }
    base.update(host_consts())
    if "nc" not in _NC_CACHE:
        _NC_CACHE["nc"] = build(debug=False)
    nc = _NC_CACHE["nc"]
    in_maps = [dict(base, x0=x[c]) for c in range(8)]
    res = run_bass_kernel_spmd(nc, in_maps, core_ids=list(range(8)))
    return np.stack([r["out"] for r in res.results], axis=0).astype(np.float32)


# version 15 (final)
# speedup vs baseline: 1.4723x; 1.0106x over previous
import math
import numpy as np
import ml_dtypes
from contextlib import ExitStack

import concourse.bass as bass
import concourse.mybir as mybir
from concourse.bass_utils import run_bass_kernel_spmd

F32 = mybir.dt.float32
BF16 = mybir.dt.bfloat16
AF = mybir.ActivationFunctionType
ALU = mybir.AluOpType
AX = mybir.AxisListType

P = 128
S = 2048
DM = 2048
NH = 16
HD = 128
NE = 8
FF = 2048
NT = S // P
ND = DM // P
SOFT_SCALE = float(np.float32(1.0 / math.sqrt(HD)))
EPS = float(np.float32(1e-5))
GOLD_C = float(np.float32(1.0 / np.e))
GOLD_L = float(np.float32(0.5 - np.log(4.0 / 3.0)))
GOLD_U = float(np.float32(0.5))

ENGS = ("t", "v", "s", "p", "y")


class Res:
    __slots__ = ("w", "r")

    def __init__(self):
        self.w = None
        self.r = []


def seed(*olds):
    n = Res()
    toks = []
    for o in olds:
        toks.extend(o.r)
        if o.w is not None:
            toks.append(o.w)
    n.r = toks
    return n


class Em:
    def __init__(self, nc):
        self.nc = nc
        self.th = {e: [] for e in ENGS}
        self.sem = {}
        self.cnt = {}
        self.waited = {e: {} for e in ENGS}

    def mksem(self, ctx, names):
        for n in names:
            self.sem[n] = ctx.enter_context(self.nc.semaphore(n))
            self.cnt[n] = 0

    def wait(self, eng, sname, val):
        if val is None or val <= 0:
            return
        if self.waited[eng].get(sname, 0) >= val:
            return
        self.waited[eng][sname] = val
        h = self.sem[sname]
        self.th[eng].append(lambda E, h=h, v=val: E.wait_ge(h, v))

    def _wait_tok(self, eng, tok):
        self.wait(eng, tok[0], tok[1])

    def barrier_read(self, eng, reads):
        for r in reads:
            if r.w is not None:
                self._wait_tok(eng, r.w)

    def warw(self, eng, res):
        for rr in res.r:
            self._wait_tok(eng, rr)
        if res.w is not None:
            self._wait_tok(eng, res.w)

    def raw(self, eng, fn):
        self.th[eng].append(fn)

    def inc(self, eng, fn, dma=False):
        if dma:
            sname = {"y": "dy", "p": "dp", "s": "ds", "v": "dv"}[eng]
            step = 16
        else:
            sname = eng
            step = 1
        h = self.sem[sname]
        self.cnt[sname] += step
        self.th[eng].append(lambda E, fn=fn, h=h, s=step: fn(E).then_inc(h, s))
        return (sname, self.cnt[sname])

    def emit(self, eng, fn, reads=(), writes=(), dma=False):
        for r in reads:
            if r.w is not None:
                self._wait_tok(eng, r.w)
        for w in writes:
            self.warw(eng, w)
        tok = self.inc(eng, fn, dma=dma)
        for r in reads:
            r.r.append(tok)
        for w in writes:
            w.w = tok
            w.r = []
        return tok


def _np_rope_tables():
    inv_freq = 1.0 / 10000.0 ** (np.arange(0, HD, 2, dtype=np.float32) / HD)
    t = np.arange(S, dtype=np.float32)
    freqs = np.einsum("i,j->ij", t, inv_freq)
    emb = np.concatenate((freqs, freqs), axis=-1)
    cosT = np.ascontiguousarray(np.cos(emb).astype(np.float32).T)
    sinT = np.ascontiguousarray(np.sin(emb).astype(np.float32).T)
    sin_rope = sinT.copy()
    sin_rope[:64, :] *= -1.0
    return cosT, sin_rope


def host_consts():
    ident = np.eye(P, dtype=np.float32).astype(ml_dtypes.bfloat16)
    cosT, sinT = _np_rope_tables()
    mask = np.where(np.arange(P)[:, None] <= np.arange(P)[None, :], 1.0, 0.0)
    maskc = mask.astype(np.float32).astype(ml_dtypes.bfloat16)
    rev = np.broadcast_to(np.arange(NE, 0, -1, dtype=np.float32)[None, :], (P, NE)).copy()
    ones1 = np.ones((1, P), dtype=np.float32)
    ones1b = np.ones((1, P), dtype=np.float32).astype(ml_dtypes.bfloat16)
    return {"c_ident": ident, "c_cos": cosT, "c_sin": sinT, "c_mask": maskc,
            "c_rev": rev, "c_ones1": ones1, "c_ones1b": ones1b}


def build(debug=False):
    nc = bass.Bass("TRN2", target_bir_lowering=False, dynamic_dma_scratch_size=16384)
    dt = nc.dram_tensor
    kind_sp = "ExternalOutput" if debug else "Internal"

    x0_h = dt("x0", [S, DM], F32, kind="ExternalInput").ap()
    wq_h = dt("wq", [DM, DM], F32, kind="ExternalInput").ap()
    wk_h = dt("wk", [DM, DM], F32, kind="ExternalInput").ap()
    wv_h = dt("wv", [DM, DM], F32, kind="ExternalInput").ap()
    wo_h = dt("wo", [DM, DM], F32, kind="ExternalInput").ap()
    wg_h = dt("wg", [NE, DM, FF], F32, kind="ExternalInput").ap()
    wu_h = dt("wu", [NE, DM, FF], F32, kind="ExternalInput").ap()
    wd_h = dt("wd", [NE, FF, DM], F32, kind="ExternalInput").ap()
    wr_h = dt("wr", [DM, NE], F32, kind="ExternalInput").ap()
    temp_h = dt("temp", [1, 1], F32, kind="ExternalInput").ap()
    c_ident = dt("c_ident", [P, P], BF16, kind="ExternalInput").ap()
    c_cos = dt("c_cos", [P, S], F32, kind="ExternalInput").ap()
    c_sin = dt("c_sin", [P, S], F32, kind="ExternalInput").ap()
    c_mask = dt("c_mask", [P, P], BF16, kind="ExternalInput").ap()
    c_rev = dt("c_rev", [P, NE], F32, kind="ExternalInput").ap()
    c_ones1 = dt("c_ones1", [1, P], F32, kind="ExternalInput").ap()
    c_ones1b = dt("c_ones1b", [1, P], BF16, kind="ExternalInput").ap()

    out_h = dt("out", [S, DM], F32, kind="ExternalOutput").ap()

    dbg1 = dt("dbg_ln1bf", [P, 2048], BF16, kind=kind_sp).ap() if debug else None
    dbg2 = dt("dbg_x0f", [P, 2048], F32, kind=kind_sp).ap() if debug else None
    ln1t_sp = dt("ln1t_sp", [NT, P, ND, P], BF16, kind=kind_sp).ap()
    q_sp = dt("q_sp", [NH, P, S], BF16, kind=kind_sp).ap()
    ao_sp = dt("ao_sp", [NT, P, DM], BF16, kind=kind_sp).ap()
    x1_sp = dt("x1_sp", [S, DM], F32, kind=kind_sp).ap()
    ln2t_sp = dt("ln2t_sp", [2, P, ND, 1024], BF16, kind=kind_sp).ap()
    wg_c = dt("wg_c", [NE, DM, FF], BF16, kind="Internal").ap()
    wu_c = dt("wu_c", [NE, DM, FF], BF16, kind="Internal").ap()
    wd_c = dt("wd_c", [NE, FF, DM], BF16, kind="Internal").ap()

    with ExitStack() as ctx:
        en = ctx.enter_context
        A1 = en(nc.sbuf_tensor("A1", [P, 32768], BF16))
        A2 = en(nc.sbuf_tensor("A2", [P, 32768], BF16))
        A3 = en(nc.sbuf_tensor("A3", [P, 32768], BF16))
        wstage = en(nc.sbuf_tensor("wstage", [P, 2048], F32))
        cast_buf = en(nc.sbuf_tensor("cast_buf", [P, 2048], BF16))
        stt = en(nc.sbuf_tensor("stt", [P, 32], F32))
        sttC = en(nc.sbuf_tensor("sttC", [P, 32], F32))
        rt = en(nc.sbuf_tensor("rt", [P, 176], F32))
        w_all = en(nc.sbuf_tensor("w_all", [P, 128], F32))
        w_bfe = en(nc.sbuf_tensor("w_bfe", [P, 128], BF16))
        ident = en(nc.sbuf_tensor("ident", [P, P], BF16))
        maskc = en(nc.sbuf_tensor("maskc", [P, P], BF16))
        rev = en(nc.sbuf_tensor("rev", [P, NE], F32))
        ones1 = en(nc.sbuf_tensor("ones1", [1, P], F32))
        ones1b = en(nc.sbuf_tensor("ones1b", [1, P], BF16))
        wr_hi = en(nc.sbuf_tensor("wr_hi", [P, ND * NE], BF16))
        wr_lo = en(nc.sbuf_tensor("wr_lo", [P, ND * NE], BF16))
        wr_f = en(nc.sbuf_tensor("wr_f", [P, NE], F32))
        temp_s = en(nc.sbuf_tensor("temp_s", [1, 1], F32))
        pb0 = en(nc.psum_tensor("pb0", [P, 2048], F32))
        pbB = en(nc.psum_tensor("pbB", [P, 2048], F32))
        pb4 = pbB[:, 0:512]
        pb5 = pbB[:, 512:1024]
        pb6 = pbB[:, 1024:1536]
        pb7 = pbB[:, 1536:2048]
        tpb4 = pb4.bitcast(BF16)
        tpb5 = pb5.bitcast(BF16)
        wT = A3[0:8, 28672:30720]
        wrow = A3[0:1, 30720:31744]

        em = Em(nc)
        em.mksem(ctx, ["t", "v", "s", "p", "dy", "dp", "ds", "dv"])

        b = [Res() for _ in range(8)]
        r_consts = Res()
        r_tabs = Res()
        r_wr = Res()
        r_temp = Res()
        r_stt = Res()
        r_rt = Res()
        r_xfa = Res()
        r_wstage = [Res(), Res()]
        r_castbuf = [Res(), Res()]
        wslot_ctr = [0]
        r_spill = {k: Res() for k in
                   ("ln1t", "q", "ao", "x1", "ln2t", "out")}
        for nm in ("wg", "wu", "wd"):
            for ex_ in range(NE):
                r_spill[nm + str(ex_)] = Res()

        def mm_open(bank):
            em.warw("t", bank)

        def mm_raw(dst, lhs, rhs, start, stop):
            em.raw("t", lambda E, d=dst, l=lhs, r=rhs, st=start, sp=stop:
                   E.matmul(d, l, r, start=st, stop=sp))

        def mm_close(dst, lhs, rhs, start, bank, readers):
            tok = em.inc("t", lambda E, d=dst, l=lhs, r=rhs, st=start:
                         E.matmul(d, l, r, start=st, stop=True))
            bank.w = tok
            bank.r = []
            for r in readers:
                r.r.append(tok)
            return tok

        def tr_raw(dst, src):
            em.raw("t", lambda E, d=dst, s=src: E.transpose(d, s, ident[:]))

        def tr_close(dst, src, bank, readers):
            tok = em.inc("t", lambda E, d=dst, s=src: E.transpose(d, s, ident[:]))
            bank.w = tok
            bank.r = []
            for r in readers:
                r.r.append(tok)
            return tok

        def prep_chunk(w_rows_ap, dst_fn, dst_res):
            for hf in range(2):
                cb, cres = stage_load_cast(w_rows_ap[:, hf * 1024:(hf + 1) * 1024])
                em.barrier_read("t", [cres, r_consts])
                pb = tpb4 if hf == 0 else tpb5
                bank = b[4 + hf]
                mm_open(bank)
                for j in range(8):
                    s = cb[:, j * P:(j + 1) * P]
                    d = pb[:, j * P:(j + 1) * P]
                    if j < 7:
                        tr_raw(d, s)
                    else:
                        tr_close(d, s, bank, [cres])
                sp = pb[:, :].rearrange("p (c n) -> p c n", n=P)
                em.emit("v", lambda E, s=sp, d=dst_fn(hf): E.tensor_copy(d, s),
                        reads=[bank], writes=[dst_res])

        def transpose16(src_bf, src_res, dst_fn, dst_res, nhalf=2):
            em.barrier_read("t", [src_res, r_consts])
            for half in range(nhalf):
                pb = tpb4 if half % 2 == 0 else tpb5
                bank = b[4 + (half % 2)]
                mm_open(bank)
                for j in range(8):
                    dc = half * 8 + j
                    s = src_bf[:, dc * P:(dc + 1) * P]
                    d = pb[:, j * P:(j + 1) * P]
                    if j < 7:
                        tr_raw(d, s)
                    else:
                        tr_close(d, s, bank, [src_res])
                sp = pb[:, :].rearrange("p (c n) -> p c n", n=P)
                em.emit("v", lambda E, s=sp, d=dst_fn(half): E.tensor_copy(d, s),
                        reads=[bank], writes=[dst_res])

        for dst_ap, src_ap, res in ((ident[:], c_ident, r_consts), (maskc[:], c_mask, r_consts),
                                    (rev[:], c_rev, r_consts), (ones1[:], c_ones1, r_consts),
                                    (ones1b[:], c_ones1b, r_consts), (temp_s[:], temp_h, r_temp)):
            em.emit("y", lambda E, d=dst_ap, s=src_ap: E.dma_start(out=d, in_=s),
                    writes=[res], dma=True)
        cosT = A3[:, 0:4096].bitcast(F32)
        sinT = A3[:, 4096:8192].bitcast(F32)
        em.emit("y", lambda E: E.dma_start(out=cosT, in_=c_cos), writes=[r_tabs], dma=True)
        em.emit("y", lambda E: E.dma_start(out=sinT, in_=c_sin), writes=[r_tabs], dma=True)
        for dc_ in range(ND):
            em.emit("y", lambda E, s=wr_h[dc_ * P:(dc_ + 1) * P, :]: E.dma_start(out=wr_f[:], in_=s),
                    writes=[r_wr], dma=True)
            em.emit("v", lambda E, d=wr_hi[:, dc_ * NE:(dc_ + 1) * NE]: E.tensor_copy(d, wr_f[:]),
                    reads=[r_wr], writes=[r_wr])
            em.emit("v", lambda E, d=wr_lo[:, dc_ * NE:(dc_ + 1) * NE],
                    h=wr_hi[:, dc_ * NE:(dc_ + 1) * NE]:
                    E.tensor_tensor(d, wr_f[:], h, ALU.subtract),
                    reads=[r_wr], writes=[r_wr])
        em.emit("t", lambda E: E.matmul(pb6[:, 0:1], ones1[:], temp_s[:], start=True, stop=True),
                reads=[r_consts, r_temp], writes=[b[6]])
        em.emit("v", lambda E: E.tensor_copy(stt[:, 15:16], pb6[:, 0:1]), reads=[b[6]], writes=[r_stt])
        em.emit("v", lambda E: E.memset(stt[:, 23:24], EPS), writes=[r_stt])
        em.emit("v", lambda E: E.memset(stt[:, 30:31], -20.0), writes=[r_stt])
        em.emit("v", lambda E: E.reciprocal(stt[:, 9:10], stt[:, 15:16]), reads=[r_stt], writes=[r_stt])

        cache_jobs = []
        for ex in range(NE):
            for mat_h, mat_c, name in ((wg_h, wg_c, "wg"), (wu_h, wu_c, "wu"), (wd_h, wd_c, "wd")):
                for rc in range(ND):
                    for hf in range(2):
                        cache_jobs.append((mat_h[ex, rc * P:(rc + 1) * P, hf * 1024:(hf + 1) * 1024],
                                           mat_c[ex, rc * P:(rc + 1) * P, hf * 1024:(hf + 1) * 1024],
                                           name + str(ex)))
        cache_pos = [0]

        def stage_load_cast(src_ap):
            sl = wslot_ctr[0] % 2
            wslot_ctr[0] += 1
            wv = wstage[:, sl * 1024:(sl + 1) * 1024]
            cb = cast_buf[:, sl * 1024:(sl + 1) * 1024]
            qe = "y" if sl == 0 else "p"
            em.emit(qe, lambda E, d=wv, s=src_ap: E.dma_start(out=d, in_=s),
                    writes=[r_wstage[sl]], dma=True)
            if sl == 0:
                em.emit("s", lambda E, d=cb, s=wv: E.activation(out=d, in_=s, func=AF.Copy,
                                                                bias=0.0, scale=1.0),
                        reads=[r_wstage[sl]], writes=[r_castbuf[sl]])
            else:
                em.emit("v", lambda E, d=cb, s=wv: E.tensor_copy(d, s),
                        reads=[r_wstage[sl]], writes=[r_castbuf[sl]])
            return cb, r_castbuf[sl]

        def emit_cache_chunks(k):
            for _ in range(k):
                if cache_pos[0] >= len(cache_jobs):
                    return
                src, dst, name = cache_jobs[cache_pos[0]]
                cache_pos[0] += 1
                cb, cres = stage_load_cast(src)
                em.emit("s", lambda E, d=dst, s=cb: E.dma_start(out=d, in_=s),
                        reads=[cres], writes=[r_spill[name]], dma=True)

        def emit_ln(src_ap, src_res, dst_ap, dst_res, dump_ap, dump_writes, lo=None):
            em.emit("v", lambda E: E.reduce_sum(stt[:, 0:1], src_ap, axis=AX.X),
                    reads=[src_res], writes=[r_stt])
            em.emit("s", lambda E: E.mul(stt[:, 1:2], stt[:, 0:1], -1.0 / DM),
                    reads=[r_stt], writes=[r_stt])
            em.emit("s", lambda E: E.activation(out=dump_ap, in_=src_ap, func=AF.Square,
                                                bias=stt[:, 1:2], scale=1.0,
                                                accum_out=stt[:, 2:3]),
                    reads=[src_res, r_stt], writes=[r_stt] + list(dump_writes))
            em.emit("s", lambda E: E.activation(out=stt[:, 3:4], in_=stt[:, 2:3], func=AF.Ln,
                                                bias=stt[:, 23:24], scale=float(np.float32(1.0 / DM))),
                    reads=[r_stt], writes=[r_stt])
            em.emit("s", lambda E: E.activation(out=stt[:, 4:5], in_=stt[:, 3:4], func=AF.Exp,
                                                bias=0.0, scale=-0.5),
                    reads=[r_stt], writes=[r_stt])
            if lo is None:
                em.emit("v", lambda E: E.tensor_scalar(out=dst_ap, in0=src_ap,
                                                       scalar1=stt[:, 1:2], scalar2=stt[:, 4:5],
                                                       op0=ALU.add, op1=ALU.mult),
                        reads=[src_res, r_stt], writes=[dst_res])
            else:
                lnf_ap, lnf_res, lo_ap, lo_res = lo
                em.emit("v", lambda E: E.tensor_scalar(out=lnf_ap, in0=src_ap,
                                                       scalar1=stt[:, 1:2], scalar2=stt[:, 4:5],
                                                       op0=ALU.add, op1=ALU.mult),
                        reads=[src_res, r_stt], writes=[lnf_res])
                em.emit("s", lambda E: E.activation(out=dst_ap, in_=lnf_ap, func=AF.Copy,
                                                    bias=0.0, scale=1.0),
                        reads=[lnf_res], writes=[dst_res])
                em.emit("v", lambda E: E.tensor_tensor(lo_ap, lnf_ap, dst_ap, ALU.subtract),
                        reads=[lnf_res, dst_res], writes=[lo_res])

        r_ln1T = Res()
        r_A2s = [Res(), Res()]
        ln1bf2 = [A2[:, 0:2048], A2[:, 2048:4096]]
        xf_p1 = [A2[:, 4096:8192].bitcast(F32), A2[:, 8192:12288].bitcast(F32)]
        r_xf_p1 = [Res(), Res()]
        A1v = A1[:, :].rearrange("p (dc n) -> p dc n", n=2048)
        for tt in range(NT):
            xsl = tt % 2
            em.emit("y", lambda E, d=xf_p1[xsl][:], s=x0_h[tt * P:(tt + 1) * P, :]: E.dma_start(out=d, in_=s),
                    writes=[r_xf_p1[xsl]], dma=True)
            emit_ln(xf_p1[xsl][:], r_xf_p1[xsl], ln1bf2[xsl], r_A2s[xsl], pb0[:, 0:2048],
                    [b[0], b[1], b[2], b[3]])
            transpose16(ln1bf2[xsl], r_A2s[xsl],
                        lambda hf, tt=tt: A1v[:, hf * 8:hf * 8 + 8, tt * P:(tt + 1) * P],
                        r_ln1T)
            em.emit("p", lambda E, s=A1v[:, :, tt * P:(tt + 1) * P], d=ln1t_sp[tt]:
                    E.dma_start(out=d, in_=s),
                    reads=[r_ln1T], writes=[r_spill["ln1t"]], dma=True)
            emit_cache_chunks(12)

        r_t12 = Res()
        r_wqt = seed(r_A2s[0], r_A2s[1])
        r_wkt = Res()
        r_K = seed(r_wqt, r_xf_p1[0], r_xf_p1[1])
        t12f = A3[:, 8192:12288].bitcast(F32)
        t1 = t12f[:, 0:512]
        t2 = t12f[:, 512:1024]
        qbf2 = [A3[:, 12288:12800], A3[:, 12800:13312]]
        r_qbf = [Res(), Res()]
        wkt_st = A3[:, 13312:15360]

        def emit_proj(w_src_h, mode):
            for oc in range(NH):
                if mode == "q":
                    wt_res, wt_buf = r_wqt, A2[:, 0:2048]
                else:
                    wt_res, wt_buf = r_wkt, wkt_st
                wtv = wt_buf.rearrange("p (c n) -> p c n", n=P)
                prep_chunk(w_src_h[oc * P:(oc + 1) * P, :],
                           lambda hf, wtv=wtv: wtv[:, hf * 8:hf * 8 + 8, :], wt_res)
                em.barrier_read("t", [wt_res, r_ln1T])
                for tb in range(4):
                    bank = b[tb]
                    for dc in range(ND):
                        lhs = wt_buf[:, dc * P:(dc + 1) * P]
                        rhs = A1[:, dc * 2048 + tb * 512: dc * 2048 + (tb + 1) * 512]
                        dst = pb0[:, tb * 512:(tb + 1) * 512]
                        if dc == 0:
                            mm_open(bank)
                            mm_raw(dst, lhs, rhs, True, False)
                        elif dc == ND - 1:
                            mm_close(dst, lhs, rhs, False, bank, [wt_res, r_ln1T])
                        else:
                            mm_raw(dst, lhs, rhs, False, False)
                    ps = pb0[:, tb * 512:(tb + 1) * 512]
                    ps_lo = pb0[64:128, tb * 512:(tb + 1) * 512]
                    ps_hi = pb0[0:64, tb * 512:(tb + 1) * 512]
                    cs = cosT[:, tb * 512:(tb + 1) * 512]
                    sn = sinT[:, tb * 512:(tb + 1) * 512]
                    em.emit("v", lambda E, p_=ps, c_=cs: E.tensor_tensor(t1[:, :], p_, c_, ALU.mult),
                            reads=[bank, r_tabs], writes=[r_t12])
                    em.emit("v", lambda E, p_=ps_lo, s_=sn[0:64, :]:
                            E.tensor_tensor(t2[0:64, :], p_, s_, ALU.mult),
                            reads=[bank, r_tabs], writes=[r_t12])
                    em.emit("v", lambda E, p_=ps_hi, s_=sn[64:128, :]:
                            E.tensor_tensor(t2[64:128, :], p_, s_, ALU.mult),
                            reads=[bank, r_tabs], writes=[r_t12])
                    if mode == "q":
                        qsl = (oc * 4 + tb) % 2
                        em.emit("v", lambda E, d=qbf2[qsl]: E.tensor_tensor(d, t1[:, :], t2[:, :], ALU.add),
                                reads=[r_t12], writes=[r_qbf[qsl]])
                        em.emit("p", lambda E, d=q_sp[oc, :, tb * 512:(tb + 1) * 512], s=qbf2[qsl]:
                                E.dma_start(out=d, in_=s),
                                reads=[r_qbf[qsl]], writes=[r_spill["q"]], dma=True)
                    else:
                        dstk = A2[:, oc * 2048 + tb * 512: oc * 2048 + (tb + 1) * 512]
                        em.emit("v", lambda E, d=dstk: E.tensor_tensor(d, t1[:, :], t2[:, :], ALU.add),
                                reads=[r_t12], writes=[r_K])
                emit_cache_chunks(8)

        emit_proj(wq_h, "q")
        emit_proj(wk_h, "k")

        VW = NH * 129
        r_wvt = [seed(r_ln1T), seed(r_ln1T)]
        r_slice = [seed(r_ln1T), seed(r_ln1T)]
        r_V = seed(r_tabs, r_t12, r_wkt, r_qbf[0], r_qbf[1], r_ln1T)

        def vtile(tt):
            if tt < 15:
                return A3[:, tt * VW:(tt + 1) * VW]
            return A1[:, 20480:20480 + VW]
        for ob in range(4):
            slot = ob % 2
            wv_slot = A1[:, slot * 8192:(slot + 1) * 8192]
            wvv = wv_slot.rearrange("p (dc n) -> p dc n", n=512)
            for j in range(4):
                oc = ob * 4 + j
                prep_chunk(wv_h[oc * P:(oc + 1) * P, :],
                           lambda hf, wvv=wvv, j=j: wvv[:, hf * 8:hf * 8 + 8, j * P:(j + 1) * P],
                           r_wvt[slot])
            for tt in range(NT):
                sl = tt % 2
                sbuf = A1[:, 16384 + sl * 2048: 16384 + (sl + 1) * 2048]
                em.emit("y", lambda E, s=ln1t_sp[tt], d=sbuf.rearrange("p (c n) -> p c n", n=P):
                        E.dma_start(out=d, in_=s),
                        reads=[r_spill["ln1t"]], writes=[r_slice[sl]], dma=True)
                bank = b[tt % 4]
                em.barrier_read("t", [r_slice[sl], r_wvt[slot]])
                dst = pb0[:, (tt % 4) * 512:((tt % 4) + 1) * 512]
                for dc in range(ND):
                    lhs = sbuf[:, dc * P:(dc + 1) * P]
                    rhs = wv_slot[:, dc * 512:(dc + 1) * 512]
                    if dc == 0:
                        mm_open(bank)
                        mm_raw(dst, lhs, rhs, True, False)
                    elif dc == ND - 1:
                        mm_close(dst, lhs, rhs, False, bank, [r_slice[sl], r_wvt[slot]])
                    else:
                        mm_raw(dst, lhs, rhs, False, False)
                vdst = vtile(tt).rearrange("p (h n) -> p h n", n=129)[:, ob * 4:(ob + 1) * 4, 0:128]
                em.emit("s", lambda E, d=vdst, s=dst.rearrange("p (h n) -> p h n", n=P):
                        E.activation(out=d, in_=s, func=AF.Copy, bias=0.0, scale=1.0),
                        reads=[bank], writes=[r_V])
                if tt % 4 == 3:
                    emit_cache_chunks(2)
        for tt in range(NT):
            em.emit("v", lambda E, d=vtile(tt).rearrange("p (h n) -> p h n", n=129)[:, :, 128:129]:
                    E.memset(d, 1.0), writes=[r_V])

        r_qh = [seed(r_wvt[0], r_wvt[1]), seed(r_wvt[0], r_wvt[1])]
        r_expc = [seed(r_wvt[0], r_wvt[1]) for _ in range(3)]
        r_stage = [seed(r_wvt[0], r_wvt[1]), seed(r_wvt[0], r_wvt[1])]
        r_sC = Res()
        qh_buf = [A1[:, 0:2048], A1[:, 2048:4096]]
        expc = [A1[:, 4096:5120], A1[:, 5120:6144]]
        stg = [A1[:, 6144:6656], A1[:, 6656:7168]]
        ec_ctr = [0]

        def qh_load(hh):
            sl = hh % 2
            em.emit("y", lambda E, d=qh_buf[sl][:], s=q_sp[hh]: E.dma_start(out=d, in_=s),
                    reads=[r_spill["q"]], writes=[r_qh[sl]], dma=True)

        def score_exp(hh, Q, kk0):
            sl = hh % 2
            ep = ec_ctr[0] % 2
            ec_ctr[0] += 1
            base = 1024 if ep == 0 else 0
            banks = (b[6], b[7]) if ep == 0 else (b[4], b[5])
            em.barrier_read("t", [r_K, r_qh[sl]])
            for t in range(2):
                kk = kk0 + t
                sb = banks[t]
                em.warw("t", sb)
                tok = em.inc("t", lambda E, d=pbB[:, base + t * 512: base + (t + 1) * 512],
                             l=A2[:, hh * 2048 + kk * P: hh * 2048 + (kk + 1) * P],
                             r=qh_buf[sl][:, Q * 512:(Q + 1) * 512]:
                             E.matmul(d, l, r, start=True, stop=True))
                sb.w = tok
                sb.r = []
                r_K.r.append(tok)
                r_qh[sl].r.append(tok)
            em.emit("s", lambda E, s=pbB[:, base: base + 1024], d=expc[ep][:]:
                    E.activation(out=d, in_=s, func=AF.Exp, bias=stt[:, 30:31],
                                 scale=SOFT_SCALE),
                    reads=[banks[0], banks[1], r_stt], writes=[r_expc[ep]])
            for t in range(2):
                kk = kk0 + t
                if kk >= 4 * Q:
                    jd = t * 4 + (kk - 4 * Q)
                    em.emit("v", lambda E, d=expc[ep][:, jd * P:(jd + 1) * P]:
                            E.tensor_tensor(d, d, maskc[:], ALU.mult),
                            reads=[r_consts], writes=[r_expc[ep]])
            return ep

        def avs(hh, Q, kk0, ep):
            sg = Q % 2
            em.barrier_read("t", [r_expc[ep], r_V])
            for t in range(2):
                kk = kk0 + t
                jls = list(range(max(kk - 4 * Q, 0), 4))
                for idx, jl in enumerate(jls):
                    qt = 4 * Q + jl
                    bank = b[jl]
                    dst = pb0[:, jl * 512: jl * 512 + 129]
                    lhs = expc[ep][:, (t * 4 + jl) * P:(t * 4 + jl + 1) * P]
                    rhs = vtile(kk)[:, hh * 129:(hh + 1) * 129]
                    if kk == 0:
                        em.warw("t", bank)
                    if kk == qt or (t == 1 and idx == len(jls) - 1):
                        tok = em.inc("t", lambda E, d=dst, l=lhs, r=rhs,
                                     s1=(kk == 0), s2=(kk == qt):
                                     E.matmul(d, l, r, start=s1, stop=s2))
                        r_expc[ep].r.append(tok)
                        r_V.r.append(tok)
                        if kk == qt:
                            bank.w = tok
                            bank.r = []
                            sc = (Q % 2) * 4 + jl
                            em.emit("v", lambda E, d=sttC[:, sc:sc + 1],
                                    s=pb0[:, jl * 512 + 128: jl * 512 + 129]:
                                    E.reciprocal(d, s),
                                    reads=[bank], writes=[r_sC])
                            em.emit("v", lambda E, d=stg[sg][:, jl * P:(jl + 1) * P],
                                    s=pb0[:, jl * 512: jl * 512 + 128],
                                    r_=sttC[:, sc:sc + 1]:
                                    E.tensor_scalar(out=d, in0=s, scalar1=r_,
                                                    scalar2=None, op0=ALU.mult),
                                    reads=[bank, r_sC], writes=[r_stage[sg]])
                    else:
                        mm_raw(dst, lhs, rhs, (kk == 0), False)

        def stage_dma(hh, Q):
            sg = Q % 2
            em.emit("p", lambda E, s=stg[sg].rearrange("p (t n) -> p t n", n=P),
                    d=ao_sp[4 * Q:4 * Q + 4, :, hh * P:(hh + 1) * P].rearrange("t p n -> p t n"):
                    E.dma_start(out=d, in_=s),
                    reads=[r_stage[sg]], writes=[r_spill["ao"]], dma=True)

        qh_load(0)
        qh_load(1)
        for hh in range(NH):
            pairs = [(Q, kk0) for Q in range(4) for kk0 in range(0, 4 * Q + 4, 2)]
            eps = {0: score_exp(hh, *pairs[0])}
            for i, (Q, kk0) in enumerate(pairs):
                if i + 1 < len(pairs):
                    eps[i + 1] = score_exp(hh, *pairs[i + 1])
                avs(hh, Q, kk0, eps[i])
                if kk0 + 1 == 4 * Q + 3:
                    stage_dma(hh, Q)
            if hh + 2 < NH:
                qh_load(hh + 2)
            emit_cache_chunks(12)

        r_WoT = seed(r_qh[0], r_qh[1], r_expc[0], r_expc[1], r_stage[0], r_stage[1],
                     r_slice[0], r_slice[1], r_V)
        r_ao = [seed(r_K), seed(r_K)]
        r_aot = seed(r_K)
        r_ln2hi = seed(r_K)
        r_ln2lo = seed(r_K)
        r_hiT = seed(r_K)
        r_loT = seed(r_K)
        r_lnf = seed(r_V)
        r_sqd = seed(r_V)
        r_wall = Res()
        r_wT = seed(r_V)

        A1w = A1[:, :].rearrange("p (hc n) -> p hc n", n=2048)
        for oc in range(NH):
            prep_chunk(wo_h[oc * P:(oc + 1) * P, :],
                       lambda hf, oc=oc: A1w[:, hf * 8:hf * 8 + 8, oc * P:(oc + 1) * P],
                       r_WoT)

        aosl = [A2[:, 0:2048], A2[:, 2048:4096]]
        aotb = A2[:, 4096:6144]
        ln2hi = A2[:, 6144:8192]
        ln2lo = A2[:, 8192:10240]
        hiT = A2[:, 10240:12288]
        loT = A2[:, 12288:14336]
        lnf = A3[:, 0:4096].bitcast(F32)
        sqd = A3[:, 8192:12288].bitcast(F32)[:, 0:2048]
        xfa2 = [A2[:, 14336:18432].bitcast(F32), A2[:, 18432:22528].bitcast(F32)]
        r_xfa2 = [seed(r_K), seed(r_K)]
        aotv = aotb.rearrange("p (c n) -> p c n", n=P)
        hiTv = hiT.rearrange("p (c n) -> p c n", n=P)
        loTv = loT.rearrange("p (c n) -> p c n", n=P)

        def c8(i):
            return rt[:, i * NE:(i + 1) * NE]

        (RZ, REX, RINH, RD0, RDIST, REX1, RFB, RZONE, RWT, RSELR, RFB2,
         RFBM, RFBW, RG1, RG2, RW0, RDIFF, RDSC, RINV, RSEL2R, RT1, RT2) = range(22)

        def c2_stageA(tt):
            sl = tt % 2
            xf = xfa2[sl]
            rxf = r_xfa2[sl]
            em.emit("y", lambda E, s=ao_sp[tt], d=aosl[sl]: E.dma_start(out=d, in_=s),
                    reads=[r_spill["ao"]], writes=[r_ao[sl]], dma=True)
            em.emit("y", lambda E, d=xf[:], s=x0_h[tt * P:(tt + 1) * P, :]: E.dma_start(out=d, in_=s),
                    writes=[rxf], dma=True)
            transpose16(aosl[sl], r_ao[sl],
                        lambda hf: aotv[:, hf * 8:hf * 8 + 8, :], r_aot)
            em.barrier_read("t", [r_aot, r_WoT])
            for hc in range(NH):
                for ob in range(4):
                    bank = b[ob]
                    lhs = aotb[:, hc * P:(hc + 1) * P]
                    rhs = A1[:, hc * 2048 + ob * 512: hc * 2048 + (ob + 1) * 512]
                    dst = pb0[:, ob * 512:(ob + 1) * 512]
                    if hc == 0:
                        mm_open(bank)
                        mm_raw(dst, lhs, rhs, True, False)
                    elif hc == NH - 1:
                        mm_close(dst, lhs, rhs, False, bank, [r_aot, r_WoT])
                    else:
                        mm_raw(dst, lhs, rhs, False, False)
            for ob in range(4):
                em.emit("v", lambda E, s=pb0[:, ob * 512:(ob + 1) * 512],
                        x=xf[:, ob * 512:(ob + 1) * 512]:
                        E.scalar_tensor_tensor(out=x, in0=s, scalar=1.0, in1=x,
                                               op0=ALU.mult, op1=ALU.add),
                        reads=[b[ob]], writes=[rxf])
            em.emit("p", lambda E, d=x1_sp[tt * P:(tt + 1) * P, :], s=xf[:]: E.dma_start(out=d, in_=s),
                    reads=[rxf], writes=[r_spill["x1"]], dma=True)

        def c2_stageB(tt):
            sl = tt % 2
            xf = xfa2[sl]
            rxf = r_xfa2[sl]
            emit_ln(xf[:], rxf, ln2hi, r_ln2hi, sqd, [r_sqd],
                    lo=(lnf[:, 0:2048], r_lnf, ln2lo, r_ln2lo))
            transpose16(ln2hi, r_ln2hi, lambda hf: hiTv[:, hf * 8:hf * 8 + 8, :], r_hiT)
            transpose16(ln2lo, r_ln2lo, lambda hf: loTv[:, hf * 8:hf * 8 + 8, :], r_loT)
            tb2, off = tt // 8, (tt % 8) * P
            em.emit("p", lambda E, d=ln2t_sp[tb2, :, :, off:off + P], s=hiTv:
                    E.dma_start(out=d, in_=s),
                    reads=[r_hiT], writes=[r_spill["ln2t"]], dma=True)
            em.barrier_read("t", [r_hiT, r_loT, r_wr])
            mm_open(b[6])
            k = 0
            nmm = ND * 3
            for dc in range(ND):
                for (lb, rb) in ((hiT, wr_hi), (loT, wr_hi), (hiT, wr_lo)):
                    lhs = lb[:, dc * P:(dc + 1) * P]
                    rhs = rb[:, dc * NE:(dc + 1) * NE]
                    if k == nmm - 1:
                        mm_close(pb6[:, 0:NE], lhs, rhs, False, b[6], [r_hiT, r_loT, r_wr])
                    else:
                        mm_raw(pb6[:, 0:NE], lhs, rhs, (k == 0), False)
                    k += 1
            em.emit("v", lambda E: E.tensor_scalar(out=c8(RZ), in0=pb6[:, 0:NE],
                                                   scalar1=stt[:, 9:10], scalar2=None, op0=ALU.mult),
                    reads=[b[6], r_stt], writes=[r_rt])
            em.emit("s", lambda E: E.activation(out=c8(REX), in_=c8(RZ), func=AF.Exp,
                                                bias=0.0, scale=-1.0),
                    reads=[r_rt], writes=[r_rt])
            em.emit("v", lambda E: E.tensor_scalar(out=c8(RT1), in0=c8(REX), scalar1=1.0,
                                                   scalar2=None, op0=ALU.add),
                    reads=[r_rt], writes=[r_rt])
            em.emit("v", lambda E: E.reciprocal(c8(RINH), c8(RT1)), reads=[r_rt], writes=[r_rt])
            em.emit("v", lambda E: E.tensor_scalar(out=c8(RD0), in0=c8(RINH), scalar1=GOLD_C,
                                                   scalar2=None, op0=ALU.subtract),
                    reads=[r_rt], writes=[r_rt])
            em.emit("s", lambda E: E.activation(out=c8(RDIST), in_=c8(RD0), func=AF.Abs,
                                                bias=0.0, scale=1.0),
                    reads=[r_rt], writes=[r_rt])
            em.emit("s", lambda E: E.activation(out=c8(REX1), in_=c8(RDIST), func=AF.Exp,
                                                bias=0.0, scale=-10.0),
                    reads=[r_rt], writes=[r_rt])
            em.emit("s", lambda E: E.activation(out=c8(RFB), in_=c8(RDIST), func=AF.Exp,
                                                bias=0.0, scale=float(np.float32(-1.0 / 0.3))),
                    reads=[r_rt], writes=[r_rt])
            em.emit("v", lambda E: E.tensor_single_scalar(c8(RG1), c8(RINH), GOLD_L, ALU.is_ge),
                    reads=[r_rt], writes=[r_rt])
            em.emit("v", lambda E: E.tensor_single_scalar(c8(RG2), c8(RINH), GOLD_U, ALU.is_le),
                    reads=[r_rt], writes=[r_rt])
            em.emit("v", lambda E: E.tensor_tensor(c8(RZONE), c8(RG1), c8(RG2), ALU.mult),
                    reads=[r_rt], writes=[r_rt])
            em.emit("v", lambda E: E.tensor_tensor(c8(RWT), c8(REX1), c8(RZONE), ALU.mult),
                    reads=[r_rt], writes=[r_rt])
            em.emit("v", lambda E: E.reduce_sum(stt[:, 10:11], c8(RWT), axis=AX.X),
                    reads=[r_rt], writes=[r_stt])
            em.emit("v", lambda E: E.tensor_single_scalar(stt[:, 11:12], stt[:, 10:11],
                                                          1e-8, ALU.is_lt),
                    reads=[r_stt], writes=[r_stt])
            em.emit("v", lambda E: E.reduce_max(stt[:, 12:13], c8(RFB), axis=AX.X),
                    reads=[r_rt], writes=[r_stt])
            em.emit("v", lambda E: E.tensor_scalar(out=c8(RT1), in0=c8(RFB), scalar1=stt[:, 12:13],
                                                   scalar2=None, op0=ALU.is_ge),
                    reads=[r_rt, r_stt], writes=[r_rt])
            em.emit("v", lambda E: E.tensor_tensor(c8(RSELR), c8(RT1), rev[:], ALU.mult),
                    reads=[r_rt, r_consts], writes=[r_rt])
            em.emit("v", lambda E: E.reduce_max(stt[:, 13:14], c8(RSELR), axis=AX.X),
                    reads=[r_rt], writes=[r_stt])
            em.emit("v", lambda E: E.tensor_scalar(out=c8(RT1), in0=c8(RSELR), scalar1=stt[:, 13:14],
                                                   scalar2=None, op0=ALU.is_ge),
                    reads=[r_rt, r_stt], writes=[r_rt])
            em.emit("v", lambda E: E.tensor_scalar(out=c8(RINV), in0=c8(RT1), scalar1=-1.0,
                                                   scalar2=1.0, op0=ALU.mult, op1=ALU.add),
                    reads=[r_rt], writes=[r_rt])
            em.emit("v", lambda E: E.tensor_tensor(c8(RFB2), c8(RFB), c8(RINV), ALU.mult),
                    reads=[r_rt], writes=[r_rt])
            em.emit("v", lambda E: E.reduce_max(stt[:, 14:15], c8(RFB2), axis=AX.X),
                    reads=[r_rt], writes=[r_stt])
            em.emit("v", lambda E: E.tensor_scalar(out=c8(RT2), in0=c8(RFB2), scalar1=stt[:, 14:15],
                                                   scalar2=None, op0=ALU.is_ge),
                    reads=[r_rt, r_stt], writes=[r_rt])
            em.emit("v", lambda E: E.tensor_tensor(c8(RSEL2R), c8(RT2), rev[:], ALU.mult),
                    reads=[r_rt, r_consts], writes=[r_rt])
            em.emit("v", lambda E: E.reduce_max(stt[:, 16:17], c8(RSEL2R), axis=AX.X),
                    reads=[r_rt], writes=[r_stt])
            em.emit("v", lambda E: E.tensor_scalar(out=c8(RT2), in0=c8(RSEL2R), scalar1=stt[:, 16:17],
                                                   scalar2=None, op0=ALU.is_ge),
                    reads=[r_rt, r_stt], writes=[r_rt])
            em.emit("v", lambda E: E.tensor_tensor(c8(RFBM), c8(RT1), c8(RT2), ALU.add),
                    reads=[r_rt], writes=[r_rt])
            em.emit("v", lambda E: E.tensor_tensor(c8(RFBW), c8(RFB), c8(RFBM), ALU.mult),
                    reads=[r_rt], writes=[r_rt])
            em.emit("v", lambda E: E.reduce_sum(stt[:, 17:18], c8(RFBW), axis=AX.X),
                    reads=[r_rt], writes=[r_stt])
            em.emit("v", lambda E: E.tensor_single_scalar(stt[:, 18:19], stt[:, 17:18],
                                                          1e-8, ALU.max),
                    reads=[r_stt], writes=[r_stt])
            em.emit("v", lambda E: E.reciprocal(stt[:, 19:20], stt[:, 18:19]),
                    reads=[r_stt], writes=[r_stt])
            em.emit("v", lambda E: E.tensor_scalar(out=c8(RFBW), in0=c8(RFBW), scalar1=stt[:, 19:20],
                                                   scalar2=None, op0=ALU.mult),
                    reads=[r_stt], writes=[r_rt])
            em.emit("v", lambda E: E.tensor_tensor(c8(RDIFF), c8(RFBW), c8(RWT), ALU.subtract),
                    reads=[r_rt], writes=[r_rt])
            em.emit("v", lambda E: E.tensor_scalar(out=c8(RDSC), in0=c8(RDIFF), scalar1=stt[:, 11:12],
                                                   scalar2=None, op0=ALU.mult),
                    reads=[r_rt, r_stt], writes=[r_rt])
            em.emit("v", lambda E: E.tensor_tensor(c8(RW0), c8(RWT), c8(RDSC), ALU.add),
                    reads=[r_rt], writes=[r_rt])
            em.emit("v", lambda E: E.reduce_sum(stt[:, 20:21], c8(RW0), axis=AX.X),
                    reads=[r_rt], writes=[r_stt])
            em.emit("v", lambda E: E.tensor_single_scalar(stt[:, 21:22], stt[:, 20:21],
                                                          1e-8, ALU.max),
                    reads=[r_stt], writes=[r_stt])
            em.emit("v", lambda E: E.reciprocal(stt[:, 22:23], stt[:, 21:22]),
                    reads=[r_stt], writes=[r_stt])
            em.emit("v", lambda E, d=w_all[:, tt * NE:(tt + 1) * NE]:
                    E.tensor_scalar(out=d, in0=c8(RW0), scalar1=stt[:, 22:23],
                                    scalar2=None, op0=ALU.mult),
                    reads=[r_rt, r_stt], writes=[r_wall])
            emit_cache_chunks(4)

        c2_stageA(0)
        for tt in range(NT):
            if tt + 1 < NT:
                c2_stageA(tt + 1)
            c2_stageB(tt)

        em.emit("v", lambda E: E.tensor_copy(w_bfe[:], w_all[:]), reads=[r_wall], writes=[r_wall])
        for tt in range(NT):
            em.barrier_read("t", [r_wall, r_consts])
            em.warw("t", b[4])
            tok = em.inc("t", lambda E, s=w_bfe[:, tt * NE:(tt + 1) * NE]:
                         E.transpose(tpb4[0:NE, 0:P], s, ident[:]))
            b[4].w = tok
            b[4].r = []
            r_wall.r.append(tok)
            em.emit("v", lambda E, d=wT[:, tt * P:(tt + 1) * P]: E.tensor_copy(d, tpb4[0:NE, 0:P]),
                    reads=[b[4]], writes=[r_wT])

        r_slot = [seed(r_WoT) for _ in range(4)]
        r_ln2blk = seed(r_ao[0], r_ao[1], r_aot, r_ln2hi, r_ln2lo, r_hiT, r_loT, r_xfa2[0], r_xfa2[1])
        r_h = seed(r_ln2blk)
        r_part = seed(r_lnf, r_sqd, r_V)
        r_wbc = seed(r_lnf, r_sqd, r_V)
        r_x1b = [seed(r_lnf, r_sqd, r_V), seed(r_lnf, r_sqd, r_V)]
        r_outst = [seed(r_lnf, r_sqd, r_V), seed(r_lnf, r_sqd, r_V)]
        r_s12 = [seed(r_lnf, r_sqd, r_V), seed(r_lnf, r_sqd, r_V)]
        r_wrow = seed(r_V)

        ln2blk = A2[:, 0:16384]
        hbuf = A2[:, 16384:32768]
        S4 = [A1[:, 0:8192], A1[:, 8192:16384], A1[:, 16384:24576], A1[:, 24576:32768]]
        partial = A3[:, 0:16384]
        wbc = A3[:, 16384:20480].bitcast(F32)[:, 0:1024]
        x1b = A3[:, 20480:22528].bitcast(F32)
        outst = A3[:, 22528:24576].bitcast(F32)
        s1b = [A3[:, 24576:25600], A3[:, 25600:26624]]
        s2b = [A3[:, 26624:27648], A3[:, 27648:28672]]

        emit_cache_chunks(len(cache_jobs))

        for tb in range(2):
            em.emit("y", lambda E, s=ln2t_sp[tb], d=ln2blk.rearrange("p (c n) -> p c n", n=1024):
                    E.dma_start(out=d, in_=s),
                    reads=[r_spill["ln2t"]], writes=[r_ln2blk], dma=True)
            loads = []
            for ex_ in range(NE):
                for fb_ in range(4):
                    loads.append(("y", "wg", ex_, fb_))
                    loads.append(("p", "wu", ex_, fb_))
                for db_ in range(4):
                    loads.append(("s", "wd", ex_, db_))
            lpos = [0]

            def issue_to(n, loads=loads, lpos=lpos):
                while lpos[0] < min(n, len(loads)):
                    i = lpos[0]
                    lpos[0] += 1
                    qe, kind, ex_, j = loads[i]
                    sl = i % 4
                    if kind == "wg":
                        src = wg_c[ex_, :, j * 512:(j + 1) * 512].rearrange("(dc p) n -> p dc n", p=P)
                    elif kind == "wu":
                        src = wu_c[ex_, :, j * 512:(j + 1) * 512].rearrange("(dc p) n -> p dc n", p=P)
                    else:
                        src = wd_c[ex_, :, j * 512:(j + 1) * 512].rearrange("(fc p) n -> p fc n", p=P)
                    em.emit(qe, lambda E, s=src, d=S4[sl].rearrange("p (c n) -> p c n", n=512):
                            E.dma_start(out=d, in_=s),
                            reads=[r_spill[kind + str(ex_)]], writes=[r_slot[sl]], dma=True)

            issue_to(4)
            for ex in range(NE):
                em.emit("y", lambda E, s=wT[ex:ex + 1, tb * 1024:(tb + 1) * 1024]:
                        E.dma_start(out=wrow, in_=s),
                        reads=[r_wT], writes=[r_wrow], dma=True)
                for half in range(2):
                    pbx = pb6 if half == 0 else pb7
                    bankx = b[6 + half]
                    em.emit("t", lambda E, d=pbx[:, 0:512], r=wrow[:, half * 512:(half + 1) * 512]:
                            E.matmul(d, ones1b[:], r, start=True, stop=True),
                            reads=[r_wrow, r_consts], writes=[bankx])
                    em.emit("v", lambda E, d=wbc[:, half * 512:(half + 1) * 512], s=pbx[:, 0:512]:
                            E.tensor_copy(d, s),
                            reads=[bankx], writes=[r_wbc])
                for fb in range(4):
                    gi = ex * 12 + 2 * fb
                    gsl_b = S4[gi % 4]
                    usl_b = S4[(gi + 1) % 4]
                    rg = r_slot[gi % 4]
                    ru = r_slot[(gi + 1) % 4]
                    for fc4 in range(4):
                        fc = fb * 4 + fc4
                        gb = (fc % 2) * 4
                        pset = pb0 if gb == 0 else pbB
                        sp12 = fc % 2
                        em.barrier_read("t", [rg, ru, r_ln2blk, r_wbc])
                        for which in range(2):
                            wslot = gsl_b if which == 0 else usl_b
                            rw = rg if which == 0 else ru
                            bps = 0 if which == 0 else 2
                            for dc in range(ND):
                                lhs = wslot[:, dc * 512 + fc4 * P: dc * 512 + (fc4 + 1) * P]
                                for th in range(2):
                                    bank = b[gb + bps + th]
                                    rhs = ln2blk[:, dc * 1024 + th * 512: dc * 1024 + (th + 1) * 512]
                                    dst = pset[:, (bps + th) * 512:(bps + th + 1) * 512]
                                    if dc == 0:
                                        mm_open(bank)
                                        mm_raw(dst, lhs, rhs, True, False)
                                    elif dc == ND - 1:
                                        mm_close(dst, lhs, rhs, False, bank,
                                                 [rw, r_ln2blk])
                                    else:
                                        mm_raw(dst, lhs, rhs, False, False)
                        em.emit("s", lambda E, o=s1b[sp12], i=pset[:, 0:1024]:
                                E.activation(out=o, in_=i, func=AF.Silu, bias=0.0, scale=1.0),
                                reads=[b[gb], b[gb + 1]], writes=[r_s12[sp12]])
                        em.emit("v", lambda E, o=s2b[sp12], i=pset[:, 1024:2048]:
                                E.tensor_tensor(o, i, wbc[:, 0:1024], ALU.mult),
                                reads=[b[gb + 2], b[gb + 3], r_wbc], writes=[r_s12[sp12]])
                        em.emit("p", lambda E, d=hbuf[:, fc * 1024:(fc + 1) * 1024],
                                a=s1b[sp12], c=s2b[sp12]:
                                E.tensor_tensor(d, a, c, ALU.mult),
                                reads=[r_s12[sp12]], writes=[r_h])
                    issue_to(gi + 2 + 4)
                for dblk in range(4):
                    di = ex * 12 + 8 + dblk
                    dsl = S4[di % 4]
                    rd = r_slot[di % 4]
                    em.barrier_read("t", [rd, r_h])
                    for t128 in range(8):
                        avp = pb6 if t128 % 2 == 0 else pb7
                        bank = b[6 + (t128 % 2)]
                        mm_open(bank)
                        for fc in range(16):
                            lhs = hbuf[:, fc * 1024 + t128 * P: fc * 1024 + (t128 + 1) * P]
                            rhs = dsl[:, fc * 512:(fc + 1) * 512]
                            if fc == 15:
                                mm_close(avp[:, 0:512], lhs, rhs, False, bank,
                                         [rd, r_h])
                            else:
                                mm_raw(avp[:, 0:512], lhs, rhs, (fc == 0), False)
                        pslice = partial[:, t128 * 2048 + dblk * 512: t128 * 2048 + (dblk + 1) * 512]
                        if ex == 0:
                            em.emit("v", lambda E, d=pslice, s=avp[:, 0:512]: E.tensor_copy(d, s),
                                    reads=[bank], writes=[r_part])
                        elif ex < NE - 1:
                            em.emit("v", lambda E, d=pslice, s=avp[:, 0:512]:
                                    E.tensor_tensor(d, s, d, ALU.add),
                                    reads=[bank, r_part], writes=[r_part])
                        else:
                            rows = tb * 1024 + t128 * P
                            xsl = x1b[:, (t128 % 2) * 512:((t128 % 2) + 1) * 512]
                            osl = outst[:, (t128 % 2) * 512:((t128 % 2) + 1) * 512]
                            em.emit("y", lambda E, d=xsl,
                                    s=x1_sp[rows:rows + P, dblk * 512:(dblk + 1) * 512]:
                                    E.dma_start(out=d, in_=s),
                                    reads=[r_spill["x1"]], writes=[r_x1b[t128 % 2]], dma=True)
                            em.emit("v", lambda E, d=osl, s=avp[:, 0:512], x=xsl:
                                    E.tensor_tensor(d, s, x, ALU.add),
                                    reads=[bank, r_x1b[t128 % 2]], writes=[r_outst[t128 % 2]])
                            em.emit("v", lambda E, d=osl, p_=pslice:
                                    E.tensor_tensor(d, d, p_, ALU.add),
                                    reads=[r_part], writes=[r_outst[t128 % 2]])
                            em.emit("p", lambda E, s=osl,
                                    d=out_h[rows:rows + P, dblk * 512:(dblk + 1) * 512]:
                                    E.dma_start(out=d, in_=s),
                                    reads=[r_outst[t128 % 2]], writes=[r_spill["out"]], dma=True)
                    issue_to(di + 1 + 4)

        em.wait("p", "dp", em.cnt["dp"])

        with nc.Block() as block:
            block.sync(lambda E: [f(E) for f in em.th["y"]])
            block.tensor(lambda E: [f(E) for f in em.th["t"]])
            block.vector(lambda E: [f(E) for f in em.th["v"]])
            block.scalar(lambda E: [f(E) for f in em.th["s"]])
            block.gpsimd(lambda E: [f(E) for f in em.th["p"]])

    return nc


_NC_CACHE = {}


def kernel(L0_attn_hidden_states, L0_attn_Wq, L0_attn_Wk, L0_attn_Wv, L0_attn_Wo,
           L0_ffn_Wg, L0_ffn_Wu, L0_ffn_Wd, L0_ffn_Wr, L0_ffn_temperature):
    x = np.ascontiguousarray(np.asarray(L0_attn_hidden_states, dtype=np.float32))
    base = {
        "wq": np.ascontiguousarray(np.asarray(L0_attn_Wq, dtype=np.float32)),
        "wk": np.ascontiguousarray(np.asarray(L0_attn_Wk, dtype=np.float32)),
        "wv": np.ascontiguousarray(np.asarray(L0_attn_Wv, dtype=np.float32)),
        "wo": np.ascontiguousarray(np.asarray(L0_attn_Wo, dtype=np.float32)),
        "wg": np.ascontiguousarray(np.asarray(L0_ffn_Wg, dtype=np.float32)),
        "wu": np.ascontiguousarray(np.asarray(L0_ffn_Wu, dtype=np.float32)),
        "wd": np.ascontiguousarray(np.asarray(L0_ffn_Wd, dtype=np.float32)),
        "wr": np.ascontiguousarray(np.asarray(L0_ffn_Wr, dtype=np.float32)),
        "temp": np.asarray(L0_ffn_temperature, dtype=np.float32).reshape(1, 1),
    }
    base.update(host_consts())
    if "nc" not in _NC_CACHE:
        _NC_CACHE["nc"] = build(debug=False)
    nc = _NC_CACHE["nc"]
    in_maps = [dict(base, x0=x[c]) for c in range(8)]
    res = run_bass_kernel_spmd(nc, in_maps, core_ids=list(range(8)))
    return np.stack([r["out"] for r in res.results], axis=0).astype(np.float32)
